# version 41
# baseline (speedup 1.0000x reference)
import numpy as np
from contextlib import ExitStack

import concourse.bass as bass
import concourse.mybir as mybir
from concourse.bass_utils import run_bass_kernel_spmd

F32 = mybir.dt.float32
BF16 = mybir.dt.bfloat16
AF = mybir.ActivationFunctionType
ALU = mybir.AluOpType

D = 1024
NT = 2048
DC = 8
NQC = 4
W = 512
NKT = 16
H = 16
FJ = 32
MSHIFT = 16.0
EPS = 1e-5
SCALE = 1.0 / 64.0

ENGS = ("t", "v", "sc", "g", "s")


class Builder:
    def __init__(self):
        self.ops = {e: [] for e in ENGS}
        self.cnt = {}
        self.owner = {}
        self.waited = {e: {} for e in ENGS}

    def sem(self, name, owner=None):
        self.cnt[name] = 0
        self.owner[name] = owner

    def op(self, eng, fn, waits=(), inc=None, amt=1):
        w2 = []
        for tok in waits:
            if tok is None:
                continue
            s, v = tok
            if v <= 0 or self.owner.get(s) == eng:
                continue
            if self.waited[eng].get(s, 0) >= v:
                continue
            self.waited[eng][s] = v
            w2.append((s, v))
        rtok = None
        if inc is not None:
            self.cnt[inc] += amt
            rtok = (inc, self.cnt[inc])
        self.ops[eng].append((fn, w2, inc, amt))
        return rtok

    def mark(self, eng_sem):
        return (eng_sem, self.cnt[eng_sem])

    def emit(self, engobj, eng, sems):
        for fn, waits, inc, amt in self.ops[eng]:
            for s, v in waits:
                engobj.wait_ge(sems[s], v)
            inst = fn(engobj)
            if inc is not None:
                inst.then_inc(sems[inc], amt)


class Bank:

    def __init__(self, ap):
        self.ap = ap
        self.wtok = None
        self.rtoks = []

    def wwaits(self):
        return list(self.rtoks) + [self.wtok]

    def wrote(self, tok):
        self.wtok = tok
        self.rtoks = []

    def rwaits(self):
        return [self.wtok]

    def read(self, tok):
        self.rtoks.append(tok)


def build_program(dbg=False):
    nc = bass.Bass("TRN2", target_bir_lowering=False)

    di = {}
    for name, shape in [
        ("x_in", [128, DC * NT]), ("c_in", [128, 8]), ("bo_in", [128, 8]),
        ("b1_in", [128, 32]), ("b2_in", [128, 8]), ("cst_in", [128, 8]),
        ("wq_in", [128, 8192]), ("wk_in", [128, 8192]), ("wv_in", [128, 8192]),
        ("wo_in", [128, 16384]), ("w1_in", [128, 32768]), ("w2_in", [128, 32768]),
        ("wmod_in", [128, 49152]), ("cos_in", [128, NT]), ("sin_in", [128, NT]),
        ("perm_in", [128, 128]), ("dmask_in", [128, 128]), ("onesln_in", [128, 128]),
        ("selg_in", [128, 256]), ("ehm_in", [128, 2048]), ("ctile_in", [128, 64]),
    ]:
        di[name] = nc.dram_tensor(name, shape, F32, kind="ExternalInput").ap()
    di["y_out"] = nc.dram_tensor("y_out", [128, DC * NT], F32, kind="ExternalOutput").ap()
    for name, shape in [("wqB", [128, 8192]), ("wkB", [128, 8192]), ("wvB", [128, 8192]),
                        ("woB", [128, 16384]), ("w1B", [128, 32768]), ("w2B", [128, 32768])]:
        di[name] = nc.dram_tensor(name, shape, BF16, kind="Internal").ap()
    if dbg:
        for nm, shape, dt_ in [("d_mod", [128, 48], F32),
                               ("d_gbf", [128, 16], F32), ("d_h1", [128, DC * W], BF16),
                               ("d_qq", [128, DC * W], BF16), ("d_k", [128, DC * W], BF16),
                               ("d_vg", [128, 4 * 1040], BF16), ("d_orw", [65, 16 * 512], BF16),
                               ("d_xa", [128, DC * NT], F32), ("d_h2", [128, DC * NT], BF16),
                               ("d_g", [128, FJ * 1024], BF16),
                               ("d_w2", [128, 4096], BF16), ("d_y", [128, 512], F32),
                               ("d_gcl", [128, 16], F32)]:
            di[nm] = nc.dram_tensor(nm, shape, dt_, kind="ExternalOutput").ap()

    b = Builder()
    sem_names = (["xdma", "tdma", "cdma", "pdma"]
                 + [f"r{i}" for i in range(3)] + [f"q{i}" for i in range(4)]
                 + ["h0", "h1", "w20", "w21"])
    for s in sem_names:
        b.sem(s, None)
    for s, o in [("t", "t"), ("v", "v"), ("sc", "sc")]:
        b.sem(s, o)

    with ExitStack() as ES:
        E = ES.enter_context
        xsb = E(nc.sbuf_tensor("xsb", [128, DC * NT], F32))
        wring = E(nc.sbuf_tensor("wring", [128, 3 * 1024], BF16))
        c_sb = E(nc.sbuf_tensor("c_sb", [128, 8], F32))
        bo_sb = E(nc.sbuf_tensor("bo_sb", [128, 8], F32))
        b1_sb = E(nc.sbuf_tensor("b1_sb", [128, 32], F32))
        b2_sb = E(nc.sbuf_tensor("b2_sb", [128, 8], F32))
        cst = E(nc.sbuf_tensor("cst", [128, 8], F32))
        ctile = E(nc.sbuf_tensor("ctile", [128, 64], F32))
        siluc = E(nc.sbuf_tensor("siluc", [128, 8], BF16))
        siluf = E(nc.sbuf_tensor("siluf", [128, 8], F32))
        mod_sb = E(nc.sbuf_tensor("mod_sb", [128, 48], F32))
        sgn_sb = E(nc.sbuf_tensor("sgn_sb", [128, 16], F32))
        gcl_sb = E(nc.sbuf_tensor("gcl_sb", [128, 16], F32))
        ca1_sb = E(nc.sbuf_tensor("ca1_sb", [128, 16], F32))
        gb_f = E(nc.sbuf_tensor("gb_f", [128, 16], F32))
        onesln = E(nc.sbuf_tensor("onesln", [128, 128], BF16))
        xbf = E(nc.sbuf_tensor("xbf", [128, 2 * W], BF16))
        xsq = E(nc.sbuf_tensor("xsq", [128, W], BF16))
        m_sb = E(nc.sbuf_tensor("m_sb", [128, W], F32))
        var_sb = E(nc.sbuf_tensor("var_sb", [128, W], F32))
        hscr = E(nc.sbuf_tensor("hscr", [128, 2 * W], F32))
        rinv_sb = E(nc.sbuf_tensor("rinv_sb", [128, W], F32))

        big = E(nc.sbuf_tensor("big", [128, 54016], BF16))
        ksb = big[:, 0:16384]
        vaug = big[:, 16384:33024]
        qqA = big[:, 33024:37120]
        qqB = big[:, 37120:41216]
        hq = big[:, 41216:45312]
        oraw = big[0:65, 41216:49408]
        ptb = big[:, 49408:51456]
        qtmp = big[:, 49408:50432]
        rsc1 = big[:, 50432:50944]
        rsc2 = big[:, 50944:51456]
        permb = big[:, 51456:51584]
        dmaskb = big[:, 51584:51712]
        selgb = big[:, 51712:51968]
        ehmb = big[:, 51968:54016]
        h2 = big[:, 0:16384]
        rscrF = big[:, 49408:51456].bitcast(F32)
        g_sb = big[:, 16384:49152]
        linv_b = big[:, 50432:50944]

        u3 = E(nc.sbuf_tensor("u3", [128, 8192], BF16))
        woring = u3[:, 0:8192]
        w2ring = u3[:, 0:8192]
        csb = E(nc.sbuf_tensor("csb", [128, 1024], BF16))
        linv_f = hscr[0:16, 0:W]

        pS = E(nc.psum_tensor("pS", [128, 2048], F32))
        psA = pS[:, 0:1024]
        psB = pS[:, 1024:2048]
        pOA = E(nc.psum_tensor("pOA", [128, 512], F32))
        pOB = E(nc.psum_tensor("pOB", [128, 512], F32))
        pC = E(nc.psum_tensor("pC", [128, 512], F32))
        pD = E(nc.psum_tensor("pD", [128, 512], F32))

        bA0, bA1 = Bank(pS[:, 0:512]), Bank(pS[:, 512:1024])
        bB0, bB1 = Bank(pS[:, 1024:1536]), Bank(pS[:, 1536:2048])
        bOA, bOB, bC, bD = Bank(pOA[:]), Bank(pOB[:]), Bank(pC[:]), Bank(pD[:])
        slots4 = [bA0, bA1, bB0, bB1]
        vbank_pairs = [(pS[:, 0:512], pS[:, 512:1024], bA0, bA1),
                       (pS[:, 1024:1536], pS[:, 1536:2048], bB0, bB1),
                       (pOA[:], pOB[:], bOA, bOB),
                       (pC[:], pD[:], bC, bD)]

        def gdma(dst_ap, src_ap, waits=(), sem="tdma"):
            return b.op("g", lambda g, d=dst_ap, s=src_ap: g.dma_start(d, s),
                        waits, inc=sem, amt=16)

        def sdma(dst_ap, src_ap, waits=(), sem="xdma"):
            return b.op("s", lambda s_, d=dst_ap, s=src_ap: s_.dma_start(d, s),
                        waits, inc=sem, amt=16)

        sdma(xsb[:], di["x_in"])
        tok_small = None
        for nm, t_ in [("c_in", c_sb), ("bo_in", bo_sb), ("b1_in", b1_sb),
                       ("b2_in", b2_sb), ("cst_in", cst), ("ctile_in", ctile)]:
            tok_small = sdma(t_[:], di[nm])
        tok_x = tok_small

        tok_tab = None
        for nm, t_ in [("perm_in", permb), ("dmask_in", dmaskb),
                       ("onesln_in", onesln[:]), ("selg_in", selgb), ("ehm_in", ehmb)]:
            tok_tab = gdma(t_, di[nm], sem="tdma")

        b.op("sc", lambda s: s.activation(siluc[:], c_sb[:], AF.Silu),
             [tok_small], inc="sc")
        b.op("v", lambda v: v.memset(big[64:128, 45312:49408], 0.0), [], inc="v")
        tok_qz = b.op("v", lambda v: v.memset(qqA[64:128, :], 0.0), [], inc="v")
        tok_qz = b.op("v", lambda v: v.memset(qqB[0:64, :], 0.0), [], inc="v")
        tok_silu = b.op("sc", lambda s: s.activation(siluf[:], c_sb[:], AF.Silu),
                        [], inc="sc")

        ring_free = [None] * 3
        ridx = [0]
        wo_slot_free = [None] * 4
        wo_slot_tok = [None] * 4
        woidx = [0]

        def wo_slot_load(src_slice, via_sync, extra=()):
            i = woidx[0] % 4
            dma = sdma if via_sync else gdma
            tok = dma(woring[:, i * 2048:(i + 1) * 2048], src_slice,
                      waits=[wo_slot_free[i]] + list(extra), sem=f"q{i}")
            wo_slot_tok[i] = tok
            woidx[0] += 1
            return i, tok

        def ring_load(src_slice, extra=(), via_sync=False):
            i = ridx[0] % 3
            dma = sdma if via_sync else gdma
            tok = dma(wring[:, i * 1024:(i + 1) * 1024], src_slice,
                      waits=[ring_free[i]] + list(extra), sem=f"r{i}")
            ridx[0] += 1
            return i, tok

        def emit_ln_stats(qc, src_off, pm=None, ps2=None, bm=None, bs2=None):
            pm = pC[:] if pm is None else pm
            ps2 = pD[:] if ps2 is None else ps2
            bm = bC if bm is None else bm
            bs2 = bD if bs2 is None else bs2
            tok_stat = None
            for c in range(DC):
                xs = xsb[:, src_off(c): src_off(c) + W]
                pp = c % 2
                tv1 = b.op("v", lambda v, xs=xs, pp=pp: v.tensor_copy(
                    xbf[:, pp * W:(pp + 1) * W], xs),
                    [tok_x, xbf_free[pp]], inc="v")
                tv2 = b.op("v", lambda v, pp=pp: v.tensor_mul(
                    xsq[:], xbf[:, pp * W:(pp + 1) * W],
                    xbf[:, pp * W:(pp + 1) * W]), [xsq_free[0]], inc="v")
                wts = [tv1, tv2, tok_tab]
                if c == 0:
                    wts = wts + bm.wwaits() + bs2.wwaits()
                b.op("t", lambda t, pp=pp, c=c, pm=pm: t.matmul(
                    pm, onesln[:], xbf[:, pp * W:(pp + 1) * W],
                    start=(c == 0), stop=(c == 7)), wts, inc="t")
                t2 = b.op("t", lambda t, c=c, ps2=ps2: t.matmul(
                    ps2, onesln[:], xsq[:],
                    start=(c == 0), stop=(c == 7)), [], inc="t")
                xbf_free[pp] = t2
                xsq_free[0] = t2
                tok_stat = t2
            bm.wrote(tok_stat)
            bs2.wrote(tok_stat)
            tok = b.op("sc", lambda s, pm=pm: s.copy(m_sb[:], pm), bm.rwaits(), inc="sc")
            bm.read(tok)
            tok_sq = b.op("sc", lambda s: s.activation(var_sb[:], m_sb[:], AF.Square),
                          [], inc="sc")
            tok = b.op("v", lambda v, ps2=ps2: v.tensor_sub(var_sb[:], ps2, var_sb[:]),
                       bs2.rwaits() + [tok_sq], inc="v")
            bs2.read(tok)
            tok = b.op("sc", lambda s: s.activation(var_sb[:], var_sb[:], AF.Ln, bias=cst[:, 0:1]),
                       [tok, tok_small], inc="sc")
            tok_rinv = b.op("sc", lambda s: s.activation(rinv_sb[:], var_sb[:], AF.Exp, scale=-0.5),
                            [], inc="sc")
            return tok_rinv

        def emit_ln_h(qc, src_off, dst, dst_off, which, tok_rinv, h_extra=()):
            o = 8 * which
            bias_base = 24 * which
            ths = []
            for c in range(DC):
                xs = xsb[:, src_off(c): src_off(c) + W]
                pp = c % 2
                hs = hscr[:, pp * W:(pp + 1) * W]
                b.op("v", lambda v, xs=xs, hs=hs: v.tensor_sub(hs, xs, m_sb[:]),
                     [tok_rinv, hscr_free[pp]], inc="v")
                tv = b.op("v", lambda v, hs=hs: v.tensor_mul(hs, hs, rinv_sb[:]), [], inc="v")
                th = b.op("sc", lambda s, hs=hs, c=c: s.activation(
                    dst[:, dst_off(c): dst_off(c) + W], hs, AF.Identity,
                    scale=ca1_sb[:, o + c:o + c + 1],
                    bias=mod_sb[:, bias_base + c:bias_base + c + 1]),
                    [tv, tok_early if which == 0 else tok_gates] + list(h_extra), inc="sc")
                hscr_free[pp] = th
                ths.append(th)
            return ths

        def emit_ln(qc, src_off, dst, dst_off, which, h_extra=()):
            tok_rinv = emit_ln_stats(qc, src_off)
            return emit_ln_h(qc, src_off, dst, dst_off, which, tok_rinv, h_extra)

        xbf_free = [None, None]
        xsq_free = [None]
        hscr_free = [None, None]
        tok_rinv_qc0 = emit_ln_stats(0, lambda c: c * NT + 0 * W)

        tok = None
        fslot_free = [None, None]
        for jj in range(16):
            i, tokd = ring_load(di["wmod_in"][:, jj * 1024:(jj + 1) * 1024])
            for d in range(DC):
                tok = b.op("t", lambda t, i=i, d=d, jj=jj: t.matmul(
                    pOB[:, jj:jj + 1],
                    wring[:, i * 1024 + d * 128:i * 1024 + (d + 1) * 128],
                    siluc[:, d:d + 1], start=(d == 0), stop=(d == 7)),
                    ([tokd, tok_silu] + (bOB.wwaits() if jj == 0 else [])) if d == 0 else [tokd],
                    inc="t")
            ring_free[i] = tok
        bOB.wrote(tok)
        tok_evA = b.op("v", lambda v: v.tensor_copy(mod_sb[:, 0:16], pOB[:, 0:16]),
                       bOB.rwaits(), inc="v")
        bOB.read(tok_evA)
        tok_early = b.op("sc", lambda s: s.activation(ca1_sb[:, 0:8], mod_sb[:, 8:16],
                                                      AF.Identity, bias=1.0),
                         [tok_evA, tok_small], inc="sc")

        pend_mod = list(range(24, 40)) + list(range(16, 24)) + list(range(40, 48))
        mod_first = [True]
        mod_last = [None]
        tok_gates = None
        tok_modsb = None

        def emit_mod_jj(jj):
            col = jj - 16
            gate_col = (16 <= jj < 24) or (40 <= jj < 48)
            first = bD.wwaits() if mod_first[0] else []
            mod_first[0] = False
            if not gate_col:
                i, tokd = ring_load(di["wmod_in"][:, jj * 1024:(jj + 1) * 1024])
                tok = None
                for d in range(DC):
                    tok = b.op("t", lambda t, i=i, d=d, col=col: t.matmul(
                        pD[:, col:col + 1],
                        wring[:, i * 1024 + d * 128:i * 1024 + (d + 1) * 128],
                        siluc[:, d:d + 1], start=(d == 0), stop=(d == 7)),
                        ([tokd, tok_silu] + first) if d == 0 else [tokd],
                        inc="t")
                ring_free[i] = tok
            else:
                tok = None
                for half in range(2):
                    fs = half
                    tokd = sdma(hscr[:, fs * W:(fs + 1) * W],
                                di["wmod_in"][:, jj * 1024 + half * 512: jj * 1024 + (half + 1) * 512],
                                waits=[fslot_free[fs], hscr_free[fs]], sem=f"h{fs}")
                    for dd in range(4):
                        d = half * 4 + dd
                        tok = b.op("t", lambda t, fs=fs, dd=dd, d=d, col=col: t.matmul(
                            pD[:, col:col + 1],
                            hscr[:, fs * W + dd * 128: fs * W + (dd + 1) * 128],
                            siluf[:, d:d + 1], start=(d == 0), stop=(d == 7)),
                            [tokd, tok_silu] + (first if d == 0 else []),
                            inc="t")
                    fslot_free[fs] = tok
            mod_last[0] = tok

        def emit_gates():
            nonlocal tok_gates, tok_modsb
            bD.wrote(mod_last[0])
            tok_modsb = b.op("v", lambda v: v.tensor_copy(mod_sb[:, 16:48], pD[:, 0:32]),
                             bD.rwaits(), inc="v")
            bD.read(tok_modsb)
            b.op("sc", lambda s: s.activation(gcl_sb[:, 0:8], mod_sb[:, 16:24], AF.Abs),
                 [tok_modsb, tok_small], inc="sc")
            tok_ab = b.op("sc", lambda s: s.activation(gcl_sb[:, 8:16], mod_sb[:, 40:48], AF.Abs),
                          [], inc="sc")
            b.op("v", lambda v: v.tensor_max(gcl_sb[:], gcl_sb[:], ctile[:, 0:16]), [tok_ab], inc="v")
            b.op("sc", lambda s: s.activation(gb_f[:, 0:8], mod_sb[:, 16:24], AF.Tanh, scale=1e30),
                 [tok_modsb], inc="sc")
            tok_sg = b.op("sc", lambda s: s.activation(gb_f[:, 8:16], mod_sb[:, 40:48], AF.Tanh, scale=1e30),
                          [], inc="sc")
            tok_dbg2 = None
            if dbg:
                tok_dbg2 = sdma(di["d_gbf"], gb_f[:], waits=[tok_sg])
            b.op("v", lambda v: v.tensor_mul(gcl_sb[:], gcl_sb[:], gb_f[:]),
                 [tok_sg, tok_dbg2], inc="v")
            b.op("sc", lambda s: s.activation(ca1_sb[:, 8:16], mod_sb[:, 32:40],
                                              AF.Identity, bias=1.0),
                 [tok_modsb], inc="sc")
            b.op("v", lambda v: v.tensor_mul(gb_f[:, 0:8], gcl_sb[:, 0:8], bo_sb[:]),
                 [tok_small], inc="v")
            tok_gates = b.op("v", lambda v: v.tensor_mul(gb_f[:, 8:16], gcl_sb[:, 8:16], b2_sb[:]),
                             [], inc="v")
            hscr_free[0] = fslot_free[0]
            hscr_free[1] = fslot_free[1]
            if dbg:
                sdma(di["d_mod"], mod_sb[:], waits=[tok_gates, tok_modsb])
                sdma(di["d_gcl"], gcl_sb[:], waits=[tok_gates])


        pc_full = [None]
        w1_pre = {}
        ln2_th = {}

        def emit_precast():
            plan = [("wo_in", "woB", 16384)]
            for fn, bn, width in plan:
                for c0 in range(0, width, 2048):
                    i, tok = wo_slot_load(di[fn][:, c0:c0 + 2048], via_sync=False)
                    tokw = sdma(di[bn][:, c0:c0 + 2048], woring[:, i * 2048:(i + 1) * 2048],
                                waits=[tok], sem="pdma")
                    wo_slot_free[i] = tokw
            pc_full[0] = tokw

        qtmp_free_t = [None, None]
        qtmp_free_v = [None, None]
        rsc_free = [None, None]
        rscr_free = [None, None]
        tok_rinv_next = tok_rinv_qc0
        cs_tok = [None]
        cs_free = [None]
        vaug_tok = [None] * NKT
        oraw_free = None
        qkv_gate = None
        for qc in range(NQC):
            gdma(csb[:, 0:512], di["cos_in"][:, qc * W:(qc + 1) * W], waits=[cs_free[0]],
                 sem="cdma")
            cs_tok[0] = gdma(csb[:, 512:1024], di["sin_in"][:, qc * W:(qc + 1) * W],
                             waits=[], sem="cdma")
            th1 = emit_ln_h(qc, lambda c, qc=qc: c * NT + qc * W, hq,
                            lambda c: c * W, 0, tok_rinv_next,
                            h_extra=[qkv_gate])
            tok_h = th1[-1]

            tok_qk = None
            pend_rope = None

            def emit_rope(mi, m, proj, tq):
                nonlocal tok_qk
                qh = qtmp[:, (mi % 2) * 512:(mi % 2) * 512 + 512]
                rb = [bOA, bOB][mi % 2]
                tr = b.op("t", lambda t, rb=rb, qh=qh: t.matmul(
                    rb.ap, permb, qh, start=True, stop=True),
                    [tq, tok_tab] + rb.wwaits(), inc="t")
                rb.wrote(tr)
                qtmp_free_t[mi % 2] = tr
                t1 = b.op("v", lambda v, qh=qh: v.tensor_mul(
                    rsc1, qh, csb[:, 0:512]),
                    [rsc_free[0], cs_tok[0], tq], inc="v")
                qtmp_free_v[mi % 2] = t1
                t2 = b.op("v", lambda v, rb=rb: v.tensor_mul(
                    rsc2, rb.ap, csb[:, 512:1024]),
                    rb.rwaits() + [rsc_free[1], t1], inc="v")
                rb.read(t2)
                cs_free[0] = t2
                if proj == 0:
                    b.op("v", lambda v, m=m, qc=qc: v.tensor_add(
                        qqA[0:64, m * W:(m + 1) * W], rsc1[0:64, :], rsc2[0:64, :]),
                        [tok_qz], inc="v")
                    t3 = b.op("v", lambda v, m=m, qc=qc: v.tensor_add(
                        qqB[64:128, m * W:(m + 1) * W], rsc1[64:128, :], rsc2[64:128, :]),
                        [], inc="v")
                else:
                    dst = ksb[:, m * NT + qc * W: m * NT + qc * W + W]
                    t3 = b.op("v", lambda v, dst=dst: v.tensor_add(dst, rsc1, rsc2),
                              [], inc="v")
                rsc_free[0] = t3
                rsc_free[1] = t3
                tok_qk = t3

            for proj in range(2):
                wsrc = di["wq_in" if proj == 0 else "wk_in"]
                for m in range(DC):
                    mi = proj * DC + m
                    if m % 2 == 0:
                        islot, tokd = wo_slot_load(wsrc[:, m * 1024:(m + 2) * 1024],
                                                   via_sync=False)
                    sub = m % 2
                    sl = slots4[mi % 4]
                    wts = [tokd] + sl.wwaits()
                    tok = None
                    for d in range(DC):
                        tok = b.op("t", lambda t, islot=islot, sub=sub, d=d, sl=sl: t.matmul(
                            sl.ap,
                            woring[:, islot * 2048 + sub * 1024 + d * 128:
                                   islot * 2048 + sub * 1024 + (d + 1) * 128],
                            hq[:, d * W:(d + 1) * W], start=(d == 0), stop=(d == 7)),
                            (wts if d == 0 else []) + [th1[d]], inc="t")
                    sl.wrote(tok)
                    if sub == 1:
                        wo_slot_free[islot] = tok
                    tq = b.op("sc", lambda s, sl=sl, mi=mi: s.copy(
                        qtmp[:, (mi % 2) * 512:(mi % 2) * 512 + 512], sl.ap),
                        sl.rwaits() + [qtmp_free_t[mi % 2], qtmp_free_v[mi % 2]], inc="sc")
                    sl.read(tq)
                    if pend_rope is not None:
                        emit_rope(*pend_rope)
                    pend_rope = (mi, m, proj, tq)
                    if qc == 0 and pend_mod:
                        emit_mod_jj(pend_mod.pop(0))
            emit_rope(*pend_rope)
            pend_rope = None

            wv_slots = []
            for dpair in range(4):
                islot, tokd = wo_slot_load(
                    di["wv_in"][:, dpair * 2048:(dpair + 1) * 2048], via_sync=False)
                wv_slots.append((islot, tokd))
            for wave in range(2):
                for d in range(DC):
                    islot, tokd = wv_slots[d // 2]
                    sub = d % 2
                    for tw in range(2):
                        tt = wave * 2 + tw
                        h0, h1, bk0, bk1 = vbank_pairs[tw]
                        wts = [tokd, th1[d]] + (bk0.wwaits() + bk1.wwaits() if d == 0 else [])
                        b.op("t", lambda t, islot=islot, sub=sub, d=d, tt=tt, h0=h0: t.matmul(
                            h0, hq[:, d * W + tt * 128: d * W + tt * 128 + 128],
                            woring[:, islot * 2048 + sub * 1024: islot * 2048 + sub * 1024 + 512],
                            start=(d == 0), stop=(d == 7)), wts, inc="t")
                        tok = b.op("t", lambda t, islot=islot, sub=sub, d=d, tt=tt, h1=h1: t.matmul(
                            h1, hq[:, d * W + tt * 128: d * W + tt * 128 + 128],
                            woring[:, islot * 2048 + sub * 1024 + 512: islot * 2048 + sub * 1024 + 1024],
                            start=(d == 0), stop=(d == 7)), [], inc="t")
                        if d == 7:
                            bk0.wrote(tok)
                            bk1.wrote(tok)
                    if qc == 0 and wave == 0 and pend_mod:
                        emit_mod_jj(pend_mod.pop(0))
                if wave == 1:
                    for isl, _ in wv_slots:
                        wo_slot_free[isl] = tok
                for tw in range(2):
                    tt = wave * 2 + tw
                    kt = 4 * qc + tt
                    h0, h1, bk0, bk1 = vbank_pairs[tw]
                    vv_ = vaug[:, kt * 1040:(kt + 1) * 1040].rearrange("p (h x) -> p h x", x=65)
                    pa = h0.rearrange("p (h x) -> p h x", x=64)
                    pb = h1.rearrange("p (h x) -> p h x", x=64)
                    b.op("v", lambda v, vv=vv_, pa=pa: v.tensor_copy(vv[:, 0:8, 0:64], pa),
                         bk0.rwaits() + bk1.rwaits(), inc="v")
                    b.op("v", lambda v, vv=vv_, pb=pb: v.tensor_copy(vv[:, 8:16, 0:64], pb),
                         [], inc="v")
                    tokm = b.op("v", lambda v, vv=vv_: v.memset(vv[:, :, 64:65], 1.0), [], inc="v")
                    bk0.read(tokm)
                    bk1.read(tokm)
                    vaug_tok[kt] = tokm

            if qc == 1:
                emit_precast()
            if qc == 3:
                for j in range(3):
                    w1_pre[j] = ring_load(di["w1_in"][:, j * 1024:(j + 1) * 1024])

            Sbufs = [(psA, bA0, bA1), (psB, bB0, bB1)]
            exp_tok = [None, None]
            pv_tok = [None, None]
            nkt = 4 * qc + 4
            tokP = None
            tokg = None
            pend_gather = None

            def emit_gather(hp, toA, toB):
                wts = [toA, toB, tok_tab]
                if hp == 0:
                    wts = wts + bC.wwaits()
                b.op("t", lambda t, hp=hp: t.matmul(
                    pC[0:16, :], selgb[:, (2 * hp) * 16:(2 * hp) * 16 + 16],
                    big[:, 41216 + (2 * hp) * 512: 41216 + (2 * hp) * 512 + 512],
                    start=(hp == 0), stop=False), wts, inc="t")
                return b.op("t", lambda t, hp=hp: t.matmul(
                    pC[0:16, :], selgb[:, (2 * hp + 1) * 16:(2 * hp + 1) * 16 + 16],
                    big[:, 41216 + (2 * hp + 1) * 512: 41216 + (2 * hp + 1) * 512 + 512],
                    start=False, stop=(hp == 7)), [], inc="t")

            for hp in range(DC):
                if qc == 0 and pend_mod:
                    emit_mod_jj(pend_mod.pop(0))
                pend = None

                def emit_S(kt):
                    lo = max(0, kt * 128 - qc * W)
                    w = W - lo
                    ps, s0, s1 = Sbufs[kt % 2]
                    wts = [tok_qk, exp_tok[kt % 2]] + s0.wwaits() + s1.wwaits()
                    b.op("t", lambda t, ps=ps, hp=hp, kt=kt, lo=lo, w=w: t.matmul(
                        ps[:, lo:lo + w],
                        ksb[:, hp * NT + kt * 128: hp * NT + (kt + 1) * 128],
                        qqA[:, hp * W + lo: hp * W + lo + w],
                        start=True, stop=True), wts, inc="t")
                    tokB = b.op("t", lambda t, ps=ps, hp=hp, kt=kt, lo=lo, w=w: t.matmul(
                        ps[:, 512 + lo:512 + lo + w],
                        ksb[:, hp * NT + kt * 128: hp * NT + (kt + 1) * 128],
                        qqB[:, hp * W + lo: hp * W + lo + w],
                        start=True, stop=True), [], inc="t")
                    s0.wrote(tokB)
                    s1.wrote(tokB)
                    pin = ps[:].rearrange("p (h x) -> p h x", x=512)[:, :, lo:lo + w]
                    pout = ptb[:, (kt % 2) * 1024:(kt % 2 + 1) * 1024].rearrange(
                        "p (h x) -> p h x", x=512)[:, :, lo:lo + w]
                    te = b.op("sc", lambda s, pin=pin, pout=pout: s.activation(
                        pout, pin, AF.Exp, scale=SCALE, bias=cst[:, 1:2]),
                        [tokB, tok_small, tok_qk], inc="sc")
                    s0.read(te)
                    s1.read(te)
                    exp_tok[kt % 2] = te
                    tok_ready = te
                    if kt >= 4 * qc:
                        base_ = (kt % 2) * 1024
                        pmA = ptb[:, base_ + lo: base_ + lo + 128]
                        pmB = ptb[:, base_ + 512 + lo: base_ + 512 + lo + 128]
                        b.op("v", lambda v, pm=pmA: v.tensor_mul(pm, pm, dmaskb),
                             [te, tok_tab], inc="v")
                        tok_ready = b.op("v", lambda v, pm=pmB: v.tensor_mul(pm, pm, dmaskb),
                                         [], inc="v")
                    return tok_ready

                def emit_PV(kt, tok_ready):
                    nonlocal tokP
                    lo = max(0, kt * 128 - qc * W)
                    w = W - lo
                    wtsA = [tok_ready, vaug_tok[kt]]
                    if kt == 0:
                        wtsA = wtsA + bOA.wwaits() + bOB.wwaits()
                    vaA = vaug[:, kt * 1040 + (2 * hp) * 65: kt * 1040 + (2 * hp) * 65 + 65]
                    vaB = vaug[:, kt * 1040 + (2 * hp + 1) * 65: kt * 1040 + (2 * hp + 1) * 65 + 65]
                    b.op("t", lambda t, va=vaA, kt=kt, lo=lo, w=w, last=(kt == nkt - 1): t.matmul(
                        pOA[0:65, lo:lo + w], va,
                        ptb[:, (kt % 2) * 1024 + lo:(kt % 2) * 1024 + lo + w],
                        start=(kt == 0), stop=last), wtsA, inc="t")
                    tokP = b.op("t", lambda t, va=vaB, kt=kt, lo=lo, w=w, last=(kt == nkt - 1): t.matmul(
                        pOB[0:65, lo:lo + w], va,
                        ptb[:, (kt % 2) * 1024 + 512 + lo:(kt % 2) * 1024 + 512 + lo + w],
                        start=(kt == 0), stop=last), [], inc="t")
                    pv_tok[kt % 2] = tokP

                for kt in range(nkt):
                    tr = emit_S(kt)
                    if kt == 0 and pend_gather is not None:
                        emit_gather(*pend_gather)
                        pend_gather = None
                    if pend is not None:
                        emit_PV(*pend)
                    pend = (kt, tr)
                emit_PV(*pend)
                bOA.wrote(tokP)
                bOB.wrote(tokP)
                toA = b.op("sc", lambda s, hp=hp: s.copy(
                    oraw[:, (2 * hp) * 512:(2 * hp) * 512 + 512], pOA[0:65, :]),
                    bOA.rwaits() + [oraw_free], inc="sc")
                bOA.read(toA)
                toB = b.op("v", lambda v, hp=hp: v.tensor_copy(
                    oraw[:, (2 * hp + 1) * 512:(2 * hp + 1) * 512 + 512], pOB[0:65, :]),
                    bOB.rwaits(), inc="v")
                bOB.read(toB)
                pend_gather = (hp, toA, toB)
            if qc == 0:
                assert not pend_mod
                emit_gates()
            tokg = emit_gather(*pend_gather)
            pend_gather = None
            bC.wrote(tokg)
            tok = b.op("sc", lambda s: s.activation(var_sb[0:16, :], pC[0:16, :], AF.Ln),
                       bC.rwaits() + [rsc_free[0], hscr_free[0], hscr_free[1]], inc="sc")
            bC.read(tok)
            tok = b.op("sc", lambda s: s.activation(linv_f, var_sb[0:16, :], AF.Exp, scale=-1.0),
                       [], inc="sc")
            tok_linv = b.op("v", lambda v: v.tensor_copy(linv_b[0:16, :], linv_f), [tok], inc="v")
            if qc < NQC - 1:
                tok_rinv_next = emit_ln_stats(qc + 1, lambda c, q2=qc + 1: c * NT + q2 * W,
                                              pm=psA[:, 0:512], ps2=psA[:, 512:1024],
                                              bm=bA0, bs2=bA1)
            hscr_free[0] = tok_linv
            hscr_free[1] = tok_linv
            if qc == NQC - 1:
                ln2_tr0 = emit_ln_stats(0, lambda c: c * NT,
                                        pm=psA[:, 0:512], ps2=psA[:, 512:1024],
                                        bm=bA0, bs2=bA1)
                ln2_th[0] = emit_ln_h(0, lambda c: c * NT, h2,
                                      lambda c: c * NT, 1, ln2_tr0)
                ln2_tr1 = emit_ln_stats(1, lambda c: c * NT + W,
                                        pm=psB[:, 0:512], ps2=psB[:, 512:1024],
                                        bm=bB0, bs2=bB1)
                ln2_th[1] = emit_ln_h(1, lambda c: c * NT + W, h2,
                                      lambda c: c * NT + W, 1, ln2_tr1)
            norm_last = None
            for h in range(H):
                lb = [bD, bC][h % 2]
                tokL = b.op("t", lambda t, h=h, lb=lb: t.matmul(
                    lb.ap, ehmb[:, h * 128:(h + 1) * 128], linv_b,
                    start=True, stop=True), [tok_linv] + lb.wwaits(), inc="t")
                lb.wrote(tokL)
                tokN = b.op("v", lambda v, h=h, lb=lb: v.tensor_mul(
                    oraw[0:64, h * 512:h * 512 + 512], lb.ap[0:64, :],
                    oraw[0:64, h * 512:h * 512 + 512]), lb.rwaits(), inc="v")
                lb.read(tokN)
                norm_last = tokN
            rsc_free[0] = tokL
            rsc_free[1] = tokL

            rscr_free[0] = pv_tok[0] if pv_tok[0] is not None else rscr_free[0]
            rscr_free[1] = pv_tok[1] if pv_tok[1] is not None else rscr_free[1]
            tok_wo = None
            for e in range(DC):
                wsrc = di["wo_in"] if qc == 0 else di["woB"]
                i, tokd = wo_slot_load(wsrc[:, e * 2048:(e + 1) * 2048], via_sync=(qc > 0),
                                       extra=([pc_full[0]] if qc > 0 else []))
                ab = [bOA, bOB][e % 2]
                wts = [tokd, norm_last] + ab.wwaits()
                tok = None
                for h in range(H):
                    tok = b.op("t", lambda t, e=e, h=h, ab=ab, i=i: t.matmul(
                        ab.ap,
                        woring[:, i * 2048 + h * 128:i * 2048 + (h + 1) * 128],
                        big[:, 41216 + h * 512: 41216 + h * 512 + 512],
                        start=(h == 0), stop=(h == 15)),
                        wts if h == 0 else [], inc="t")
                ab.wrote(tok)
                wo_slot_free[i] = tok
                tok_wo = tok
                xs = xsb[:, e * NT + qc * W: e * NT + qc * W + W]
                rscr = rscrF[:, (e % 2) * W:(e % 2) * W + W]
                tok = b.op("sc", lambda s, e=e, ab=ab, rs=rscr: s.activation(
                    rs, ab.ap, AF.Identity,
                    scale=gcl_sb[:, e:e + 1], bias=gb_f[:, e:e + 1]),
                    ab.rwaits() + [tok_gates, rscr_free[e % 2]], inc="sc")
                ab.read(tok)
                resid_last = b.op("v", lambda v, xs=xs, rs=rscr: v.tensor_add(
                    xs, xs, rs), [tok], inc="v")
                rscr_free[e % 2] = resid_last
            oraw_free = tok_wo
            qkv_gate = tok_wo
            if dbg and qc == 0:
                sdma(di["d_h1"], hq, waits=[tok_h])
                sdma(di["d_qq"], qqA, waits=[tok_qk])
                kview = ksb.rearrange("p (m n) -> p m n", n=NT)[:, :, 0:W]
                sdma(di["d_k"], kview, waits=[tok_qk])
                sdma(di["d_vg"], vaug[:, 0:4 * 1040], waits=[vaug_tok[3]])
                sdma(di["d_orw"], oraw, waits=[norm_last])
            if dbg and qc == NQC - 1:
                sdma(di["d_xa"], xsb[:], waits=[resid_last])

        tbar = b.mark("t")
        vbar = b.mark("v")

        tok_h2 = ln2_th[1][-1]

        if dbg:
            sdma(di["d_h2"], h2, waits=[tok_h2])
        w2_free = [None, None]
        g_free = [None]
        tok_h2b = None
        for th in range(2):
            tok_gall = None
            for j in range(FJ):
                if j in w1_pre:
                    i, tokd = w1_pre.pop(j)
                else:
                    i, tokd = ring_load(di["w1_in"][:, j * 1024:(j + 1) * 1024])
                for qch in range(2):
                    sl = slots4[(j * 2 + qch) % 4]
                    base = th * 1024 + qch * W
                    ths = ln2_th[th * 2 + qch]
                    wts = [tokd] + sl.wwaits() + ([tbar] if (th == 0 and j < 2) else [])
                    tok = None
                    for d in range(DC):
                        tok = b.op("t", lambda t, i=i, d=d, sl=sl, base=base: t.matmul(
                            sl.ap, wring[:, i * 1024 + d * 128:i * 1024 + (d + 1) * 128],
                            h2[:, d * NT + base: d * NT + base + W],
                            start=(d == 0), stop=(d == 7)),
                            (wts if d == 0 else []) + [ths[d]], inc="t")
                    sl.wrote(tok)
                    if qch == 1:
                        ring_free[i] = tok
                    tokg2 = b.op("sc", lambda s, sl=sl, j=j, qch=qch, th=th: s.activation(
                        g_sb[:, j * 1024 + qch * W: j * 1024 + qch * W + W], sl.ap,
                        AF.Gelu, bias=b1_sb[:, j:j + 1]),
                        sl.rwaits() + [g_free[0], tok_small], inc="sc")
                    sl.read(tokg2)
                    tok_gall = tokg2
            if dbg and th == 0:
                sdma(di["d_g"], g_sb, waits=[tok_gall])
            if th == 0:
                for j in range(3):
                    w1_pre[j] = ring_load(di["w1_in"][:, j * 1024:(j + 1) * 1024])
            for e in range(DC):
                if th == 0 and e in (0, 1, 2):
                    if e == 0:
                        ln2_tr2 = emit_ln_stats(2, lambda c: c * NT + 2 * W)
                    elif e == 1:
                        ln2_th[2] = emit_ln_h(2, lambda c: c * NT + 2 * W, h2,
                                              lambda c: c * NT + 2 * W, 1, ln2_tr2)
                        ln2_tr3 = emit_ln_stats(3, lambda c: c * NT + 3 * W)
                    else:
                        ln2_th[3] = emit_ln_h(3, lambda c: c * NT + 3 * W, h2,
                                              lambda c: c * NT + 3 * W, 1, ln2_tr3)
                tokd = gdma(w2ring[:, (e % 2) * 4096:(e % 2 + 1) * 4096],
                            di["w2_in"][:, e * 4096:(e + 1) * 4096],
                            waits=[w2_free[e % 2]] + ([tbar, vbar] if (th == 0 and e < 2) else []),
                            sem=f"w2{e % 2}")
                for qch in range(2):
                    yb = [bOA, bOB][(e * 2 + qch) % 2]
                    wts = [tokd, tok_gall] + yb.wwaits()
                    tok = None
                    for j in range(FJ):
                        tok = b.op("t", lambda t, e=e, j=j, qch=qch, yb=yb: t.matmul(
                            yb.ap,
                            w2ring[:, (e % 2) * 4096 + j * 128:(e % 2) * 4096 + (j + 1) * 128],
                            g_sb[:, j * 1024 + qch * W: j * 1024 + qch * W + W],
                            start=(j == 0), stop=(j == 31)),
                            wts if j == 0 else [], inc="t")
                    yb.wrote(tok)
                    g_free[0] = tok
                    w2_free[e % 2] = tok
                    xs = xsb[:, e * NT + th * 1024 + qch * W: e * NT + th * 1024 + qch * W + W]
                    rscr = rscrF[:, ((e * 2 + qch) % 2) * W:((e * 2 + qch) % 2) * W + W]
                    tok = b.op("sc", lambda s, e=e, yb=yb, rs=rscr: s.activation(
                        rs, yb.ap, AF.Identity,
                        scale=gcl_sb[:, 8 + e:8 + e + 1], bias=gb_f[:, 8 + e:8 + e + 1]),
                        yb.rwaits() + [rscr_free[(e * 2 + qch) % 2]], inc="sc")
                    yb.read(tok)
                    tok = b.op("v", lambda v, xs=xs, rs=rscr: v.tensor_add(
                        xs, xs, rs), [tok], inc="v")
                    rscr_free[(e * 2 + qch) % 2] = tok
                    yo = di["y_out"][:, e * NT + th * 1024 + qch * W:
                                     e * NT + th * 1024 + qch * W + W]
                    sdma(yo, xs, waits=[tok])

        sems = {}
        for nm in sem_names + ["t", "v", "sc"]:
            sems[nm] = E(nc.semaphore(nm))
        block = E(nc.Block())
        n_out = b.cnt["xdma"]

        @block.tensor
        def _(t):
            b.emit(t, "t", sems)

        @block.vector
        def _(v):
            b.emit(v, "v", sems)

        @block.scalar
        def _(s):
            b.emit(s, "sc", sems)

        @block.gpsimd
        def _(g):
            b.emit(g, "g", sems)

        @block.sync
        def _(s):
            b.emit(s, "s", sems)
            s.wait_ge(sems["xdma"], n_out)

    return nc



def _tile_lhs(A):
    K, M = A.shape
    return np.ascontiguousarray(
        A.reshape(K // 128, 128, M // 128, 128).transpose(1, 2, 0, 3).reshape(128, -1))


def _make_consts():
    n = np.arange(NT, dtype=np.float64)
    p = np.arange(128)
    i = (p % 64) // 2
    inv = 1.0 / (10000.0 ** (2.0 * i / 64.0))
    ang = n[None, :] * inv[:, None]
    cos = np.cos(ang).astype(np.float32)
    sin = np.sin(ang).astype(np.float32)
    perm = np.zeros((128, 128), np.float32)
    for ii in range(64):
        perm[2 * ii + 1, 2 * ii] = -1.0
        perm[2 * ii, 2 * ii + 1] = 1.0
    dmask = np.triu(np.ones((128, 128), np.float32))
    onesln = np.full((128, 128), 1.0 / 1024.0, np.float32)
    selg = np.zeros((128, 256), np.float32)
    for h in range(16):
        selg[64, h * 16 + h] = 1.0
    ehm = np.zeros((128, 2048), np.float32)
    for h in range(16):
        ehm[h, h * 128:(h + 1) * 128] = 1.0
    cstm = np.zeros((128, 8), np.float32)
    cstm[:, 0] = EPS
    cstm[:, 1] = -MSHIFT
    cstm[:, 2] = 1.0
    cstm[:, 3] = 0.25
    cstm[:, 4] = 1e-30
    ctile = np.zeros((128, 64), np.float32)
    ctile[:, 0:16] = 0.25
    ctile[:, 32:48] = 1.0
    ctile[:, 48:64] = 1e38
    return cos, sin, perm, dmask, onesln, selg, ehm, cstm, ctile


def kernel(L0_attn_x, L0_attn_Wq, L0_attn_Wk, L0_attn_Wv, L0_attn_Wo, L0_attn_bo,
           L0_ffn_w1, L0_ffn_b1, L0_ffn_w2, L0_ffn_b2, L0_c, L0_wmod):
    x = np.asarray(L0_attn_x, np.float32)
    Wq = np.asarray(L0_attn_Wq, np.float32)
    Wk = np.asarray(L0_attn_Wk, np.float32)
    Wv = np.asarray(L0_attn_Wv, np.float32)
    Wo = np.asarray(L0_attn_Wo, np.float32)
    bo = np.asarray(L0_attn_bo, np.float32)
    w1 = np.asarray(L0_ffn_w1, np.float32)
    b1 = np.asarray(L0_ffn_b1, np.float32)
    w2 = np.asarray(L0_ffn_w2, np.float32)
    b2 = np.asarray(L0_ffn_b2, np.float32)
    c = np.asarray(L0_c, np.float32)
    wmod = np.asarray(L0_wmod, np.float32)

    wq_h = _tile_lhs(np.ascontiguousarray(Wq.T))
    wk_h = _tile_lhs(np.ascontiguousarray(Wk.T))
    wv_h = np.ascontiguousarray(Wv.T.reshape(8, 128, 1024).transpose(1, 0, 2).reshape(128, 8192))
    wo_h = np.zeros((128, 16384), np.float32)
    wo_h[0:64] = Wo.T.reshape(16, 64, 8, 128).transpose(1, 2, 0, 3).reshape(64, 16384)
    w1_h = _tile_lhs(w1)
    w2_h = _tile_lhs(w2)
    wmod_h = _tile_lhs(wmod)
    cos, sin, perm, dmask, onesln, selg, ehm, cstm, ctile_h = _make_consts()
    bo_h = np.ascontiguousarray(bo.reshape(8, 128).T)
    b1_h = np.ascontiguousarray(b1.reshape(32, 128).T)
    b2_h = np.ascontiguousarray(b2.reshape(8, 128).T)

    nc = build_program()
    shared = dict(wq_in=wq_h, wk_in=wk_h, wv_in=wv_h, wo_in=wo_h, w1_in=w1_h,
                  w2_in=w2_h, wmod_in=wmod_h, cos_in=cos, sin_in=sin,
                  perm_in=perm, dmask_in=dmask, onesln_in=onesln, selg_in=selg,
                  ehm_in=ehm, cst_in=cstm, ctile_in=ctile_h,
                  bo_in=bo_h, b1_in=b1_h, b2_in=b2_h)
    in_maps = []
    for bi in range(8):
        xb = np.ascontiguousarray(
            x[bi].T.reshape(8, 128, NT).transpose(1, 0, 2).reshape(128, 8 * NT))
        cb = np.ascontiguousarray(c[bi].reshape(8, 128).T)
        m = dict(shared)
        m["x_in"] = xb
        m["c_in"] = cb
        in_maps.append(m)

    res = run_bass_kernel_spmd(nc, in_maps, core_ids=list(range(8)))
    outs = []
    for bi in range(8):
        y = np.asarray(res.results[bi]["y_out"])
        outs.append(y.reshape(128, 8, NT).transpose(1, 0, 2).reshape(1024, NT).T)
    return np.stack(outs).astype(np.float32)


# version 42
# speedup vs baseline: 1.1607x; 1.1607x over previous
import numpy as np
from contextlib import ExitStack

import concourse.bass as bass
import concourse.mybir as mybir
from concourse.bass_utils import run_bass_kernel_spmd

F32 = mybir.dt.float32
BF16 = mybir.dt.bfloat16
AF = mybir.ActivationFunctionType
ALU = mybir.AluOpType

D = 1024
NT = 2048
DC = 8
NQC = 4
W = 512
NKT = 16
H = 16
FJ = 32
MSHIFT = 16.0
EPS = 1e-5
SCALE = 1.0 / 64.0

ENGS = ("t", "v", "sc", "g", "s")


class Builder:
    def __init__(self):
        self.ops = {e: [] for e in ENGS}
        self.cnt = {}
        self.owner = {}
        self.waited = {e: {} for e in ENGS}

    def sem(self, name, owner=None):
        self.cnt[name] = 0
        self.owner[name] = owner

    def op(self, eng, fn, waits=(), inc=None, amt=1):
        w2 = []
        for tok in waits:
            if tok is None:
                continue
            s, v = tok
            if v <= 0 or self.owner.get(s) == eng:
                continue
            if self.waited[eng].get(s, 0) >= v:
                continue
            self.waited[eng][s] = v
            w2.append((s, v))
        rtok = None
        if inc is not None:
            self.cnt[inc] += amt
            rtok = (inc, self.cnt[inc])
        self.ops[eng].append((fn, w2, inc, amt))
        return rtok

    def mark(self, eng_sem):
        return (eng_sem, self.cnt[eng_sem])

    def emit(self, engobj, eng, sems):
        for fn, waits, inc, amt in self.ops[eng]:
            for s, v in waits:
                engobj.wait_ge(sems[s], v)
            inst = fn(engobj)
            if inc is not None:
                inst.then_inc(sems[inc], amt)


class Bank:

    def __init__(self, ap):
        self.ap = ap
        self.wtok = None
        self.rtoks = []

    def wwaits(self):
        return list(self.rtoks) + [self.wtok]

    def wrote(self, tok):
        self.wtok = tok
        self.rtoks = []

    def rwaits(self):
        return [self.wtok]

    def read(self, tok):
        self.rtoks.append(tok)


def build_program(dbg=False):
    nc = bass.Bass("TRN2", target_bir_lowering=False)

    di = {}
    for name, shape in [
        ("x_in", [128, DC * NT]), ("c_in", [128, 8]), ("bo_in", [128, 8]),
        ("b1_in", [128, 32]), ("b2_in", [128, 8]), ("cst_in", [128, 8]),
        ("wq_in", [128, 8192]), ("wk_in", [128, 8192]), ("wv_in", [128, 8192]),
        ("wo_in", [128, 16384]), ("w1_in", [128, 32768]), ("w2_in", [128, 32768]),
        ("wmod_in", [128, 49152]), ("cos_in", [128, NT]), ("sin_in", [128, NT]),
        ("perm_in", [128, 128]), ("dmask_in", [128, 128]), ("onesln_in", [128, 128]),
        ("selg_in", [128, 256]), ("ehm_in", [128, 2048]), ("ctile_in", [128, 64]),
    ]:
        di[name] = nc.dram_tensor(name, shape, F32, kind="ExternalInput").ap()
    di["y_out"] = nc.dram_tensor("y_out", [128, DC * NT], F32, kind="ExternalOutput").ap()
    for name, shape in [("wqB", [128, 8192]), ("wkB", [128, 8192]), ("wvB", [128, 8192]),
                        ("woB", [128, 16384]), ("w1B", [128, 32768]), ("w2B", [128, 32768])]:
        di[name] = nc.dram_tensor(name, shape, BF16, kind="Internal").ap()
    if dbg:
        for nm, shape, dt_ in [("d_mod", [128, 48], F32),
                               ("d_gbf", [128, 16], F32), ("d_h1", [128, DC * W], BF16),
                               ("d_qq", [128, DC * W], BF16), ("d_k", [128, DC * W], BF16),
                               ("d_vg", [128, 4 * 1040], BF16), ("d_orw", [65, 16 * 512], BF16),
                               ("d_xa", [128, DC * NT], F32), ("d_h2", [128, DC * NT], BF16),
                               ("d_g", [128, FJ * 1024], BF16),
                               ("d_w2", [128, 4096], BF16), ("d_y", [128, 512], F32),
                               ("d_gcl", [128, 16], F32)]:
            di[nm] = nc.dram_tensor(nm, shape, dt_, kind="ExternalOutput").ap()

    b = Builder()
    sem_names = (["xdma", "tdma", "cdma", "pdma"]
                 + [f"r{i}" for i in range(3)] + [f"q{i}" for i in range(4)]
                 + ["h0", "h1", "w20", "w21"])
    for s in sem_names:
        b.sem(s, None)
    for s, o in [("t", "t"), ("v", "v"), ("sc", "sc")]:
        b.sem(s, o)

    with ExitStack() as ES:
        E = ES.enter_context
        xsb = E(nc.sbuf_tensor("xsb", [128, DC * NT], F32))
        wring = E(nc.sbuf_tensor("wring", [128, 3 * 1024], BF16))
        c_sb = E(nc.sbuf_tensor("c_sb", [128, 8], F32))
        bo_sb = E(nc.sbuf_tensor("bo_sb", [128, 8], F32))
        b1_sb = E(nc.sbuf_tensor("b1_sb", [128, 32], F32))
        b2_sb = E(nc.sbuf_tensor("b2_sb", [128, 8], F32))
        cst = E(nc.sbuf_tensor("cst", [128, 8], F32))
        ctile = E(nc.sbuf_tensor("ctile", [128, 64], F32))
        siluc = E(nc.sbuf_tensor("siluc", [128, 8], BF16))
        siluf = E(nc.sbuf_tensor("siluf", [128, 8], F32))
        mod_sb = E(nc.sbuf_tensor("mod_sb", [128, 48], F32))
        sgn_sb = E(nc.sbuf_tensor("sgn_sb", [128, 16], F32))
        gcl_sb = E(nc.sbuf_tensor("gcl_sb", [128, 16], F32))
        ca1_sb = E(nc.sbuf_tensor("ca1_sb", [128, 16], F32))
        gb_f = E(nc.sbuf_tensor("gb_f", [128, 16], F32))
        onesln = E(nc.sbuf_tensor("onesln", [128, 128], BF16))
        xbf = E(nc.sbuf_tensor("xbf", [128, 2 * W], BF16))
        xsq = E(nc.sbuf_tensor("xsq", [128, W], BF16))
        m_sb = E(nc.sbuf_tensor("m_sb", [128, W], F32))
        var_sb = E(nc.sbuf_tensor("var_sb", [128, W], F32))
        hscr = E(nc.sbuf_tensor("hscr", [128, 2 * W], F32))
        rinv_sb = E(nc.sbuf_tensor("rinv_sb", [128, W], F32))

        big = E(nc.sbuf_tensor("big", [128, 54016], BF16))
        ksb = big[:, 0:16384]
        vaug = big[:, 16384:33024]
        qqA = big[:, 33024:37120]
        qqB = big[:, 37120:41216]
        hq = big[:, 41216:45312]
        oraw = big[0:65, 41216:49408]
        ptb = big[:, 49408:51456]
        qtmp = big[:, 49408:50432]
        rsc1 = big[:, 50432:50944]
        rsc2 = big[:, 50944:51456]
        permb = big[:, 51456:51584]
        dmaskb = big[:, 51584:51712]
        selgb = big[:, 51712:51968]
        ehmb = big[:, 51968:54016]
        h2 = big[:, 0:16384]
        rscrF = big[:, 49408:51456].bitcast(F32)
        g_sb = big[:, 16384:49152]
        linv_b = big[:, 50432:50944]

        u3 = E(nc.sbuf_tensor("u3", [128, 8192], BF16))
        woring = u3[:, 0:8192]
        w2ring = u3[:, 0:8192]
        csb = E(nc.sbuf_tensor("csb", [128, 1024], BF16))
        linv_f = hscr[0:16, 0:W]

        pS = E(nc.psum_tensor("pS", [128, 2048], F32))
        psA = pS[:, 0:1024]
        psB = pS[:, 1024:2048]
        pOA = E(nc.psum_tensor("pOA", [128, 512], F32))
        pOB = E(nc.psum_tensor("pOB", [128, 512], F32))
        pC = E(nc.psum_tensor("pC", [128, 512], F32))
        pD = E(nc.psum_tensor("pD", [128, 512], F32))

        bA0, bA1 = Bank(pS[:, 0:512]), Bank(pS[:, 512:1024])
        bB0, bB1 = Bank(pS[:, 1024:1536]), Bank(pS[:, 1536:2048])
        bOA, bOB, bC, bD = Bank(pOA[:]), Bank(pOB[:]), Bank(pC[:]), Bank(pD[:])
        slots4 = [bA0, bA1, bB0, bB1]
        vbank_pairs = [(pS[:, 0:512], pS[:, 512:1024], bA0, bA1),
                       (pS[:, 1024:1536], pS[:, 1536:2048], bB0, bB1),
                       (pOA[:], pOB[:], bOA, bOB),
                       (pC[:], pD[:], bC, bD)]

        def gdma(dst_ap, src_ap, waits=(), sem="tdma"):
            return b.op("g", lambda g, d=dst_ap, s=src_ap: g.dma_start(d, s),
                        waits, inc=sem, amt=16)

        def sdma(dst_ap, src_ap, waits=(), sem="xdma"):
            return b.op("s", lambda s_, d=dst_ap, s=src_ap: s_.dma_start(d, s),
                        waits, inc=sem, amt=16)

        sdma(xsb[:], di["x_in"])
        tok_small = None
        for nm, t_ in [("c_in", c_sb), ("bo_in", bo_sb), ("b1_in", b1_sb),
                       ("b2_in", b2_sb), ("cst_in", cst), ("ctile_in", ctile)]:
            tok_small = sdma(t_[:], di[nm])
        tok_x = tok_small

        tok_tab = None
        for nm, t_ in [("perm_in", permb), ("dmask_in", dmaskb),
                       ("onesln_in", onesln[:]), ("selg_in", selgb), ("ehm_in", ehmb)]:
            tok_tab = gdma(t_, di[nm], sem="tdma")

        b.op("sc", lambda s: s.activation(siluc[:], c_sb[:], AF.Silu),
             [tok_small], inc="sc")
        b.op("v", lambda v: v.memset(big[64:128, 45312:49408], 0.0), [], inc="v")
        tok_qz = b.op("v", lambda v: v.memset(qqA[64:128, :], 0.0), [], inc="v")
        tok_qz = b.op("v", lambda v: v.memset(qqB[0:64, :], 0.0), [], inc="v")
        tok_silu = b.op("sc", lambda s: s.activation(siluf[:], c_sb[:], AF.Silu),
                        [], inc="sc")

        ring_free = [None] * 3
        ridx = [0]
        wo_slot_free = [None] * 4
        wo_slot_tok = [None] * 4
        woidx = [0]

        def wo_slot_load(src_slice, via_sync, extra=()):
            i = woidx[0] % 4
            dma = sdma if via_sync else gdma
            tok = dma(woring[:, i * 2048:(i + 1) * 2048], src_slice,
                      waits=[wo_slot_free[i]] + list(extra), sem=f"q{i}")
            wo_slot_tok[i] = tok
            woidx[0] += 1
            return i, tok

        def ring_load(src_slice, extra=(), via_sync=False):
            i = ridx[0] % 3
            dma = sdma if via_sync else gdma
            tok = dma(wring[:, i * 1024:(i + 1) * 1024], src_slice,
                      waits=[ring_free[i]] + list(extra), sem=f"r{i}")
            ridx[0] += 1
            return i, tok

        def emit_ln_stats(qc, src_off, pm=None, ps2=None, bm=None, bs2=None):
            pm = pC[:] if pm is None else pm
            ps2 = pD[:] if ps2 is None else ps2
            bm = bC if bm is None else bm
            bs2 = bD if bs2 is None else bs2
            tok_stat = None
            for c in range(DC):
                xs = xsb[:, src_off(c): src_off(c) + W]
                pp = c % 2
                tv1 = b.op("v", lambda v, xs=xs, pp=pp: v.tensor_copy(
                    xbf[:, pp * W:(pp + 1) * W], xs),
                    [tok_x, xbf_free[pp]], inc="v")
                tv2 = b.op("v", lambda v, pp=pp: v.tensor_mul(
                    xsq[:], xbf[:, pp * W:(pp + 1) * W],
                    xbf[:, pp * W:(pp + 1) * W]), [xsq_free[0]], inc="v")
                wts = [tv1, tv2, tok_tab]
                if c == 0:
                    wts = wts + bm.wwaits() + bs2.wwaits()
                b.op("t", lambda t, pp=pp, c=c, pm=pm: t.matmul(
                    pm, onesln[:], xbf[:, pp * W:(pp + 1) * W],
                    start=(c == 0), stop=(c == 7)), wts, inc="t")
                t2 = b.op("t", lambda t, c=c, ps2=ps2: t.matmul(
                    ps2, onesln[:], xsq[:],
                    start=(c == 0), stop=(c == 7)), [], inc="t")
                xbf_free[pp] = t2
                xsq_free[0] = t2
                tok_stat = t2
            bm.wrote(tok_stat)
            bs2.wrote(tok_stat)
            tok = b.op("sc", lambda s, pm=pm: s.copy(m_sb[:], pm), bm.rwaits(), inc="sc")
            bm.read(tok)
            tok_sq = b.op("sc", lambda s: s.activation(var_sb[:], m_sb[:], AF.Square),
                          [], inc="sc")
            tok = b.op("v", lambda v, ps2=ps2: v.tensor_sub(var_sb[:], ps2, var_sb[:]),
                       bs2.rwaits() + [tok_sq], inc="v")
            bs2.read(tok)
            tok = b.op("sc", lambda s: s.activation(var_sb[:], var_sb[:], AF.Ln, bias=cst[:, 0:1]),
                       [tok, tok_small], inc="sc")
            tok_rinv = b.op("sc", lambda s: s.activation(rinv_sb[:], var_sb[:], AF.Exp, scale=-0.5),
                            [], inc="sc")
            return tok_rinv

        def emit_ln_h(qc, src_off, dst, dst_off, which, tok_rinv, h_extra=()):
            o = 8 * which
            bias_base = 24 * which
            ths = []
            for c in range(DC):
                xs = xsb[:, src_off(c): src_off(c) + W]
                pp = c % 2
                hs = hscr[:, pp * W:(pp + 1) * W]
                b.op("v", lambda v, xs=xs, hs=hs: v.tensor_sub(hs, xs, m_sb[:]),
                     [tok_rinv, hscr_free[pp]], inc="v")
                tv = b.op("v", lambda v, hs=hs: v.tensor_mul(hs, hs, rinv_sb[:]), [], inc="v")
                th = b.op("sc", lambda s, hs=hs, c=c: s.activation(
                    dst[:, dst_off(c): dst_off(c) + W], hs, AF.Identity,
                    scale=ca1_sb[:, o + c:o + c + 1],
                    bias=mod_sb[:, bias_base + c:bias_base + c + 1]),
                    [tv, tok_early if which == 0 else tok_gates] + list(h_extra), inc="sc")
                hscr_free[pp] = th
                ths.append(th)
            return ths

        def emit_ln(qc, src_off, dst, dst_off, which, h_extra=()):
            tok_rinv = emit_ln_stats(qc, src_off)
            return emit_ln_h(qc, src_off, dst, dst_off, which, tok_rinv, h_extra)

        xbf_free = [None, None]
        xsq_free = [None]
        hscr_free = [None, None]
        tok_rinv_qc0 = emit_ln_stats(0, lambda c: c * NT + 0 * W)

        tok = None
        fslot_free = [None, None]
        for jj in range(16):
            i, tokd = ring_load(di["wmod_in"][:, jj * 1024:(jj + 1) * 1024])
            for d in range(DC):
                tok = b.op("t", lambda t, i=i, d=d, jj=jj: t.matmul(
                    pOB[:, jj:jj + 1],
                    wring[:, i * 1024 + d * 128:i * 1024 + (d + 1) * 128],
                    siluc[:, d:d + 1], start=(d == 0), stop=(d == 7)),
                    ([tokd, tok_silu] + (bOB.wwaits() if jj == 0 else [])) if d == 0 else [tokd],
                    inc="t")
            ring_free[i] = tok
        bOB.wrote(tok)
        tok_evA = b.op("v", lambda v: v.tensor_copy(mod_sb[:, 0:16], pOB[:, 0:16]),
                       bOB.rwaits(), inc="v")
        bOB.read(tok_evA)
        tok_early = b.op("sc", lambda s: s.activation(ca1_sb[:, 0:8], mod_sb[:, 8:16],
                                                      AF.Identity, bias=1.0),
                         [tok_evA, tok_small], inc="sc")

        pend_mod = list(range(24, 40)) + list(range(16, 24)) + list(range(40, 48))
        mod_first = [True]
        mod_last = [None]
        tok_gates = None
        tok_modsb = None

        def emit_mod_jj(jj):
            col = jj - 16
            gate_col = (16 <= jj < 24) or (40 <= jj < 48)
            first = bD.wwaits() if mod_first[0] else []
            mod_first[0] = False
            if not gate_col:
                i, tokd = ring_load(di["wmod_in"][:, jj * 1024:(jj + 1) * 1024])
                tok = None
                for d in range(DC):
                    tok = b.op("t", lambda t, i=i, d=d, col=col: t.matmul(
                        pD[:, col:col + 1],
                        wring[:, i * 1024 + d * 128:i * 1024 + (d + 1) * 128],
                        siluc[:, d:d + 1], start=(d == 0), stop=(d == 7)),
                        ([tokd, tok_silu] + first) if d == 0 else [tokd],
                        inc="t")
                ring_free[i] = tok
            else:
                tok = None
                for half in range(2):
                    fs = half
                    tokd = sdma(hscr[:, fs * W:(fs + 1) * W],
                                di["wmod_in"][:, jj * 1024 + half * 512: jj * 1024 + (half + 1) * 512],
                                waits=[fslot_free[fs], hscr_free[fs]], sem=f"h{fs}")
                    for dd in range(4):
                        d = half * 4 + dd
                        tok = b.op("t", lambda t, fs=fs, dd=dd, d=d, col=col: t.matmul(
                            pD[:, col:col + 1],
                            hscr[:, fs * W + dd * 128: fs * W + (dd + 1) * 128],
                            siluf[:, d:d + 1], start=(d == 0), stop=(d == 7)),
                            [tokd, tok_silu] + (first if d == 0 else []),
                            inc="t")
                    fslot_free[fs] = tok
            mod_last[0] = tok

        def emit_gates():
            nonlocal tok_gates, tok_modsb
            bD.wrote(mod_last[0])
            tok_modsb = b.op("v", lambda v: v.tensor_copy(mod_sb[:, 16:48], pD[:, 0:32]),
                             bD.rwaits(), inc="v")
            bD.read(tok_modsb)
            b.op("sc", lambda s: s.activation(gcl_sb[:, 0:8], mod_sb[:, 16:24], AF.Abs),
                 [tok_modsb, tok_small], inc="sc")
            tok_ab = b.op("sc", lambda s: s.activation(gcl_sb[:, 8:16], mod_sb[:, 40:48], AF.Abs),
                          [], inc="sc")
            b.op("v", lambda v: v.tensor_max(gcl_sb[:], gcl_sb[:], ctile[:, 0:16]), [tok_ab], inc="v")
            b.op("sc", lambda s: s.activation(gb_f[:, 0:8], mod_sb[:, 16:24], AF.Tanh, scale=1e30),
                 [tok_modsb], inc="sc")
            tok_sg = b.op("sc", lambda s: s.activation(gb_f[:, 8:16], mod_sb[:, 40:48], AF.Tanh, scale=1e30),
                          [], inc="sc")
            tok_dbg2 = None
            if dbg:
                tok_dbg2 = sdma(di["d_gbf"], gb_f[:], waits=[tok_sg])
            b.op("v", lambda v: v.tensor_mul(gcl_sb[:], gcl_sb[:], gb_f[:]),
                 [tok_sg, tok_dbg2], inc="v")
            b.op("sc", lambda s: s.activation(ca1_sb[:, 8:16], mod_sb[:, 32:40],
                                              AF.Identity, bias=1.0),
                 [tok_modsb], inc="sc")
            b.op("v", lambda v: v.tensor_mul(gb_f[:, 0:8], gcl_sb[:, 0:8], bo_sb[:]),
                 [tok_small], inc="v")
            tok_gates = b.op("v", lambda v: v.tensor_mul(gb_f[:, 8:16], gcl_sb[:, 8:16], b2_sb[:]),
                             [], inc="v")
            hscr_free[0] = fslot_free[0]
            hscr_free[1] = fslot_free[1]
            if dbg:
                sdma(di["d_mod"], mod_sb[:], waits=[tok_gates, tok_modsb])
                sdma(di["d_gcl"], gcl_sb[:], waits=[tok_gates])


        pc_full = [None]
        w1_pre = {}

        def emit_precast():
            plan = [("wo_in", "woB", 16384)]
            for fn, bn, width in plan:
                for c0 in range(0, width, 2048):
                    i, tok = wo_slot_load(di[fn][:, c0:c0 + 2048], via_sync=False)
                    tokw = sdma(di[bn][:, c0:c0 + 2048], woring[:, i * 2048:(i + 1) * 2048],
                                waits=[tok], sem="pdma")
                    wo_slot_free[i] = tokw
            pc_full[0] = tokw

        qtmp_free_t = [None, None]
        qtmp_free_v = [None, None]
        rsc_free = [None, None]
        rscr_free = [None, None]
        tok_rinv_next = tok_rinv_qc0
        cs_tok = [None]
        cs_free = [None]
        vaug_tok = [None] * NKT
        oraw_free = None
        qkv_gate = None
        for qc in range(NQC):
            gdma(csb[:, 0:512], di["cos_in"][:, qc * W:(qc + 1) * W], waits=[cs_free[0]],
                 sem="cdma")
            cs_tok[0] = gdma(csb[:, 512:1024], di["sin_in"][:, qc * W:(qc + 1) * W],
                             waits=[], sem="cdma")
            th1 = emit_ln_h(qc, lambda c, qc=qc: c * NT + qc * W, hq,
                            lambda c: c * W, 0, tok_rinv_next,
                            h_extra=[qkv_gate])
            tok_h = th1[-1]

            tok_qk = None
            pend_rope = None

            def emit_rope(mi, m, proj, tq):
                nonlocal tok_qk
                qh = qtmp[:, (mi % 2) * 512:(mi % 2) * 512 + 512]
                rb = [bOA, bOB][mi % 2]
                tr = b.op("t", lambda t, rb=rb, qh=qh: t.matmul(
                    rb.ap, permb, qh, start=True, stop=True),
                    [tq, tok_tab] + rb.wwaits(), inc="t")
                rb.wrote(tr)
                qtmp_free_t[mi % 2] = tr
                t1 = b.op("v", lambda v, qh=qh: v.tensor_mul(
                    rsc1, qh, csb[:, 0:512]),
                    [rsc_free[0], cs_tok[0], tq], inc="v")
                qtmp_free_v[mi % 2] = t1
                t2 = b.op("v", lambda v, rb=rb: v.tensor_mul(
                    rsc2, rb.ap, csb[:, 512:1024]),
                    rb.rwaits() + [rsc_free[1], t1], inc="v")
                rb.read(t2)
                cs_free[0] = t2
                if proj == 0:
                    b.op("v", lambda v, m=m, qc=qc: v.tensor_add(
                        qqA[0:64, m * W:(m + 1) * W], rsc1[0:64, :], rsc2[0:64, :]),
                        [tok_qz], inc="v")
                    t3 = b.op("v", lambda v, m=m, qc=qc: v.tensor_add(
                        qqB[64:128, m * W:(m + 1) * W], rsc1[64:128, :], rsc2[64:128, :]),
                        [], inc="v")
                else:
                    dst = ksb[:, m * NT + qc * W: m * NT + qc * W + W]
                    t3 = b.op("v", lambda v, dst=dst: v.tensor_add(dst, rsc1, rsc2),
                              [], inc="v")
                rsc_free[0] = t3
                rsc_free[1] = t3
                tok_qk = t3

            for proj in range(2):
                wsrc = di["wq_in" if proj == 0 else "wk_in"]
                for m in range(DC):
                    mi = proj * DC + m
                    if m % 2 == 0:
                        islot, tokd = wo_slot_load(wsrc[:, m * 1024:(m + 2) * 1024],
                                                   via_sync=False)
                    sub = m % 2
                    sl = slots4[mi % 4]
                    wts = [tokd] + sl.wwaits()
                    tok = None
                    for d in range(DC):
                        tok = b.op("t", lambda t, islot=islot, sub=sub, d=d, sl=sl: t.matmul(
                            sl.ap,
                            woring[:, islot * 2048 + sub * 1024 + d * 128:
                                   islot * 2048 + sub * 1024 + (d + 1) * 128],
                            hq[:, d * W:(d + 1) * W], start=(d == 0), stop=(d == 7)),
                            (wts if d == 0 else []) + [th1[d]], inc="t")
                    sl.wrote(tok)
                    if sub == 1:
                        wo_slot_free[islot] = tok
                    tq = b.op("sc", lambda s, sl=sl, mi=mi: s.copy(
                        qtmp[:, (mi % 2) * 512:(mi % 2) * 512 + 512], sl.ap),
                        sl.rwaits() + [qtmp_free_t[mi % 2], qtmp_free_v[mi % 2]], inc="sc")
                    sl.read(tq)
                    if pend_rope is not None:
                        emit_rope(*pend_rope)
                    pend_rope = (mi, m, proj, tq)
                    if qc == 0 and pend_mod:
                        emit_mod_jj(pend_mod.pop(0))
            emit_rope(*pend_rope)
            pend_rope = None

            wv_slots = []
            for dpair in range(4):
                islot, tokd = wo_slot_load(
                    di["wv_in"][:, dpair * 2048:(dpair + 1) * 2048], via_sync=False)
                wv_slots.append((islot, tokd))
            for wave in range(2):
                for d in range(DC):
                    islot, tokd = wv_slots[d // 2]
                    sub = d % 2
                    for tw in range(2):
                        tt = wave * 2 + tw
                        h0, h1, bk0, bk1 = vbank_pairs[tw]
                        wts = [tokd, th1[d]] + (bk0.wwaits() + bk1.wwaits() if d == 0 else [])
                        b.op("t", lambda t, islot=islot, sub=sub, d=d, tt=tt, h0=h0: t.matmul(
                            h0, hq[:, d * W + tt * 128: d * W + tt * 128 + 128],
                            woring[:, islot * 2048 + sub * 1024: islot * 2048 + sub * 1024 + 512],
                            start=(d == 0), stop=(d == 7)), wts, inc="t")
                        tok = b.op("t", lambda t, islot=islot, sub=sub, d=d, tt=tt, h1=h1: t.matmul(
                            h1, hq[:, d * W + tt * 128: d * W + tt * 128 + 128],
                            woring[:, islot * 2048 + sub * 1024 + 512: islot * 2048 + sub * 1024 + 1024],
                            start=(d == 0), stop=(d == 7)), [], inc="t")
                        if d == 7:
                            bk0.wrote(tok)
                            bk1.wrote(tok)
                    if qc == 0 and wave == 0 and pend_mod:
                        emit_mod_jj(pend_mod.pop(0))
                if wave == 1:
                    for isl, _ in wv_slots:
                        wo_slot_free[isl] = tok
                for tw in range(2):
                    tt = wave * 2 + tw
                    kt = 4 * qc + tt
                    h0, h1, bk0, bk1 = vbank_pairs[tw]
                    vv_ = vaug[:, kt * 1040:(kt + 1) * 1040].rearrange("p (h x) -> p h x", x=65)
                    pa = h0.rearrange("p (h x) -> p h x", x=64)
                    pb = h1.rearrange("p (h x) -> p h x", x=64)
                    b.op("v", lambda v, vv=vv_, pa=pa: v.tensor_copy(vv[:, 0:8, 0:64], pa),
                         bk0.rwaits() + bk1.rwaits(), inc="v")
                    b.op("v", lambda v, vv=vv_, pb=pb: v.tensor_copy(vv[:, 8:16, 0:64], pb),
                         [], inc="v")
                    tokm = b.op("v", lambda v, vv=vv_: v.memset(vv[:, :, 64:65], 1.0), [], inc="v")
                    bk0.read(tokm)
                    bk1.read(tokm)
                    vaug_tok[kt] = tokm

            if qc == 1:
                emit_precast()
            if qc == 3:
                for j in range(3):
                    w1_pre[j] = ring_load(di["w1_in"][:, j * 1024:(j + 1) * 1024])

            Sbufs = [(psA, bA0, bA1), (psB, bB0, bB1)]
            exp_tok = [None, None]
            pv_tok = [None, None]
            nkt = 4 * qc + 4
            tokP = None
            tokg = None
            pend_gather = None

            def emit_gather(hp, toA, toB):
                wts = [toA, toB, tok_tab]
                if hp == 0:
                    wts = wts + bC.wwaits()
                b.op("t", lambda t, hp=hp: t.matmul(
                    pC[0:16, :], selgb[:, (2 * hp) * 16:(2 * hp) * 16 + 16],
                    big[:, 41216 + (2 * hp) * 512: 41216 + (2 * hp) * 512 + 512],
                    start=(hp == 0), stop=False), wts, inc="t")
                return b.op("t", lambda t, hp=hp: t.matmul(
                    pC[0:16, :], selgb[:, (2 * hp + 1) * 16:(2 * hp + 1) * 16 + 16],
                    big[:, 41216 + (2 * hp + 1) * 512: 41216 + (2 * hp + 1) * 512 + 512],
                    start=False, stop=(hp == 7)), [], inc="t")

            for hp in range(DC):
                if qc == 0 and pend_mod:
                    emit_mod_jj(pend_mod.pop(0))
                pend = None

                def emit_S(kt):
                    lo = max(0, kt * 128 - qc * W)
                    w = W - lo
                    ps, s0, s1 = Sbufs[kt % 2]
                    wts = [tok_qk, exp_tok[kt % 2]] + s0.wwaits() + s1.wwaits()
                    b.op("t", lambda t, ps=ps, hp=hp, kt=kt, lo=lo, w=w: t.matmul(
                        ps[:, lo:lo + w],
                        ksb[:, hp * NT + kt * 128: hp * NT + (kt + 1) * 128],
                        qqA[:, hp * W + lo: hp * W + lo + w],
                        start=True, stop=True), wts, inc="t")
                    tokB = b.op("t", lambda t, ps=ps, hp=hp, kt=kt, lo=lo, w=w: t.matmul(
                        ps[:, 512 + lo:512 + lo + w],
                        ksb[:, hp * NT + kt * 128: hp * NT + (kt + 1) * 128],
                        qqB[:, hp * W + lo: hp * W + lo + w],
                        start=True, stop=True), [], inc="t")
                    s0.wrote(tokB)
                    s1.wrote(tokB)
                    pin = ps[:].rearrange("p (h x) -> p h x", x=512)[:, :, lo:lo + w]
                    pout = ptb[:, (kt % 2) * 1024:(kt % 2 + 1) * 1024].rearrange(
                        "p (h x) -> p h x", x=512)[:, :, lo:lo + w]
                    te = b.op("sc", lambda s, pin=pin, pout=pout: s.activation(
                        pout, pin, AF.Exp, scale=SCALE, bias=cst[:, 1:2]),
                        [tokB, tok_small, tok_qk], inc="sc")
                    s0.read(te)
                    s1.read(te)
                    exp_tok[kt % 2] = te
                    tok_ready = te
                    if kt >= 4 * qc:
                        base_ = (kt % 2) * 1024
                        pmA = ptb[:, base_ + lo: base_ + lo + 128]
                        pmB = ptb[:, base_ + 512 + lo: base_ + 512 + lo + 128]
                        b.op("v", lambda v, pm=pmA: v.tensor_mul(pm, pm, dmaskb),
                             [te, tok_tab], inc="v")
                        tok_ready = b.op("v", lambda v, pm=pmB: v.tensor_mul(pm, pm, dmaskb),
                                         [], inc="v")
                    return tok_ready

                def emit_PV(kt, tok_ready):
                    nonlocal tokP
                    lo = max(0, kt * 128 - qc * W)
                    w = W - lo
                    wtsA = [tok_ready, vaug_tok[kt]]
                    if kt == 0:
                        wtsA = wtsA + bOA.wwaits() + bOB.wwaits()
                    vaA = vaug[:, kt * 1040 + (2 * hp) * 65: kt * 1040 + (2 * hp) * 65 + 65]
                    vaB = vaug[:, kt * 1040 + (2 * hp + 1) * 65: kt * 1040 + (2 * hp + 1) * 65 + 65]
                    b.op("t", lambda t, va=vaA, kt=kt, lo=lo, w=w, last=(kt == nkt - 1): t.matmul(
                        pOA[0:65, lo:lo + w], va,
                        ptb[:, (kt % 2) * 1024 + lo:(kt % 2) * 1024 + lo + w],
                        start=(kt == 0), stop=last), wtsA, inc="t")
                    tokP = b.op("t", lambda t, va=vaB, kt=kt, lo=lo, w=w, last=(kt == nkt - 1): t.matmul(
                        pOB[0:65, lo:lo + w], va,
                        ptb[:, (kt % 2) * 1024 + 512 + lo:(kt % 2) * 1024 + 512 + lo + w],
                        start=(kt == 0), stop=last), [], inc="t")
                    pv_tok[kt % 2] = tokP

                for kt in range(nkt):
                    tr = emit_S(kt)
                    if kt == 0 and pend_gather is not None:
                        emit_gather(*pend_gather)
                        pend_gather = None
                    if pend is not None:
                        emit_PV(*pend)
                    pend = (kt, tr)
                emit_PV(*pend)
                bOA.wrote(tokP)
                bOB.wrote(tokP)
                toA = b.op("sc", lambda s, hp=hp: s.copy(
                    oraw[:, (2 * hp) * 512:(2 * hp) * 512 + 512], pOA[0:65, :]),
                    bOA.rwaits() + [oraw_free], inc="sc")
                bOA.read(toA)
                toB = b.op("v", lambda v, hp=hp: v.tensor_copy(
                    oraw[:, (2 * hp + 1) * 512:(2 * hp + 1) * 512 + 512], pOB[0:65, :]),
                    bOB.rwaits(), inc="v")
                bOB.read(toB)
                pend_gather = (hp, toA, toB)
            if qc == 0:
                assert not pend_mod
                emit_gates()
            tokg = emit_gather(*pend_gather)
            pend_gather = None
            bC.wrote(tokg)
            tok = b.op("sc", lambda s: s.activation(var_sb[0:16, :], pC[0:16, :], AF.Ln),
                       bC.rwaits() + [rsc_free[0], hscr_free[0], hscr_free[1]], inc="sc")
            bC.read(tok)
            tok = b.op("sc", lambda s: s.activation(linv_f, var_sb[0:16, :], AF.Exp, scale=-1.0),
                       [], inc="sc")
            tok_linv = b.op("v", lambda v: v.tensor_copy(linv_b[0:16, :], linv_f), [tok], inc="v")
            if qc < NQC - 1:
                tok_rinv_next = emit_ln_stats(qc + 1, lambda c, q2=qc + 1: c * NT + q2 * W,
                                              pm=psA[:, 0:512], ps2=psA[:, 512:1024],
                                              bm=bA0, bs2=bA1)
            hscr_free[0] = tok_linv
            hscr_free[1] = tok_linv
            norm_last = None
            for h in range(H):
                lb = [bD, bC][h % 2]
                tokL = b.op("t", lambda t, h=h, lb=lb: t.matmul(
                    lb.ap, ehmb[:, h * 128:(h + 1) * 128], linv_b,
                    start=True, stop=True), [tok_linv] + lb.wwaits(), inc="t")
                lb.wrote(tokL)
                tokN = b.op("v", lambda v, h=h, lb=lb: v.tensor_mul(
                    oraw[0:64, h * 512:h * 512 + 512], lb.ap[0:64, :],
                    oraw[0:64, h * 512:h * 512 + 512]), lb.rwaits(), inc="v")
                lb.read(tokN)
                norm_last = tokN
            rsc_free[0] = tokL
            rsc_free[1] = tokL

            rscr_free[0] = pv_tok[0] if pv_tok[0] is not None else rscr_free[0]
            rscr_free[1] = pv_tok[1] if pv_tok[1] is not None else rscr_free[1]
            tok_wo = None
            for e in range(DC):
                wsrc = di["wo_in"] if qc == 0 else di["woB"]
                i, tokd = wo_slot_load(wsrc[:, e * 2048:(e + 1) * 2048], via_sync=(qc > 0),
                                       extra=([pc_full[0]] if qc > 0 else []))
                ab = [bOA, bOB][e % 2]
                wts = [tokd, norm_last] + ab.wwaits()
                tok = None
                for h in range(H):
                    tok = b.op("t", lambda t, e=e, h=h, ab=ab, i=i: t.matmul(
                        ab.ap,
                        woring[:, i * 2048 + h * 128:i * 2048 + (h + 1) * 128],
                        big[:, 41216 + h * 512: 41216 + h * 512 + 512],
                        start=(h == 0), stop=(h == 15)),
                        wts if h == 0 else [], inc="t")
                ab.wrote(tok)
                wo_slot_free[i] = tok
                tok_wo = tok
                xs = xsb[:, e * NT + qc * W: e * NT + qc * W + W]
                rscr = rscrF[:, (e % 2) * W:(e % 2) * W + W]
                tok = b.op("sc", lambda s, e=e, ab=ab, rs=rscr: s.activation(
                    rs, ab.ap, AF.Identity,
                    scale=gcl_sb[:, e:e + 1], bias=gb_f[:, e:e + 1]),
                    ab.rwaits() + [tok_gates, rscr_free[e % 2]], inc="sc")
                ab.read(tok)
                resid_last = b.op("v", lambda v, xs=xs, rs=rscr: v.tensor_add(
                    xs, xs, rs), [tok], inc="v")
                rscr_free[e % 2] = resid_last
            oraw_free = tok_wo
            qkv_gate = tok_wo
            if dbg and qc == 0:
                sdma(di["d_h1"], hq, waits=[tok_h])
                sdma(di["d_qq"], qqA, waits=[tok_qk])
                kview = ksb.rearrange("p (m n) -> p m n", n=NT)[:, :, 0:W]
                sdma(di["d_k"], kview, waits=[tok_qk])
                sdma(di["d_vg"], vaug[:, 0:4 * 1040], waits=[vaug_tok[3]])
                sdma(di["d_orw"], oraw, waits=[norm_last])
            if dbg and qc == NQC - 1:
                sdma(di["d_xa"], xsb[:], waits=[resid_last])

        tbar = b.mark("t")
        vbar = b.mark("v")

        ln2_th = {}
        for qc in range(2):
            ln2_th[qc] = emit_ln(qc, lambda c, qc=qc: c * NT + qc * W, h2,
                                 lambda c, qc=qc: c * NT + qc * W, 1,
                                 h_extra=[tbar] if qc == 0 else ())
        tok_h2 = ln2_th[1][-1]

        if dbg:
            sdma(di["d_h2"], h2, waits=[tok_h2])
        w2_free = [None, None]
        g_free = [None]
        tok_h2b = None
        for th in range(2):
            tok_gall = None
            for j in range(FJ):
                if j in w1_pre:
                    i, tokd = w1_pre.pop(j)
                else:
                    i, tokd = ring_load(di["w1_in"][:, j * 1024:(j + 1) * 1024])
                for qch in range(2):
                    sl = slots4[(j * 2 + qch) % 4]
                    base = th * 1024 + qch * W
                    ths = ln2_th[th * 2 + qch]
                    wts = [tokd] + sl.wwaits() + ([tbar] if (th == 0 and j < 2) else [])
                    tok = None
                    for d in range(DC):
                        tok = b.op("t", lambda t, i=i, d=d, sl=sl, base=base: t.matmul(
                            sl.ap, wring[:, i * 1024 + d * 128:i * 1024 + (d + 1) * 128],
                            h2[:, d * NT + base: d * NT + base + W],
                            start=(d == 0), stop=(d == 7)),
                            (wts if d == 0 else []) + [ths[d]], inc="t")
                    sl.wrote(tok)
                    if qch == 1:
                        ring_free[i] = tok
                    tokg2 = b.op("sc", lambda s, sl=sl, j=j, qch=qch, th=th: s.activation(
                        g_sb[:, j * 1024 + qch * W: j * 1024 + qch * W + W], sl.ap,
                        AF.Gelu, bias=b1_sb[:, j:j + 1]),
                        sl.rwaits() + [g_free[0], tok_small], inc="sc")
                    sl.read(tokg2)
                    tok_gall = tokg2
                if th == 0:
                    if j == 16:
                        ln2_tr2 = emit_ln_stats(2, lambda c: c * NT + 2 * W)
                    elif j == 24:
                        ln2_th[2] = emit_ln_h(2, lambda c: c * NT + 2 * W, h2,
                                              lambda c: c * NT + 2 * W, 1, ln2_tr2)
                    elif j == 28:
                        ln2_tr3 = emit_ln_stats(3, lambda c: c * NT + 3 * W)
            if th == 0:
                ln2_th[3] = emit_ln_h(3, lambda c: c * NT + 3 * W, h2,
                                      lambda c: c * NT + 3 * W, 1, ln2_tr3)
            if dbg and th == 0:
                sdma(di["d_g"], g_sb, waits=[tok_gall])
            if th == 0:
                for j in range(3):
                    w1_pre[j] = ring_load(di["w1_in"][:, j * 1024:(j + 1) * 1024])
            for e in range(DC):
                tokd = gdma(w2ring[:, (e % 2) * 4096:(e % 2 + 1) * 4096],
                            di["w2_in"][:, e * 4096:(e + 1) * 4096],
                            waits=[w2_free[e % 2]] + ([tbar, vbar] if (th == 0 and e < 2) else []),
                            sem=f"w2{e % 2}")
                for qch in range(2):
                    yb = [bOA, bOB][(e * 2 + qch) % 2]
                    wts = [tokd, tok_gall] + yb.wwaits()
                    tok = None
                    for j in range(FJ):
                        tok = b.op("t", lambda t, e=e, j=j, qch=qch, yb=yb: t.matmul(
                            yb.ap,
                            w2ring[:, (e % 2) * 4096 + j * 128:(e % 2) * 4096 + (j + 1) * 128],
                            g_sb[:, j * 1024 + qch * W: j * 1024 + qch * W + W],
                            start=(j == 0), stop=(j == 31)),
                            wts if j == 0 else [], inc="t")
                    yb.wrote(tok)
                    g_free[0] = tok
                    w2_free[e % 2] = tok
                    xs = xsb[:, e * NT + th * 1024 + qch * W: e * NT + th * 1024 + qch * W + W]
                    rscr = rscrF[:, ((e * 2 + qch) % 2) * W:((e * 2 + qch) % 2) * W + W]
                    tok = b.op("sc", lambda s, e=e, yb=yb, rs=rscr: s.activation(
                        rs, yb.ap, AF.Identity,
                        scale=gcl_sb[:, 8 + e:8 + e + 1], bias=gb_f[:, 8 + e:8 + e + 1]),
                        yb.rwaits() + [rscr_free[(e * 2 + qch) % 2]], inc="sc")
                    yb.read(tok)
                    tok = b.op("v", lambda v, xs=xs, rs=rscr: v.tensor_add(
                        xs, xs, rs), [tok], inc="v")
                    rscr_free[(e * 2 + qch) % 2] = tok
                    yo = di["y_out"][:, e * NT + th * 1024 + qch * W:
                                     e * NT + th * 1024 + qch * W + W]
                    sdma(yo, xs, waits=[tok])

        sems = {}
        for nm in sem_names + ["t", "v", "sc"]:
            sems[nm] = E(nc.semaphore(nm))
        block = E(nc.Block())
        n_out = b.cnt["xdma"]

        @block.tensor
        def _(t):
            b.emit(t, "t", sems)

        @block.vector
        def _(v):
            b.emit(v, "v", sems)

        @block.scalar
        def _(s):
            b.emit(s, "sc", sems)

        @block.gpsimd
        def _(g):
            b.emit(g, "g", sems)

        @block.sync
        def _(s):
            b.emit(s, "s", sems)
            s.wait_ge(sems["xdma"], n_out)

    return nc



def _tile_lhs(A):
    K, M = A.shape
    return np.ascontiguousarray(
        A.reshape(K // 128, 128, M // 128, 128).transpose(1, 2, 0, 3).reshape(128, -1))


def _make_consts():
    n = np.arange(NT, dtype=np.float64)
    p = np.arange(128)
    i = (p % 64) // 2
    inv = 1.0 / (10000.0 ** (2.0 * i / 64.0))
    ang = n[None, :] * inv[:, None]
    cos = np.cos(ang).astype(np.float32)
    sin = np.sin(ang).astype(np.float32)
    perm = np.zeros((128, 128), np.float32)
    for ii in range(64):
        perm[2 * ii + 1, 2 * ii] = -1.0
        perm[2 * ii, 2 * ii + 1] = 1.0
    dmask = np.triu(np.ones((128, 128), np.float32))
    onesln = np.full((128, 128), 1.0 / 1024.0, np.float32)
    selg = np.zeros((128, 256), np.float32)
    for h in range(16):
        selg[64, h * 16 + h] = 1.0
    ehm = np.zeros((128, 2048), np.float32)
    for h in range(16):
        ehm[h, h * 128:(h + 1) * 128] = 1.0
    cstm = np.zeros((128, 8), np.float32)
    cstm[:, 0] = EPS
    cstm[:, 1] = -MSHIFT
    cstm[:, 2] = 1.0
    cstm[:, 3] = 0.25
    cstm[:, 4] = 1e-30
    ctile = np.zeros((128, 64), np.float32)
    ctile[:, 0:16] = 0.25
    ctile[:, 32:48] = 1.0
    ctile[:, 48:64] = 1e38
    return cos, sin, perm, dmask, onesln, selg, ehm, cstm, ctile


def kernel(L0_attn_x, L0_attn_Wq, L0_attn_Wk, L0_attn_Wv, L0_attn_Wo, L0_attn_bo,
           L0_ffn_w1, L0_ffn_b1, L0_ffn_w2, L0_ffn_b2, L0_c, L0_wmod):
    x = np.asarray(L0_attn_x, np.float32)
    Wq = np.asarray(L0_attn_Wq, np.float32)
    Wk = np.asarray(L0_attn_Wk, np.float32)
    Wv = np.asarray(L0_attn_Wv, np.float32)
    Wo = np.asarray(L0_attn_Wo, np.float32)
    bo = np.asarray(L0_attn_bo, np.float32)
    w1 = np.asarray(L0_ffn_w1, np.float32)
    b1 = np.asarray(L0_ffn_b1, np.float32)
    w2 = np.asarray(L0_ffn_w2, np.float32)
    b2 = np.asarray(L0_ffn_b2, np.float32)
    c = np.asarray(L0_c, np.float32)
    wmod = np.asarray(L0_wmod, np.float32)

    wq_h = _tile_lhs(np.ascontiguousarray(Wq.T))
    wk_h = _tile_lhs(np.ascontiguousarray(Wk.T))
    wv_h = np.ascontiguousarray(Wv.T.reshape(8, 128, 1024).transpose(1, 0, 2).reshape(128, 8192))
    wo_h = np.zeros((128, 16384), np.float32)
    wo_h[0:64] = Wo.T.reshape(16, 64, 8, 128).transpose(1, 2, 0, 3).reshape(64, 16384)
    w1_h = _tile_lhs(w1)
    w2_h = _tile_lhs(w2)
    wmod_h = _tile_lhs(wmod)
    cos, sin, perm, dmask, onesln, selg, ehm, cstm, ctile_h = _make_consts()
    bo_h = np.ascontiguousarray(bo.reshape(8, 128).T)
    b1_h = np.ascontiguousarray(b1.reshape(32, 128).T)
    b2_h = np.ascontiguousarray(b2.reshape(8, 128).T)

    nc = build_program()
    shared = dict(wq_in=wq_h, wk_in=wk_h, wv_in=wv_h, wo_in=wo_h, w1_in=w1_h,
                  w2_in=w2_h, wmod_in=wmod_h, cos_in=cos, sin_in=sin,
                  perm_in=perm, dmask_in=dmask, onesln_in=onesln, selg_in=selg,
                  ehm_in=ehm, cst_in=cstm, ctile_in=ctile_h,
                  bo_in=bo_h, b1_in=b1_h, b2_in=b2_h)
    in_maps = []
    for bi in range(8):
        xb = np.ascontiguousarray(
            x[bi].T.reshape(8, 128, NT).transpose(1, 0, 2).reshape(128, 8 * NT))
        cb = np.ascontiguousarray(c[bi].reshape(8, 128).T)
        m = dict(shared)
        m["x_in"] = xb
        m["c_in"] = cb
        in_maps.append(m)

    res = run_bass_kernel_spmd(nc, in_maps, core_ids=list(range(8)))
    outs = []
    for bi in range(8):
        y = np.asarray(res.results[bi]["y_out"])
        outs.append(y.reshape(128, 8, NT).transpose(1, 0, 2).reshape(1024, NT).T)
    return np.stack(outs).astype(np.float32)


# version 43
# speedup vs baseline: 1.1670x; 1.0054x over previous
import numpy as np
from contextlib import ExitStack

import concourse.bass as bass
import concourse.mybir as mybir
from concourse.bass_utils import run_bass_kernel_spmd

F32 = mybir.dt.float32
BF16 = mybir.dt.bfloat16
AF = mybir.ActivationFunctionType
ALU = mybir.AluOpType

D = 1024
NT = 2048
DC = 8
NQC = 4
W = 512
NKT = 16
H = 16
FJ = 32
MSHIFT = 16.0
EPS = 1e-5
SCALE = 1.0 / 64.0

ENGS = ("t", "v", "sc", "g", "s")


class Builder:
    def __init__(self):
        self.ops = {e: [] for e in ENGS}
        self.cnt = {}
        self.owner = {}
        self.waited = {e: {} for e in ENGS}

    def sem(self, name, owner=None):
        self.cnt[name] = 0
        self.owner[name] = owner

    def op(self, eng, fn, waits=(), inc=None, amt=1):
        w2 = []
        for tok in waits:
            if tok is None:
                continue
            s, v = tok
            if v <= 0 or self.owner.get(s) == eng:
                continue
            if self.waited[eng].get(s, 0) >= v:
                continue
            self.waited[eng][s] = v
            w2.append((s, v))
        rtok = None
        if inc is not None:
            self.cnt[inc] += amt
            rtok = (inc, self.cnt[inc])
        self.ops[eng].append((fn, w2, inc, amt))
        return rtok

    def mark(self, eng_sem):
        return (eng_sem, self.cnt[eng_sem])

    def emit(self, engobj, eng, sems):
        for fn, waits, inc, amt in self.ops[eng]:
            for s, v in waits:
                engobj.wait_ge(sems[s], v)
            inst = fn(engobj)
            if inc is not None:
                inst.then_inc(sems[inc], amt)


class Bank:

    def __init__(self, ap):
        self.ap = ap
        self.wtok = None
        self.rtoks = []

    def wwaits(self):
        return list(self.rtoks) + [self.wtok]

    def wrote(self, tok):
        self.wtok = tok
        self.rtoks = []

    def rwaits(self):
        return [self.wtok]

    def read(self, tok):
        self.rtoks.append(tok)


def build_program(dbg=False):
    nc = bass.Bass("TRN2", target_bir_lowering=False)

    di = {}
    for name, shape in [
        ("x_in", [128, DC * NT]), ("c_in", [128, 8]), ("bo_in", [128, 8]),
        ("b1_in", [128, 32]), ("b2_in", [128, 8]), ("cst_in", [128, 8]),
        ("wq_in", [128, 8192]), ("wk_in", [128, 8192]), ("wv_in", [128, 8192]),
        ("wo_in", [128, 16384]), ("w1_in", [128, 32768]), ("w2_in", [128, 32768]),
        ("wmod_in", [128, 49152]), ("cos_in", [128, NT]), ("sin_in", [128, NT]),
        ("perm_in", [128, 128]), ("dmask_in", [128, 128]), ("onesln_in", [128, 128]),
        ("selg_in", [128, 256]), ("ehm_in", [128, 2048]), ("ctile_in", [128, 64]),
    ]:
        di[name] = nc.dram_tensor(name, shape, F32, kind="ExternalInput").ap()
    di["y_out"] = nc.dram_tensor("y_out", [128, DC * NT], F32, kind="ExternalOutput").ap()
    for name, shape in [("wqB", [128, 8192]), ("wkB", [128, 8192]), ("wvB", [128, 8192]),
                        ("woB", [128, 16384]), ("w1B", [128, 32768]), ("w2B", [128, 32768])]:
        di[name] = nc.dram_tensor(name, shape, BF16, kind="Internal").ap()
    if dbg:
        for nm, shape, dt_ in [("d_mod", [128, 48], F32),
                               ("d_gbf", [128, 16], F32), ("d_h1", [128, DC * W], BF16),
                               ("d_qq", [128, DC * W], BF16), ("d_k", [128, DC * W], BF16),
                               ("d_vg", [128, 4 * 1040], BF16), ("d_orw", [65, 16 * 512], BF16),
                               ("d_xa", [128, DC * NT], F32), ("d_h2", [128, DC * NT], BF16),
                               ("d_g", [128, FJ * 1024], BF16),
                               ("d_w2", [128, 4096], BF16), ("d_y", [128, 512], F32),
                               ("d_gcl", [128, 16], F32)]:
            di[nm] = nc.dram_tensor(nm, shape, dt_, kind="ExternalOutput").ap()

    b = Builder()
    sem_names = (["xdma", "tdma", "cdma", "pdma"]
                 + [f"r{i}" for i in range(3)] + [f"q{i}" for i in range(4)]
                 + ["h0", "h1", "w20", "w21"])
    for s in sem_names:
        b.sem(s, None)
    for s, o in [("t", "t"), ("v", "v"), ("sc", "sc")]:
        b.sem(s, o)

    with ExitStack() as ES:
        E = ES.enter_context
        xsb = E(nc.sbuf_tensor("xsb", [128, DC * NT], F32))
        wring = E(nc.sbuf_tensor("wring", [128, 3 * 1024], BF16))
        c_sb = E(nc.sbuf_tensor("c_sb", [128, 8], F32))
        bo_sb = E(nc.sbuf_tensor("bo_sb", [128, 8], F32))
        b1_sb = E(nc.sbuf_tensor("b1_sb", [128, 32], F32))
        b2_sb = E(nc.sbuf_tensor("b2_sb", [128, 8], F32))
        cst = E(nc.sbuf_tensor("cst", [128, 8], F32))
        ctile = E(nc.sbuf_tensor("ctile", [128, 64], F32))
        siluc = E(nc.sbuf_tensor("siluc", [128, 8], BF16))
        siluf = E(nc.sbuf_tensor("siluf", [128, 8], F32))
        mod_sb = E(nc.sbuf_tensor("mod_sb", [128, 48], F32))
        sgn_sb = E(nc.sbuf_tensor("sgn_sb", [128, 16], F32))
        gcl_sb = E(nc.sbuf_tensor("gcl_sb", [128, 16], F32))
        ca1_sb = E(nc.sbuf_tensor("ca1_sb", [128, 16], F32))
        gb_f = E(nc.sbuf_tensor("gb_f", [128, 16], F32))
        onesln = E(nc.sbuf_tensor("onesln", [128, 128], BF16))
        xbf = E(nc.sbuf_tensor("xbf", [128, 2 * W], BF16))
        xsq = E(nc.sbuf_tensor("xsq", [128, W], BF16))
        m_sb = E(nc.sbuf_tensor("m_sb", [128, W], F32))
        var_sb = E(nc.sbuf_tensor("var_sb", [128, W], F32))
        hscr = E(nc.sbuf_tensor("hscr", [128, 2 * W], F32))
        rinv_sb = E(nc.sbuf_tensor("rinv_sb", [128, W], F32))

        big = E(nc.sbuf_tensor("big", [128, 54016], BF16))
        ksb = big[:, 0:16384]
        vaug = big[:, 16384:33024]
        qqA = big[:, 33024:37120]
        qqB = big[:, 37120:41216]
        hq = big[:, 41216:45312]
        oraw = big[0:65, 41216:49408]
        ptb = big[:, 49408:51456]
        qtmp = big[:, 49408:50432]
        rsc1 = big[:, 50432:50944]
        rsc2 = big[:, 50944:51456]
        permb = big[:, 51456:51584]
        dmaskb = big[:, 51584:51712]
        selgb = big[:, 51712:51968]
        ehmb = big[:, 51968:54016]
        h2 = big[:, 0:16384]
        rscrF = big[:, 49408:51456].bitcast(F32)
        g_sb = big[:, 16384:49152]
        linv_b = big[:, 50432:50944]

        u3 = E(nc.sbuf_tensor("u3", [128, 8192], BF16))
        woring = u3[:, 0:8192]
        w2ring = u3[:, 0:8192]
        csb = E(nc.sbuf_tensor("csb", [128, 1024], BF16))
        linv_f = hscr[0:16, 0:W]

        pS = E(nc.psum_tensor("pS", [128, 2048], F32))
        psA = pS[:, 0:1024]
        psB = pS[:, 1024:2048]
        pOA = E(nc.psum_tensor("pOA", [128, 512], F32))
        pOB = E(nc.psum_tensor("pOB", [128, 512], F32))
        pC = E(nc.psum_tensor("pC", [128, 512], F32))
        pD = E(nc.psum_tensor("pD", [128, 512], F32))

        bA0, bA1 = Bank(pS[:, 0:512]), Bank(pS[:, 512:1024])
        bB0, bB1 = Bank(pS[:, 1024:1536]), Bank(pS[:, 1536:2048])
        bOA, bOB, bC, bD = Bank(pOA[:]), Bank(pOB[:]), Bank(pC[:]), Bank(pD[:])
        slots4 = [bA0, bA1, bB0, bB1]
        vbank_pairs = [(pS[:, 0:512], pS[:, 512:1024], bA0, bA1),
                       (pS[:, 1024:1536], pS[:, 1536:2048], bB0, bB1),
                       (pOA[:], pOB[:], bOA, bOB),
                       (pC[:], pD[:], bC, bD)]

        def gdma(dst_ap, src_ap, waits=(), sem="tdma"):
            return b.op("g", lambda g, d=dst_ap, s=src_ap: g.dma_start(d, s),
                        waits, inc=sem, amt=16)

        def sdma(dst_ap, src_ap, waits=(), sem="xdma"):
            return b.op("s", lambda s_, d=dst_ap, s=src_ap: s_.dma_start(d, s),
                        waits, inc=sem, amt=16)

        sdma(xsb[:], di["x_in"])
        tok_small = None
        for nm, t_ in [("c_in", c_sb), ("bo_in", bo_sb), ("b1_in", b1_sb),
                       ("b2_in", b2_sb), ("cst_in", cst), ("ctile_in", ctile)]:
            tok_small = sdma(t_[:], di[nm])
        tok_x = tok_small

        tok_tab = None
        for nm, t_ in [("perm_in", permb), ("dmask_in", dmaskb),
                       ("onesln_in", onesln[:]), ("selg_in", selgb), ("ehm_in", ehmb)]:
            tok_tab = gdma(t_, di[nm], sem="tdma")

        b.op("sc", lambda s: s.activation(siluc[:], c_sb[:], AF.Silu),
             [tok_small], inc="sc")
        b.op("v", lambda v: v.memset(big[64:128, 45312:49408], 0.0), [], inc="v")
        tok_qz = b.op("v", lambda v: v.memset(qqA[64:128, :], 0.0), [], inc="v")
        tok_qz = b.op("v", lambda v: v.memset(qqB[0:64, :], 0.0), [], inc="v")
        tok_silu = b.op("sc", lambda s: s.activation(siluf[:], c_sb[:], AF.Silu),
                        [], inc="sc")

        ring_free = [None] * 3
        ridx = [0]
        wo_slot_free = [None] * 4
        wo_slot_tok = [None] * 4
        woidx = [0]

        def wo_slot_load(src_slice, via_sync, extra=()):
            i = woidx[0] % 4
            dma = sdma if via_sync else gdma
            tok = dma(woring[:, i * 2048:(i + 1) * 2048], src_slice,
                      waits=[wo_slot_free[i]] + list(extra), sem=f"q{i}")
            wo_slot_tok[i] = tok
            woidx[0] += 1
            return i, tok

        def ring_load(src_slice, extra=(), via_sync=False):
            i = ridx[0] % 3
            dma = sdma if via_sync else gdma
            tok = dma(wring[:, i * 1024:(i + 1) * 1024], src_slice,
                      waits=[ring_free[i]] + list(extra), sem=f"r{i}")
            ridx[0] += 1
            return i, tok

        def emit_ln_stats(qc, src_off, pm=None, ps2=None, bm=None, bs2=None):
            pm = pC[:] if pm is None else pm
            ps2 = pD[:] if ps2 is None else ps2
            bm = bC if bm is None else bm
            bs2 = bD if bs2 is None else bs2
            tok_stat = None
            for c in range(DC):
                xs = xsb[:, src_off(c): src_off(c) + W]
                pp = c % 2
                tv1 = b.op("v", lambda v, xs=xs, pp=pp: v.tensor_copy(
                    xbf[:, pp * W:(pp + 1) * W], xs),
                    [tok_x, xbf_free[pp]], inc="v")
                tv2 = b.op("v", lambda v, pp=pp: v.tensor_mul(
                    xsq[:], xbf[:, pp * W:(pp + 1) * W],
                    xbf[:, pp * W:(pp + 1) * W]), [xsq_free[0]], inc="v")
                wts = [tv1, tv2, tok_tab]
                if c == 0:
                    wts = wts + bm.wwaits() + bs2.wwaits()
                b.op("t", lambda t, pp=pp, c=c, pm=pm: t.matmul(
                    pm, onesln[:], xbf[:, pp * W:(pp + 1) * W],
                    start=(c == 0), stop=(c == 7)), wts, inc="t")
                t2 = b.op("t", lambda t, c=c, ps2=ps2: t.matmul(
                    ps2, onesln[:], xsq[:],
                    start=(c == 0), stop=(c == 7)), [], inc="t")
                xbf_free[pp] = t2
                xsq_free[0] = t2
                tok_stat = t2
            bm.wrote(tok_stat)
            bs2.wrote(tok_stat)
            tok = b.op("sc", lambda s, pm=pm: s.copy(m_sb[:], pm), bm.rwaits(), inc="sc")
            bm.read(tok)
            tok_sq = b.op("sc", lambda s: s.activation(var_sb[:], m_sb[:], AF.Square),
                          [], inc="sc")
            tok = b.op("v", lambda v, ps2=ps2: v.tensor_sub(var_sb[:], ps2, var_sb[:]),
                       bs2.rwaits() + [tok_sq], inc="v")
            bs2.read(tok)
            tok = b.op("sc", lambda s: s.activation(var_sb[:], var_sb[:], AF.Ln, bias=cst[:, 0:1]),
                       [tok, tok_small], inc="sc")
            tok_rinv = b.op("sc", lambda s: s.activation(rinv_sb[:], var_sb[:], AF.Exp, scale=-0.5),
                            [], inc="sc")
            return tok_rinv

        def emit_ln_h(qc, src_off, dst, dst_off, which, tok_rinv, h_extra=()):
            o = 8 * which
            bias_base = 24 * which
            ths = []
            for c in range(DC):
                xs = xsb[:, src_off(c): src_off(c) + W]
                pp = c % 2
                hs = hscr[:, pp * W:(pp + 1) * W]
                b.op("v", lambda v, xs=xs, hs=hs: v.tensor_sub(hs, xs, m_sb[:]),
                     [tok_rinv, hscr_free[pp]], inc="v")
                tv = b.op("v", lambda v, hs=hs: v.tensor_mul(hs, hs, rinv_sb[:]), [], inc="v")
                th = b.op("sc", lambda s, hs=hs, c=c: s.activation(
                    dst[:, dst_off(c): dst_off(c) + W], hs, AF.Identity,
                    scale=ca1_sb[:, o + c:o + c + 1],
                    bias=mod_sb[:, bias_base + c:bias_base + c + 1]),
                    [tv, tok_early if which == 0 else tok_gates] + list(h_extra), inc="sc")
                hscr_free[pp] = th
                ths.append(th)
            return ths

        def emit_ln(qc, src_off, dst, dst_off, which, h_extra=()):
            tok_rinv = emit_ln_stats(qc, src_off)
            return emit_ln_h(qc, src_off, dst, dst_off, which, tok_rinv, h_extra)

        xbf_free = [None, None]
        xsq_free = [None]
        hscr_free = [None, None]
        tok_rinv_qc0 = emit_ln_stats(0, lambda c: c * NT + 0 * W)

        tok = None
        fslot_free = [None, None]
        for jj in range(16):
            i, tokd = ring_load(di["wmod_in"][:, jj * 1024:(jj + 1) * 1024])
            for d in range(DC):
                tok = b.op("t", lambda t, i=i, d=d, jj=jj: t.matmul(
                    pOB[:, jj:jj + 1],
                    wring[:, i * 1024 + d * 128:i * 1024 + (d + 1) * 128],
                    siluc[:, d:d + 1], start=(d == 0), stop=(d == 7)),
                    ([tokd, tok_silu] + (bOB.wwaits() if jj == 0 else [])) if d == 0 else [tokd],
                    inc="t")
            ring_free[i] = tok
        bOB.wrote(tok)
        tok_evA = b.op("v", lambda v: v.tensor_copy(mod_sb[:, 0:16], pOB[:, 0:16]),
                       bOB.rwaits(), inc="v")
        bOB.read(tok_evA)
        tok_early = b.op("sc", lambda s: s.activation(ca1_sb[:, 0:8], mod_sb[:, 8:16],
                                                      AF.Identity, bias=1.0),
                         [tok_evA, tok_small], inc="sc")

        pend_mod = list(range(24, 40)) + list(range(16, 24)) + list(range(40, 48))
        mod_first = [True]
        mod_last = [None]
        tok_gates = None
        tok_modsb = None

        def emit_mod_jj(jj):
            col = jj - 16
            gate_col = (16 <= jj < 24) or (40 <= jj < 48)
            first = bD.wwaits() if mod_first[0] else []
            mod_first[0] = False
            if not gate_col:
                i, tokd = ring_load(di["wmod_in"][:, jj * 1024:(jj + 1) * 1024])
                tok = None
                for d in range(DC):
                    tok = b.op("t", lambda t, i=i, d=d, col=col: t.matmul(
                        pD[:, col:col + 1],
                        wring[:, i * 1024 + d * 128:i * 1024 + (d + 1) * 128],
                        siluc[:, d:d + 1], start=(d == 0), stop=(d == 7)),
                        ([tokd, tok_silu] + first) if d == 0 else [tokd],
                        inc="t")
                ring_free[i] = tok
            else:
                tok = None
                for half in range(2):
                    fs = half
                    tokd = sdma(hscr[:, fs * W:(fs + 1) * W],
                                di["wmod_in"][:, jj * 1024 + half * 512: jj * 1024 + (half + 1) * 512],
                                waits=[fslot_free[fs], hscr_free[fs]], sem=f"h{fs}")
                    for dd in range(4):
                        d = half * 4 + dd
                        tok = b.op("t", lambda t, fs=fs, dd=dd, d=d, col=col: t.matmul(
                            pD[:, col:col + 1],
                            hscr[:, fs * W + dd * 128: fs * W + (dd + 1) * 128],
                            siluf[:, d:d + 1], start=(d == 0), stop=(d == 7)),
                            [tokd, tok_silu] + (first if d == 0 else []),
                            inc="t")
                    fslot_free[fs] = tok
            mod_last[0] = tok

        def emit_gates():
            nonlocal tok_gates, tok_modsb
            bD.wrote(mod_last[0])
            tok_modsb = b.op("v", lambda v: v.tensor_copy(mod_sb[:, 16:48], pD[:, 0:32]),
                             bD.rwaits(), inc="v")
            bD.read(tok_modsb)
            b.op("sc", lambda s: s.activation(gcl_sb[:, 0:8], mod_sb[:, 16:24], AF.Abs),
                 [tok_modsb, tok_small], inc="sc")
            tok_ab = b.op("sc", lambda s: s.activation(gcl_sb[:, 8:16], mod_sb[:, 40:48], AF.Abs),
                          [], inc="sc")
            b.op("v", lambda v: v.tensor_max(gcl_sb[:], gcl_sb[:], ctile[:, 0:16]), [tok_ab], inc="v")
            b.op("sc", lambda s: s.activation(gb_f[:, 0:8], mod_sb[:, 16:24], AF.Tanh, scale=1e30),
                 [tok_modsb], inc="sc")
            tok_sg = b.op("sc", lambda s: s.activation(gb_f[:, 8:16], mod_sb[:, 40:48], AF.Tanh, scale=1e30),
                          [], inc="sc")
            tok_dbg2 = None
            if dbg:
                tok_dbg2 = sdma(di["d_gbf"], gb_f[:], waits=[tok_sg])
            b.op("v", lambda v: v.tensor_mul(gcl_sb[:], gcl_sb[:], gb_f[:]),
                 [tok_sg, tok_dbg2], inc="v")
            b.op("sc", lambda s: s.activation(ca1_sb[:, 8:16], mod_sb[:, 32:40],
                                              AF.Identity, bias=1.0),
                 [tok_modsb], inc="sc")
            b.op("v", lambda v: v.tensor_mul(gb_f[:, 0:8], gcl_sb[:, 0:8], bo_sb[:]),
                 [tok_small], inc="v")
            tok_gates = b.op("v", lambda v: v.tensor_mul(gb_f[:, 8:16], gcl_sb[:, 8:16], b2_sb[:]),
                             [], inc="v")
            hscr_free[0] = fslot_free[0]
            hscr_free[1] = fslot_free[1]
            if dbg:
                sdma(di["d_mod"], mod_sb[:], waits=[tok_gates, tok_modsb])
                sdma(di["d_gcl"], gcl_sb[:], waits=[tok_gates])


        pc_full = [None]
        w1_pre = {}

        def emit_precast():
            plan = [("wo_in", "woB", 16384)]
            for fn, bn, width in plan:
                for c0 in range(0, width, 2048):
                    i, tok = wo_slot_load(di[fn][:, c0:c0 + 2048], via_sync=False)
                    tokw = sdma(di[bn][:, c0:c0 + 2048], woring[:, i * 2048:(i + 1) * 2048],
                                waits=[tok], sem="pdma")
                    wo_slot_free[i] = tokw
            pc_full[0] = tokw

        qtmp_free_t = [None, None]
        qtmp_free_v = [None, None]
        rsc_free = [None, None]
        rscr_free = [None, None]
        tok_rinv_next = tok_rinv_qc0
        cs_tok = [None]
        cs_free = [None]
        vaug_tok = [None] * NKT
        oraw_free = None
        qkv_gate = None
        for qc in range(NQC):
            gdma(csb[:, 0:512], di["cos_in"][:, qc * W:(qc + 1) * W], waits=[cs_free[0]],
                 sem="cdma")
            cs_tok[0] = gdma(csb[:, 512:1024], di["sin_in"][:, qc * W:(qc + 1) * W],
                             waits=[], sem="cdma")
            th1 = emit_ln_h(qc, lambda c, qc=qc: c * NT + qc * W, hq,
                            lambda c: c * W, 0, tok_rinv_next,
                            h_extra=[qkv_gate])
            tok_h = th1[-1]

            tok_qk = None
            pend_rope = None

            def emit_rope(mi, m, proj, tq):
                nonlocal tok_qk
                qh = qtmp[:, (mi % 2) * 512:(mi % 2) * 512 + 512]
                rb = [bOA, bOB][mi % 2]
                tr = b.op("t", lambda t, rb=rb, qh=qh: t.matmul(
                    rb.ap, permb, qh, start=True, stop=True),
                    [tq, tok_tab] + rb.wwaits(), inc="t")
                rb.wrote(tr)
                qtmp_free_t[mi % 2] = tr
                t1 = b.op("v", lambda v, qh=qh: v.tensor_mul(
                    rsc1, qh, csb[:, 0:512]),
                    [rsc_free[0], cs_tok[0], tq], inc="v")
                qtmp_free_v[mi % 2] = t1
                t2 = b.op("v", lambda v, rb=rb: v.tensor_mul(
                    rsc2, rb.ap, csb[:, 512:1024]),
                    rb.rwaits() + [rsc_free[1], t1], inc="v")
                rb.read(t2)
                cs_free[0] = t2
                if proj == 0:
                    b.op("v", lambda v, m=m, qc=qc: v.tensor_add(
                        qqA[0:64, m * W:(m + 1) * W], rsc1[0:64, :], rsc2[0:64, :]),
                        [tok_qz], inc="v")
                    t3 = b.op("v", lambda v, m=m, qc=qc: v.tensor_add(
                        qqB[64:128, m * W:(m + 1) * W], rsc1[64:128, :], rsc2[64:128, :]),
                        [], inc="v")
                else:
                    dst = ksb[:, m * NT + qc * W: m * NT + qc * W + W]
                    t3 = b.op("v", lambda v, dst=dst: v.tensor_add(dst, rsc1, rsc2),
                              [], inc="v")
                rsc_free[0] = t3
                rsc_free[1] = t3
                tok_qk = t3

            for proj in range(2):
                wsrc = di["wq_in" if proj == 0 else "wk_in"]
                for m in range(DC):
                    mi = proj * DC + m
                    if m % 2 == 0:
                        islot, tokd = wo_slot_load(wsrc[:, m * 1024:(m + 2) * 1024],
                                                   via_sync=False)
                    sub = m % 2
                    sl = slots4[mi % 4]
                    wts = [tokd] + sl.wwaits()
                    tok = None
                    for d in range(DC):
                        tok = b.op("t", lambda t, islot=islot, sub=sub, d=d, sl=sl: t.matmul(
                            sl.ap,
                            woring[:, islot * 2048 + sub * 1024 + d * 128:
                                   islot * 2048 + sub * 1024 + (d + 1) * 128],
                            hq[:, d * W:(d + 1) * W], start=(d == 0), stop=(d == 7)),
                            (wts if d == 0 else []) + [th1[d]], inc="t")
                    sl.wrote(tok)
                    if sub == 1:
                        wo_slot_free[islot] = tok
                    tq = b.op("sc", lambda s, sl=sl, mi=mi: s.copy(
                        qtmp[:, (mi % 2) * 512:(mi % 2) * 512 + 512], sl.ap),
                        sl.rwaits() + [qtmp_free_t[mi % 2], qtmp_free_v[mi % 2]], inc="sc")
                    sl.read(tq)
                    if pend_rope is not None:
                        emit_rope(*pend_rope)
                    pend_rope = (mi, m, proj, tq)
                    if qc == 0 and pend_mod:
                        emit_mod_jj(pend_mod.pop(0))
            emit_rope(*pend_rope)
            pend_rope = None

            wv_slots = []
            for dpair in range(4):
                islot, tokd = wo_slot_load(
                    di["wv_in"][:, dpair * 2048:(dpair + 1) * 2048], via_sync=False)
                wv_slots.append((islot, tokd))
            for wave in range(2):
                for d in range(DC):
                    islot, tokd = wv_slots[d // 2]
                    sub = d % 2
                    for tw in range(2):
                        tt = wave * 2 + tw
                        h0, h1, bk0, bk1 = vbank_pairs[tw]
                        wts = [tokd, th1[d]] + (bk0.wwaits() + bk1.wwaits() if d == 0 else [])
                        b.op("t", lambda t, islot=islot, sub=sub, d=d, tt=tt, h0=h0: t.matmul(
                            h0, hq[:, d * W + tt * 128: d * W + tt * 128 + 128],
                            woring[:, islot * 2048 + sub * 1024: islot * 2048 + sub * 1024 + 512],
                            start=(d == 0), stop=(d == 7)), wts, inc="t")
                        tok = b.op("t", lambda t, islot=islot, sub=sub, d=d, tt=tt, h1=h1: t.matmul(
                            h1, hq[:, d * W + tt * 128: d * W + tt * 128 + 128],
                            woring[:, islot * 2048 + sub * 1024 + 512: islot * 2048 + sub * 1024 + 1024],
                            start=(d == 0), stop=(d == 7)), [], inc="t")
                        if d == 7:
                            bk0.wrote(tok)
                            bk1.wrote(tok)
                    if qc == 0 and wave == 0 and pend_mod:
                        emit_mod_jj(pend_mod.pop(0))
                if wave == 1:
                    for isl, _ in wv_slots:
                        wo_slot_free[isl] = tok
                for tw in range(2):
                    tt = wave * 2 + tw
                    kt = 4 * qc + tt
                    h0, h1, bk0, bk1 = vbank_pairs[tw]
                    vv_ = vaug[:, kt * 1040:(kt + 1) * 1040].rearrange("p (h x) -> p h x", x=65)
                    pa = h0.rearrange("p (h x) -> p h x", x=64)
                    pb = h1.rearrange("p (h x) -> p h x", x=64)
                    b.op("v", lambda v, vv=vv_, pa=pa: v.tensor_copy(vv[:, 0:8, 0:64], pa),
                         bk0.rwaits() + bk1.rwaits(), inc="v")
                    b.op("v", lambda v, vv=vv_, pb=pb: v.tensor_copy(vv[:, 8:16, 0:64], pb),
                         [], inc="v")
                    tokm = b.op("v", lambda v, vv=vv_: v.memset(vv[:, :, 64:65], 1.0), [], inc="v")
                    bk0.read(tokm)
                    bk1.read(tokm)
                    vaug_tok[kt] = tokm

            if qc == 1:
                emit_precast()
            if qc == 3:
                for j in range(3):
                    w1_pre[j] = ring_load(di["w1_in"][:, j * 1024:(j + 1) * 1024])

            Sbufs = [(psA, bA0, bA1), (psB, bB0, bB1)]
            exp_tok = [None, None]
            pv_tok = [None, None]
            nkt = 4 * qc + 4
            tokP = None
            tokg = None
            pend_gather = None

            def emit_gather(hp, toA, toB):
                wts = [toA, toB, tok_tab]
                if hp == 0:
                    wts = wts + bC.wwaits()
                b.op("t", lambda t, hp=hp: t.matmul(
                    pC[0:16, :], selgb[:, (2 * hp) * 16:(2 * hp) * 16 + 16],
                    big[:, 41216 + (2 * hp) * 512: 41216 + (2 * hp) * 512 + 512],
                    start=(hp == 0), stop=False), wts, inc="t")
                return b.op("t", lambda t, hp=hp: t.matmul(
                    pC[0:16, :], selgb[:, (2 * hp + 1) * 16:(2 * hp + 1) * 16 + 16],
                    big[:, 41216 + (2 * hp + 1) * 512: 41216 + (2 * hp + 1) * 512 + 512],
                    start=False, stop=(hp == 7)), [], inc="t")

            def emit_S(hp, kt):
                lo = max(0, kt * 128 - qc * W)
                w = W - lo
                ps, s0, s1 = Sbufs[kt % 2]
                wts = [tok_qk, exp_tok[kt % 2]] + s0.wwaits() + s1.wwaits()
                b.op("t", lambda t, ps=ps, hp=hp, kt=kt, lo=lo, w=w: t.matmul(
                    ps[:, lo:lo + w],
                    ksb[:, hp * NT + kt * 128: hp * NT + (kt + 1) * 128],
                    qqA[:, hp * W + lo: hp * W + lo + w],
                    start=True, stop=True), wts, inc="t")
                tokB = b.op("t", lambda t, ps=ps, hp=hp, kt=kt, lo=lo, w=w: t.matmul(
                    ps[:, 512 + lo:512 + lo + w],
                    ksb[:, hp * NT + kt * 128: hp * NT + (kt + 1) * 128],
                    qqB[:, hp * W + lo: hp * W + lo + w],
                    start=True, stop=True), [], inc="t")
                s0.wrote(tokB)
                s1.wrote(tokB)
                pin = ps[:].rearrange("p (h x) -> p h x", x=512)[:, :, lo:lo + w]
                pout = ptb[:, (kt % 2) * 1024:(kt % 2 + 1) * 1024].rearrange(
                    "p (h x) -> p h x", x=512)[:, :, lo:lo + w]
                te = b.op("sc", lambda s, pin=pin, pout=pout: s.activation(
                    pout, pin, AF.Exp, scale=SCALE, bias=cst[:, 1:2]),
                    [tokB, tok_small, tok_qk], inc="sc")
                s0.read(te)
                s1.read(te)
                exp_tok[kt % 2] = te
                tok_ready = te
                if kt >= 4 * qc:
                    base_ = (kt % 2) * 1024
                    pmA = ptb[:, base_ + lo: base_ + lo + 128]
                    pmB = ptb[:, base_ + 512 + lo: base_ + 512 + lo + 128]
                    b.op("v", lambda v, pm=pmA: v.tensor_mul(pm, pm, dmaskb),
                         [te, tok_tab], inc="v")
                    tok_ready = b.op("v", lambda v, pm=pmB: v.tensor_mul(pm, pm, dmaskb),
                                     [], inc="v")
                return tok_ready

            def emit_PV(hp, kt, tok_ready):
                nonlocal tokP
                lo = max(0, kt * 128 - qc * W)
                w = W - lo
                wtsA = [tok_ready, vaug_tok[kt]]
                if kt == 0:
                    wtsA = wtsA + bOA.wwaits() + bOB.wwaits()
                vaA = vaug[:, kt * 1040 + (2 * hp) * 65: kt * 1040 + (2 * hp) * 65 + 65]
                vaB = vaug[:, kt * 1040 + (2 * hp + 1) * 65: kt * 1040 + (2 * hp + 1) * 65 + 65]
                b.op("t", lambda t, va=vaA, kt=kt, lo=lo, w=w, last=(kt == nkt - 1): t.matmul(
                    pOA[0:65, lo:lo + w], va,
                    ptb[:, (kt % 2) * 1024 + lo:(kt % 2) * 1024 + lo + w],
                    start=(kt == 0), stop=last), wtsA, inc="t")
                tokP = b.op("t", lambda t, va=vaB, kt=kt, lo=lo, w=w, last=(kt == nkt - 1): t.matmul(
                    pOB[0:65, lo:lo + w], va,
                    ptb[:, (kt % 2) * 1024 + 512 + lo:(kt % 2) * 1024 + 512 + lo + w],
                    start=(kt == 0), stop=last), [], inc="t")
                pv_tok[kt % 2] = tokP

            pend_prev = None

            def flush_hp(hp):
                nonlocal pend_prev, pend_gather
                emit_PV(*pend_prev)
                pend_prev = None
                bOA.wrote(tokP)
                bOB.wrote(tokP)
                toA = b.op("sc", lambda s, hp=hp: s.copy(
                    oraw[:, (2 * hp) * 512:(2 * hp) * 512 + 512], pOA[0:65, :]),
                    bOA.rwaits() + [oraw_free], inc="sc")
                bOA.read(toA)
                toB = b.op("v", lambda v, hp=hp: v.tensor_copy(
                    oraw[:, (2 * hp + 1) * 512:(2 * hp + 1) * 512 + 512], pOB[0:65, :]),
                    bOB.rwaits(), inc="v")
                bOB.read(toB)
                pend_gather = (hp, toA, toB)

            for hp in range(DC):
                if qc == 0 and pend_mod:
                    emit_mod_jj(pend_mod.pop(0))
                pend = None
                for kt in range(nkt):
                    tr = emit_S(hp, kt)
                    if kt == 0 and pend_prev is not None:
                        flush_hp(hp - 1)
                    if kt == 1 and pend_gather is not None:
                        emit_gather(*pend_gather)
                        pend_gather = None
                    if pend is not None:
                        emit_PV(hp, *pend)
                    pend = (kt, tr)
                pend_prev = (hp,) + pend
            flush_hp(DC - 1)
            if qc == 0:
                assert not pend_mod
                emit_gates()
            tokg = emit_gather(*pend_gather)
            pend_gather = None
            bC.wrote(tokg)
            tok = b.op("sc", lambda s: s.activation(var_sb[0:16, :], pC[0:16, :], AF.Ln),
                       bC.rwaits() + [rsc_free[0], hscr_free[0], hscr_free[1]], inc="sc")
            bC.read(tok)
            tok = b.op("sc", lambda s: s.activation(linv_f, var_sb[0:16, :], AF.Exp, scale=-1.0),
                       [], inc="sc")
            tok_linv = b.op("v", lambda v: v.tensor_copy(linv_b[0:16, :], linv_f), [tok], inc="v")
            if qc < NQC - 1:
                tok_rinv_next = emit_ln_stats(qc + 1, lambda c, q2=qc + 1: c * NT + q2 * W,
                                              pm=psA[:, 0:512], ps2=psA[:, 512:1024],
                                              bm=bA0, bs2=bA1)
            hscr_free[0] = tok_linv
            hscr_free[1] = tok_linv
            norm_last = None
            for h in range(H):
                lb = [bD, bC][h % 2]
                tokL = b.op("t", lambda t, h=h, lb=lb: t.matmul(
                    lb.ap, ehmb[:, h * 128:(h + 1) * 128], linv_b,
                    start=True, stop=True), [tok_linv] + lb.wwaits(), inc="t")
                lb.wrote(tokL)
                tokN = b.op("v", lambda v, h=h, lb=lb: v.tensor_mul(
                    oraw[0:64, h * 512:h * 512 + 512], lb.ap[0:64, :],
                    oraw[0:64, h * 512:h * 512 + 512]), lb.rwaits(), inc="v")
                lb.read(tokN)
                norm_last = tokN
            rsc_free[0] = tokL
            rsc_free[1] = tokL

            rscr_free[0] = pv_tok[0] if pv_tok[0] is not None else rscr_free[0]
            rscr_free[1] = pv_tok[1] if pv_tok[1] is not None else rscr_free[1]
            tok_wo = None
            for e in range(DC):
                wsrc = di["wo_in"] if qc == 0 else di["woB"]
                i, tokd = wo_slot_load(wsrc[:, e * 2048:(e + 1) * 2048], via_sync=(qc > 0),
                                       extra=([pc_full[0]] if qc > 0 else []))
                ab = [bOA, bOB][e % 2]
                wts = [tokd, norm_last] + ab.wwaits()
                tok = None
                for h in range(H):
                    tok = b.op("t", lambda t, e=e, h=h, ab=ab, i=i: t.matmul(
                        ab.ap,
                        woring[:, i * 2048 + h * 128:i * 2048 + (h + 1) * 128],
                        big[:, 41216 + h * 512: 41216 + h * 512 + 512],
                        start=(h == 0), stop=(h == 15)),
                        wts if h == 0 else [], inc="t")
                ab.wrote(tok)
                wo_slot_free[i] = tok
                tok_wo = tok
                xs = xsb[:, e * NT + qc * W: e * NT + qc * W + W]
                rscr = rscrF[:, (e % 2) * W:(e % 2) * W + W]
                tok = b.op("sc", lambda s, e=e, ab=ab, rs=rscr: s.activation(
                    rs, ab.ap, AF.Identity,
                    scale=gcl_sb[:, e:e + 1], bias=gb_f[:, e:e + 1]),
                    ab.rwaits() + [tok_gates, rscr_free[e % 2]], inc="sc")
                ab.read(tok)
                resid_last = b.op("v", lambda v, xs=xs, rs=rscr: v.tensor_add(
                    xs, xs, rs), [tok], inc="v")
                rscr_free[e % 2] = resid_last
            oraw_free = tok_wo
            qkv_gate = tok_wo
            if dbg and qc == 0:
                sdma(di["d_h1"], hq, waits=[tok_h])
                sdma(di["d_qq"], qqA, waits=[tok_qk])
                kview = ksb.rearrange("p (m n) -> p m n", n=NT)[:, :, 0:W]
                sdma(di["d_k"], kview, waits=[tok_qk])
                sdma(di["d_vg"], vaug[:, 0:4 * 1040], waits=[vaug_tok[3]])
                sdma(di["d_orw"], oraw, waits=[norm_last])
            if dbg and qc == NQC - 1:
                sdma(di["d_xa"], xsb[:], waits=[resid_last])

        tbar = b.mark("t")
        vbar = b.mark("v")

        ln2_th = {}
        for qc in range(2):
            ln2_th[qc] = emit_ln(qc, lambda c, qc=qc: c * NT + qc * W, h2,
                                 lambda c, qc=qc: c * NT + qc * W, 1,
                                 h_extra=[tbar] if qc == 0 else ())
        tok_h2 = ln2_th[1][-1]

        if dbg:
            sdma(di["d_h2"], h2, waits=[tok_h2])
        w2_free = [None, None]
        g_free = [None]
        tok_h2b = None
        for th in range(2):
            tok_gall = None
            for j in range(FJ):
                if j in w1_pre:
                    i, tokd = w1_pre.pop(j)
                else:
                    i, tokd = ring_load(di["w1_in"][:, j * 1024:(j + 1) * 1024])
                for qch in range(2):
                    sl = slots4[(j * 2 + qch) % 4]
                    base = th * 1024 + qch * W
                    ths = ln2_th[th * 2 + qch]
                    wts = [tokd] + sl.wwaits() + ([tbar] if (th == 0 and j < 2) else [])
                    tok = None
                    for d in range(DC):
                        tok = b.op("t", lambda t, i=i, d=d, sl=sl, base=base: t.matmul(
                            sl.ap, wring[:, i * 1024 + d * 128:i * 1024 + (d + 1) * 128],
                            h2[:, d * NT + base: d * NT + base + W],
                            start=(d == 0), stop=(d == 7)),
                            (wts if d == 0 else []) + [ths[d]], inc="t")
                    sl.wrote(tok)
                    if qch == 1:
                        ring_free[i] = tok
                    tokg2 = b.op("sc", lambda s, sl=sl, j=j, qch=qch, th=th: s.activation(
                        g_sb[:, j * 1024 + qch * W: j * 1024 + qch * W + W], sl.ap,
                        AF.Gelu, bias=b1_sb[:, j:j + 1]),
                        sl.rwaits() + [g_free[0], tok_small], inc="sc")
                    sl.read(tokg2)
                    tok_gall = tokg2
            if dbg and th == 0:
                sdma(di["d_g"], g_sb, waits=[tok_gall])
            if th == 0:
                for j in range(3):
                    w1_pre[j] = ring_load(di["w1_in"][:, j * 1024:(j + 1) * 1024])
            for e in range(DC):
                if th == 0 and e in (0, 1, 2):
                    if e == 0:
                        ln2_tr2 = emit_ln_stats(2, lambda c: c * NT + 2 * W)
                    elif e == 1:
                        ln2_th[2] = emit_ln_h(2, lambda c: c * NT + 2 * W, h2,
                                              lambda c: c * NT + 2 * W, 1, ln2_tr2)
                        ln2_tr3 = emit_ln_stats(3, lambda c: c * NT + 3 * W)
                    else:
                        ln2_th[3] = emit_ln_h(3, lambda c: c * NT + 3 * W, h2,
                                              lambda c: c * NT + 3 * W, 1, ln2_tr3)
                tokd = gdma(w2ring[:, (e % 2) * 4096:(e % 2 + 1) * 4096],
                            di["w2_in"][:, e * 4096:(e + 1) * 4096],
                            waits=[w2_free[e % 2]] + ([tbar, vbar] if (th == 0 and e < 2) else []),
                            sem=f"w2{e % 2}")
                for qch in range(2):
                    yb = [bOA, bOB][(e * 2 + qch) % 2]
                    wts = [tokd, tok_gall] + yb.wwaits()
                    tok = None
                    for j in range(FJ):
                        tok = b.op("t", lambda t, e=e, j=j, qch=qch, yb=yb: t.matmul(
                            yb.ap,
                            w2ring[:, (e % 2) * 4096 + j * 128:(e % 2) * 4096 + (j + 1) * 128],
                            g_sb[:, j * 1024 + qch * W: j * 1024 + qch * W + W],
                            start=(j == 0), stop=(j == 31)),
                            wts if j == 0 else [], inc="t")
                    yb.wrote(tok)
                    g_free[0] = tok
                    w2_free[e % 2] = tok
                    xs = xsb[:, e * NT + th * 1024 + qch * W: e * NT + th * 1024 + qch * W + W]
                    rscr = rscrF[:, ((e * 2 + qch) % 2) * W:((e * 2 + qch) % 2) * W + W]
                    tok = b.op("sc", lambda s, e=e, yb=yb, rs=rscr: s.activation(
                        rs, yb.ap, AF.Identity,
                        scale=gcl_sb[:, 8 + e:8 + e + 1], bias=gb_f[:, 8 + e:8 + e + 1]),
                        yb.rwaits() + [rscr_free[(e * 2 + qch) % 2]], inc="sc")
                    yb.read(tok)
                    tok = b.op("v", lambda v, xs=xs, rs=rscr: v.tensor_add(
                        xs, xs, rs), [tok], inc="v")
                    rscr_free[(e * 2 + qch) % 2] = tok
                    yo = di["y_out"][:, e * NT + th * 1024 + qch * W:
                                     e * NT + th * 1024 + qch * W + W]
                    sdma(yo, xs, waits=[tok])

        sems = {}
        for nm in sem_names + ["t", "v", "sc"]:
            sems[nm] = E(nc.semaphore(nm))
        block = E(nc.Block())
        n_out = b.cnt["xdma"]

        @block.tensor
        def _(t):
            b.emit(t, "t", sems)

        @block.vector
        def _(v):
            b.emit(v, "v", sems)

        @block.scalar
        def _(s):
            b.emit(s, "sc", sems)

        @block.gpsimd
        def _(g):
            b.emit(g, "g", sems)

        @block.sync
        def _(s):
            b.emit(s, "s", sems)
            s.wait_ge(sems["xdma"], n_out)

    return nc



def _tile_lhs(A):
    K, M = A.shape
    return np.ascontiguousarray(
        A.reshape(K // 128, 128, M // 128, 128).transpose(1, 2, 0, 3).reshape(128, -1))


def _make_consts():
    n = np.arange(NT, dtype=np.float64)
    p = np.arange(128)
    i = (p % 64) // 2
    inv = 1.0 / (10000.0 ** (2.0 * i / 64.0))
    ang = n[None, :] * inv[:, None]
    cos = np.cos(ang).astype(np.float32)
    sin = np.sin(ang).astype(np.float32)
    perm = np.zeros((128, 128), np.float32)
    for ii in range(64):
        perm[2 * ii + 1, 2 * ii] = -1.0
        perm[2 * ii, 2 * ii + 1] = 1.0
    dmask = np.triu(np.ones((128, 128), np.float32))
    onesln = np.full((128, 128), 1.0 / 1024.0, np.float32)
    selg = np.zeros((128, 256), np.float32)
    for h in range(16):
        selg[64, h * 16 + h] = 1.0
    ehm = np.zeros((128, 2048), np.float32)
    for h in range(16):
        ehm[h, h * 128:(h + 1) * 128] = 1.0
    cstm = np.zeros((128, 8), np.float32)
    cstm[:, 0] = EPS
    cstm[:, 1] = -MSHIFT
    cstm[:, 2] = 1.0
    cstm[:, 3] = 0.25
    cstm[:, 4] = 1e-30
    ctile = np.zeros((128, 64), np.float32)
    ctile[:, 0:16] = 0.25
    ctile[:, 32:48] = 1.0
    ctile[:, 48:64] = 1e38
    return cos, sin, perm, dmask, onesln, selg, ehm, cstm, ctile


def kernel(L0_attn_x, L0_attn_Wq, L0_attn_Wk, L0_attn_Wv, L0_attn_Wo, L0_attn_bo,
           L0_ffn_w1, L0_ffn_b1, L0_ffn_w2, L0_ffn_b2, L0_c, L0_wmod):
    x = np.asarray(L0_attn_x, np.float32)
    Wq = np.asarray(L0_attn_Wq, np.float32)
    Wk = np.asarray(L0_attn_Wk, np.float32)
    Wv = np.asarray(L0_attn_Wv, np.float32)
    Wo = np.asarray(L0_attn_Wo, np.float32)
    bo = np.asarray(L0_attn_bo, np.float32)
    w1 = np.asarray(L0_ffn_w1, np.float32)
    b1 = np.asarray(L0_ffn_b1, np.float32)
    w2 = np.asarray(L0_ffn_w2, np.float32)
    b2 = np.asarray(L0_ffn_b2, np.float32)
    c = np.asarray(L0_c, np.float32)
    wmod = np.asarray(L0_wmod, np.float32)

    wq_h = _tile_lhs(np.ascontiguousarray(Wq.T))
    wk_h = _tile_lhs(np.ascontiguousarray(Wk.T))
    wv_h = np.ascontiguousarray(Wv.T.reshape(8, 128, 1024).transpose(1, 0, 2).reshape(128, 8192))
    wo_h = np.zeros((128, 16384), np.float32)
    wo_h[0:64] = Wo.T.reshape(16, 64, 8, 128).transpose(1, 2, 0, 3).reshape(64, 16384)
    w1_h = _tile_lhs(w1)
    w2_h = _tile_lhs(w2)
    wmod_h = _tile_lhs(wmod)
    cos, sin, perm, dmask, onesln, selg, ehm, cstm, ctile_h = _make_consts()
    bo_h = np.ascontiguousarray(bo.reshape(8, 128).T)
    b1_h = np.ascontiguousarray(b1.reshape(32, 128).T)
    b2_h = np.ascontiguousarray(b2.reshape(8, 128).T)

    nc = build_program()
    shared = dict(wq_in=wq_h, wk_in=wk_h, wv_in=wv_h, wo_in=wo_h, w1_in=w1_h,
                  w2_in=w2_h, wmod_in=wmod_h, cos_in=cos, sin_in=sin,
                  perm_in=perm, dmask_in=dmask, onesln_in=onesln, selg_in=selg,
                  ehm_in=ehm, cst_in=cstm, ctile_in=ctile_h,
                  bo_in=bo_h, b1_in=b1_h, b2_in=b2_h)
    in_maps = []
    for bi in range(8):
        xb = np.ascontiguousarray(
            x[bi].T.reshape(8, 128, NT).transpose(1, 0, 2).reshape(128, 8 * NT))
        cb = np.ascontiguousarray(c[bi].reshape(8, 128).T)
        m = dict(shared)
        m["x_in"] = xb
        m["c_in"] = cb
        in_maps.append(m)

    res = run_bass_kernel_spmd(nc, in_maps, core_ids=list(range(8)))
    outs = []
    for bi in range(8):
        y = np.asarray(res.results[bi]["y_out"])
        outs.append(y.reshape(128, 8, NT).transpose(1, 0, 2).reshape(1024, NT).T)
    return np.stack(outs).astype(np.float32)


# version 44 (final)
# speedup vs baseline: 1.1692x; 1.0018x over previous
import numpy as np
from contextlib import ExitStack

import concourse.bass as bass
import concourse.mybir as mybir
from concourse.bass_utils import run_bass_kernel_spmd

F32 = mybir.dt.float32
BF16 = mybir.dt.bfloat16
AF = mybir.ActivationFunctionType
ALU = mybir.AluOpType

D = 1024
NT = 2048
DC = 8
NQC = 4
W = 512
NKT = 16
H = 16
FJ = 32
MSHIFT = 16.0
EPS = 1e-5
SCALE = 1.0 / 64.0

ENGS = ("t", "v", "sc", "g", "s")


class Builder:
    def __init__(self):
        self.ops = {e: [] for e in ENGS}
        self.cnt = {}
        self.owner = {}
        self.waited = {e: {} for e in ENGS}

    def sem(self, name, owner=None):
        self.cnt[name] = 0
        self.owner[name] = owner

    def op(self, eng, fn, waits=(), inc=None, amt=1):
        w2 = []
        for tok in waits:
            if tok is None:
                continue
            s, v = tok
            if v <= 0 or self.owner.get(s) == eng:
                continue
            if self.waited[eng].get(s, 0) >= v:
                continue
            self.waited[eng][s] = v
            w2.append((s, v))
        rtok = None
        if inc is not None:
            self.cnt[inc] += amt
            rtok = (inc, self.cnt[inc])
        self.ops[eng].append((fn, w2, inc, amt))
        return rtok

    def mark(self, eng_sem):
        return (eng_sem, self.cnt[eng_sem])

    def emit(self, engobj, eng, sems):
        for fn, waits, inc, amt in self.ops[eng]:
            for s, v in waits:
                engobj.wait_ge(sems[s], v)
            inst = fn(engobj)
            if inc is not None:
                inst.then_inc(sems[inc], amt)


class Bank:

    def __init__(self, ap):
        self.ap = ap
        self.wtok = None
        self.rtoks = []

    def wwaits(self):
        return list(self.rtoks) + [self.wtok]

    def wrote(self, tok):
        self.wtok = tok
        self.rtoks = []

    def rwaits(self):
        return [self.wtok]

    def read(self, tok):
        self.rtoks.append(tok)


def build_program(dbg=False):
    nc = bass.Bass("TRN2", target_bir_lowering=False)

    di = {}
    for name, shape in [
        ("x_in", [128, DC * NT]), ("c_in", [128, 8]), ("bo_in", [128, 8]),
        ("b1_in", [128, 32]), ("b2_in", [128, 8]), ("cst_in", [128, 8]),
        ("wq_in", [128, 8192]), ("wk_in", [128, 8192]), ("wv_in", [128, 8192]),
        ("wo_in", [128, 16384]), ("w1_in", [128, 32768]), ("w2_in", [128, 32768]),
        ("wmod_in", [128, 49152]), ("cos_in", [128, NT]), ("sin_in", [128, NT]),
        ("perm_in", [128, 128]), ("dmask_in", [128, 128]), ("onesln_in", [128, 128]),
        ("selg_in", [128, 256]), ("ehm_in", [128, 2048]), ("ctile_in", [128, 64]),
    ]:
        di[name] = nc.dram_tensor(name, shape, F32, kind="ExternalInput").ap()
    di["y_out"] = nc.dram_tensor("y_out", [128, DC * NT], F32, kind="ExternalOutput").ap()
    for name, shape in [("wqB", [128, 8192]), ("wkB", [128, 8192]), ("wvB", [128, 8192]),
                        ("woB", [128, 16384]), ("w1B", [128, 32768]), ("w2B", [128, 32768])]:
        di[name] = nc.dram_tensor(name, shape, BF16, kind="Internal").ap()
    if dbg:
        for nm, shape, dt_ in [("d_mod", [128, 48], F32),
                               ("d_gbf", [128, 16], F32), ("d_h1", [128, DC * W], BF16),
                               ("d_qq", [128, DC * W], BF16), ("d_k", [128, DC * W], BF16),
                               ("d_vg", [128, 4 * 1040], BF16), ("d_orw", [65, 16 * 512], BF16),
                               ("d_xa", [128, DC * NT], F32), ("d_h2", [128, DC * NT], BF16),
                               ("d_g", [128, FJ * 1024], BF16),
                               ("d_w2", [128, 4096], BF16), ("d_y", [128, 512], F32),
                               ("d_gcl", [128, 16], F32)]:
            di[nm] = nc.dram_tensor(nm, shape, dt_, kind="ExternalOutput").ap()

    b = Builder()
    sem_names = (["xdma", "tdma", "cdma", "pdma"]
                 + [f"r{i}" for i in range(3)] + [f"q{i}" for i in range(4)]
                 + ["h0", "h1", "w20", "w21", "w20b", "w21b"])
    for s in sem_names:
        b.sem(s, None)
    for s, o in [("t", "t"), ("v", "v"), ("sc", "sc")]:
        b.sem(s, o)

    with ExitStack() as ES:
        E = ES.enter_context
        xsb = E(nc.sbuf_tensor("xsb", [128, DC * NT], F32))
        wring = E(nc.sbuf_tensor("wring", [128, 3 * 1024], BF16))
        c_sb = E(nc.sbuf_tensor("c_sb", [128, 8], F32))
        bo_sb = E(nc.sbuf_tensor("bo_sb", [128, 8], F32))
        b1_sb = E(nc.sbuf_tensor("b1_sb", [128, 32], F32))
        b2_sb = E(nc.sbuf_tensor("b2_sb", [128, 8], F32))
        cst = E(nc.sbuf_tensor("cst", [128, 8], F32))
        ctile = E(nc.sbuf_tensor("ctile", [128, 64], F32))
        siluc = E(nc.sbuf_tensor("siluc", [128, 8], BF16))
        siluf = E(nc.sbuf_tensor("siluf", [128, 8], F32))
        mod_sb = E(nc.sbuf_tensor("mod_sb", [128, 48], F32))
        sgn_sb = E(nc.sbuf_tensor("sgn_sb", [128, 16], F32))
        gcl_sb = E(nc.sbuf_tensor("gcl_sb", [128, 16], F32))
        ca1_sb = E(nc.sbuf_tensor("ca1_sb", [128, 16], F32))
        gb_f = E(nc.sbuf_tensor("gb_f", [128, 16], F32))
        onesln = E(nc.sbuf_tensor("onesln", [128, 128], BF16))
        xbf = E(nc.sbuf_tensor("xbf", [128, 2 * W], BF16))
        xsq = E(nc.sbuf_tensor("xsq", [128, W], BF16))
        m_sb = E(nc.sbuf_tensor("m_sb", [128, W], F32))
        var_sb = E(nc.sbuf_tensor("var_sb", [128, W], F32))
        hscr = E(nc.sbuf_tensor("hscr", [128, 2 * W], F32))
        rinv_sb = E(nc.sbuf_tensor("rinv_sb", [128, W], F32))

        big = E(nc.sbuf_tensor("big", [128, 54016], BF16))
        ksb = big[:, 0:16384]
        vaug = big[:, 16384:33024]
        qqA = big[:, 33024:37120]
        qqB = big[:, 37120:41216]
        hq = big[:, 41216:45312]
        oraw = big[0:65, 41216:49408]
        ptb = big[:, 49408:51456]
        qtmp = big[:, 49408:50432]
        rsc1 = big[:, 50432:50944]
        rsc2 = big[:, 50944:51456]
        permb = big[:, 51456:51584]
        dmaskb = big[:, 51584:51712]
        selgb = big[:, 51712:51968]
        ehmb = big[:, 51968:54016]
        h2 = big[:, 0:16384]
        rscrF = big[:, 49408:51456].bitcast(F32)
        g_sb = big[:, 16384:49152]
        linv_b = big[:, 50432:50944]

        u3 = E(nc.sbuf_tensor("u3", [128, 8192], BF16))
        woring = u3[:, 0:8192]
        w2ring = u3[:, 0:8192]
        csb = E(nc.sbuf_tensor("csb", [128, 1024], BF16))
        linv_f = hscr[0:16, 0:W]

        pS = E(nc.psum_tensor("pS", [128, 2048], F32))
        psA = pS[:, 0:1024]
        psB = pS[:, 1024:2048]
        pOA = E(nc.psum_tensor("pOA", [128, 512], F32))
        pOB = E(nc.psum_tensor("pOB", [128, 512], F32))
        pC = E(nc.psum_tensor("pC", [128, 512], F32))
        pD = E(nc.psum_tensor("pD", [128, 512], F32))

        bA0, bA1 = Bank(pS[:, 0:512]), Bank(pS[:, 512:1024])
        bB0, bB1 = Bank(pS[:, 1024:1536]), Bank(pS[:, 1536:2048])
        bOA, bOB, bC, bD = Bank(pOA[:]), Bank(pOB[:]), Bank(pC[:]), Bank(pD[:])
        slots4 = [bA0, bA1, bB0, bB1]
        vbank_pairs = [(pS[:, 0:512], pS[:, 512:1024], bA0, bA1),
                       (pS[:, 1024:1536], pS[:, 1536:2048], bB0, bB1),
                       (pOA[:], pOB[:], bOA, bOB),
                       (pC[:], pD[:], bC, bD)]

        def gdma(dst_ap, src_ap, waits=(), sem="tdma"):
            return b.op("g", lambda g, d=dst_ap, s=src_ap: g.dma_start(d, s),
                        waits, inc=sem, amt=16)

        def sdma(dst_ap, src_ap, waits=(), sem="xdma"):
            return b.op("s", lambda s_, d=dst_ap, s=src_ap: s_.dma_start(d, s),
                        waits, inc=sem, amt=16)

        sdma(xsb[:], di["x_in"])
        tok_small = None
        for nm, t_ in [("c_in", c_sb), ("bo_in", bo_sb), ("b1_in", b1_sb),
                       ("b2_in", b2_sb), ("cst_in", cst), ("ctile_in", ctile)]:
            tok_small = sdma(t_[:], di[nm])
        tok_x = tok_small

        tok_tab = None
        for nm, t_ in [("perm_in", permb), ("dmask_in", dmaskb),
                       ("onesln_in", onesln[:]), ("selg_in", selgb), ("ehm_in", ehmb)]:
            tok_tab = gdma(t_, di[nm], sem="tdma")

        b.op("sc", lambda s: s.activation(siluc[:], c_sb[:], AF.Silu),
             [tok_small], inc="sc")
        b.op("v", lambda v: v.memset(big[64:128, 45312:49408], 0.0), [], inc="v")
        tok_qz = b.op("v", lambda v: v.memset(qqA[64:128, :], 0.0), [], inc="v")
        tok_qz = b.op("v", lambda v: v.memset(qqB[0:64, :], 0.0), [], inc="v")
        tok_silu = b.op("sc", lambda s: s.activation(siluf[:], c_sb[:], AF.Silu),
                        [], inc="sc")

        ring_free = [None] * 3
        ridx = [0]
        wo_slot_free = [None] * 4
        wo_slot_tok = [None] * 4
        woidx = [0]

        def wo_slot_load(src_slice, via_sync, extra=()):
            i = woidx[0] % 4
            dma = sdma if via_sync else gdma
            tok = dma(woring[:, i * 2048:(i + 1) * 2048], src_slice,
                      waits=[wo_slot_free[i]] + list(extra), sem=f"q{i}")
            wo_slot_tok[i] = tok
            woidx[0] += 1
            return i, tok

        def ring_load(src_slice, extra=(), via_sync=False):
            i = ridx[0] % 3
            dma = sdma if via_sync else gdma
            tok = dma(wring[:, i * 1024:(i + 1) * 1024], src_slice,
                      waits=[ring_free[i]] + list(extra), sem=f"r{i}")
            ridx[0] += 1
            return i, tok

        def emit_ln_stats(qc, src_off, pm=None, ps2=None, bm=None, bs2=None):
            pm = pC[:] if pm is None else pm
            ps2 = pD[:] if ps2 is None else ps2
            bm = bC if bm is None else bm
            bs2 = bD if bs2 is None else bs2
            tok_stat = None
            for c in range(DC):
                xs = xsb[:, src_off(c): src_off(c) + W]
                pp = c % 2
                tv1 = b.op("v", lambda v, xs=xs, pp=pp: v.tensor_copy(
                    xbf[:, pp * W:(pp + 1) * W], xs),
                    [tok_x, xbf_free[pp]], inc="v")
                tv2 = b.op("v", lambda v, pp=pp: v.tensor_mul(
                    xsq[:], xbf[:, pp * W:(pp + 1) * W],
                    xbf[:, pp * W:(pp + 1) * W]), [xsq_free[0]], inc="v")
                wts = [tv1, tv2, tok_tab]
                if c == 0:
                    wts = wts + bm.wwaits() + bs2.wwaits()
                b.op("t", lambda t, pp=pp, c=c, pm=pm: t.matmul(
                    pm, onesln[:], xbf[:, pp * W:(pp + 1) * W],
                    start=(c == 0), stop=(c == 7)), wts, inc="t")
                t2 = b.op("t", lambda t, c=c, ps2=ps2: t.matmul(
                    ps2, onesln[:], xsq[:],
                    start=(c == 0), stop=(c == 7)), [], inc="t")
                xbf_free[pp] = t2
                xsq_free[0] = t2
                tok_stat = t2
            bm.wrote(tok_stat)
            bs2.wrote(tok_stat)
            tok = b.op("sc", lambda s, pm=pm: s.copy(m_sb[:], pm), bm.rwaits(), inc="sc")
            bm.read(tok)
            tok_sq = b.op("sc", lambda s: s.activation(var_sb[:], m_sb[:], AF.Square),
                          [], inc="sc")
            tok = b.op("v", lambda v, ps2=ps2: v.tensor_sub(var_sb[:], ps2, var_sb[:]),
                       bs2.rwaits() + [tok_sq], inc="v")
            bs2.read(tok)
            tok = b.op("sc", lambda s: s.activation(var_sb[:], var_sb[:], AF.Ln, bias=cst[:, 0:1]),
                       [tok, tok_small], inc="sc")
            tok_rinv = b.op("sc", lambda s: s.activation(rinv_sb[:], var_sb[:], AF.Exp, scale=-0.5),
                            [], inc="sc")
            return tok_rinv

        def emit_ln_h(qc, src_off, dst, dst_off, which, tok_rinv, h_extra=()):
            o = 8 * which
            bias_base = 24 * which
            ths = []
            for c in range(DC):
                xs = xsb[:, src_off(c): src_off(c) + W]
                pp = c % 2
                hs = hscr[:, pp * W:(pp + 1) * W]
                b.op("v", lambda v, xs=xs, hs=hs: v.tensor_sub(hs, xs, m_sb[:]),
                     [tok_rinv, hscr_free[pp]], inc="v")
                tv = b.op("v", lambda v, hs=hs: v.tensor_mul(hs, hs, rinv_sb[:]), [], inc="v")
                th = b.op("sc", lambda s, hs=hs, c=c: s.activation(
                    dst[:, dst_off(c): dst_off(c) + W], hs, AF.Identity,
                    scale=ca1_sb[:, o + c:o + c + 1],
                    bias=mod_sb[:, bias_base + c:bias_base + c + 1]),
                    [tv, tok_early if which == 0 else tok_gates] + list(h_extra), inc="sc")
                hscr_free[pp] = th
                ths.append(th)
            return ths

        def emit_ln(qc, src_off, dst, dst_off, which, h_extra=()):
            tok_rinv = emit_ln_stats(qc, src_off)
            return emit_ln_h(qc, src_off, dst, dst_off, which, tok_rinv, h_extra)

        xbf_free = [None, None]
        xsq_free = [None]
        hscr_free = [None, None]
        tok_rinv_qc0 = emit_ln_stats(0, lambda c: c * NT + 0 * W)

        tok = None
        fslot_free = [None, None]
        for jj in range(16):
            i, tokd = ring_load(di["wmod_in"][:, jj * 1024:(jj + 1) * 1024])
            for d in range(DC):
                tok = b.op("t", lambda t, i=i, d=d, jj=jj: t.matmul(
                    pOB[:, jj:jj + 1],
                    wring[:, i * 1024 + d * 128:i * 1024 + (d + 1) * 128],
                    siluc[:, d:d + 1], start=(d == 0), stop=(d == 7)),
                    ([tokd, tok_silu] + (bOB.wwaits() if jj == 0 else [])) if d == 0 else [tokd],
                    inc="t")
            ring_free[i] = tok
        bOB.wrote(tok)
        tok_evA = b.op("v", lambda v: v.tensor_copy(mod_sb[:, 0:16], pOB[:, 0:16]),
                       bOB.rwaits(), inc="v")
        bOB.read(tok_evA)
        tok_early = b.op("sc", lambda s: s.activation(ca1_sb[:, 0:8], mod_sb[:, 8:16],
                                                      AF.Identity, bias=1.0),
                         [tok_evA, tok_small], inc="sc")

        pend_mod = list(range(24, 40)) + list(range(16, 24)) + list(range(40, 48))
        mod_first = [True]
        mod_last = [None]
        tok_gates = None
        tok_modsb = None

        def emit_mod_jj(jj):
            col = jj - 16
            gate_col = (16 <= jj < 24) or (40 <= jj < 48)
            first = bD.wwaits() if mod_first[0] else []
            mod_first[0] = False
            if not gate_col:
                i, tokd = ring_load(di["wmod_in"][:, jj * 1024:(jj + 1) * 1024])
                tok = None
                for d in range(DC):
                    tok = b.op("t", lambda t, i=i, d=d, col=col: t.matmul(
                        pD[:, col:col + 1],
                        wring[:, i * 1024 + d * 128:i * 1024 + (d + 1) * 128],
                        siluc[:, d:d + 1], start=(d == 0), stop=(d == 7)),
                        ([tokd, tok_silu] + first) if d == 0 else [tokd],
                        inc="t")
                ring_free[i] = tok
            else:
                tok = None
                for half in range(2):
                    fs = half
                    tokd = sdma(hscr[:, fs * W:(fs + 1) * W],
                                di["wmod_in"][:, jj * 1024 + half * 512: jj * 1024 + (half + 1) * 512],
                                waits=[fslot_free[fs], hscr_free[fs]], sem=f"h{fs}")
                    for dd in range(4):
                        d = half * 4 + dd
                        tok = b.op("t", lambda t, fs=fs, dd=dd, d=d, col=col: t.matmul(
                            pD[:, col:col + 1],
                            hscr[:, fs * W + dd * 128: fs * W + (dd + 1) * 128],
                            siluf[:, d:d + 1], start=(d == 0), stop=(d == 7)),
                            [tokd, tok_silu] + (first if d == 0 else []),
                            inc="t")
                    fslot_free[fs] = tok
            mod_last[0] = tok

        def emit_gates():
            nonlocal tok_gates, tok_modsb
            bD.wrote(mod_last[0])
            tok_modsb = b.op("v", lambda v: v.tensor_copy(mod_sb[:, 16:48], pD[:, 0:32]),
                             bD.rwaits(), inc="v")
            bD.read(tok_modsb)
            b.op("sc", lambda s: s.activation(gcl_sb[:, 0:8], mod_sb[:, 16:24], AF.Abs),
                 [tok_modsb, tok_small], inc="sc")
            tok_ab = b.op("sc", lambda s: s.activation(gcl_sb[:, 8:16], mod_sb[:, 40:48], AF.Abs),
                          [], inc="sc")
            b.op("v", lambda v: v.tensor_max(gcl_sb[:], gcl_sb[:], ctile[:, 0:16]), [tok_ab], inc="v")
            b.op("sc", lambda s: s.activation(gb_f[:, 0:8], mod_sb[:, 16:24], AF.Tanh, scale=1e30),
                 [tok_modsb], inc="sc")
            tok_sg = b.op("sc", lambda s: s.activation(gb_f[:, 8:16], mod_sb[:, 40:48], AF.Tanh, scale=1e30),
                          [], inc="sc")
            tok_dbg2 = None
            if dbg:
                tok_dbg2 = sdma(di["d_gbf"], gb_f[:], waits=[tok_sg])
            b.op("v", lambda v: v.tensor_mul(gcl_sb[:], gcl_sb[:], gb_f[:]),
                 [tok_sg, tok_dbg2], inc="v")
            b.op("sc", lambda s: s.activation(ca1_sb[:, 8:16], mod_sb[:, 32:40],
                                              AF.Identity, bias=1.0),
                 [tok_modsb], inc="sc")
            b.op("v", lambda v: v.tensor_mul(gb_f[:, 0:8], gcl_sb[:, 0:8], bo_sb[:]),
                 [tok_small], inc="v")
            tok_gates = b.op("v", lambda v: v.tensor_mul(gb_f[:, 8:16], gcl_sb[:, 8:16], b2_sb[:]),
                             [], inc="v")
            hscr_free[0] = fslot_free[0]
            hscr_free[1] = fslot_free[1]
            if dbg:
                sdma(di["d_mod"], mod_sb[:], waits=[tok_gates, tok_modsb])
                sdma(di["d_gcl"], gcl_sb[:], waits=[tok_gates])


        pc_full = [None]
        w1_pre = {}

        def emit_precast():
            plan = [("wo_in", "woB", 16384)]
            for fn, bn, width in plan:
                for c0 in range(0, width, 2048):
                    i, tok = wo_slot_load(di[fn][:, c0:c0 + 2048], via_sync=False)
                    tokw = sdma(di[bn][:, c0:c0 + 2048], woring[:, i * 2048:(i + 1) * 2048],
                                waits=[tok], sem="pdma")
                    wo_slot_free[i] = tokw
            pc_full[0] = tokw

        qtmp_free_t = [None, None]
        qtmp_free_v = [None, None]
        rsc_free = [None, None]
        rscr_free = [None, None]
        tok_rinv_next = tok_rinv_qc0
        cs_tok = [None]
        cs_free = [None]
        vaug_tok = [None] * NKT
        oraw_free = None
        qkv_gate = None
        for qc in range(NQC):
            gdma(csb[:, 0:512], di["cos_in"][:, qc * W:(qc + 1) * W], waits=[cs_free[0]],
                 sem="cdma")
            cs_tok[0] = gdma(csb[:, 512:1024], di["sin_in"][:, qc * W:(qc + 1) * W],
                             waits=[], sem="cdma")
            th1 = emit_ln_h(qc, lambda c, qc=qc: c * NT + qc * W, hq,
                            lambda c: c * W, 0, tok_rinv_next,
                            h_extra=[qkv_gate])
            tok_h = th1[-1]

            tok_qk = None
            pend_rope = None

            def emit_rope(mi, m, proj, tq):
                nonlocal tok_qk
                qh = qtmp[:, (mi % 2) * 512:(mi % 2) * 512 + 512]
                rb = [bOA, bOB][mi % 2]
                tr = b.op("t", lambda t, rb=rb, qh=qh: t.matmul(
                    rb.ap, permb, qh, start=True, stop=True),
                    [tq, tok_tab] + rb.wwaits(), inc="t")
                rb.wrote(tr)
                qtmp_free_t[mi % 2] = tr
                t1 = b.op("v", lambda v, qh=qh: v.tensor_mul(
                    rsc1, qh, csb[:, 0:512]),
                    [rsc_free[0], cs_tok[0], tq], inc="v")
                qtmp_free_v[mi % 2] = t1
                t2 = b.op("v", lambda v, rb=rb: v.tensor_mul(
                    rsc2, rb.ap, csb[:, 512:1024]),
                    rb.rwaits() + [rsc_free[1], t1], inc="v")
                rb.read(t2)
                cs_free[0] = t2
                if proj == 0:
                    b.op("v", lambda v, m=m, qc=qc: v.tensor_add(
                        qqA[0:64, m * W:(m + 1) * W], rsc1[0:64, :], rsc2[0:64, :]),
                        [tok_qz], inc="v")
                    t3 = b.op("v", lambda v, m=m, qc=qc: v.tensor_add(
                        qqB[64:128, m * W:(m + 1) * W], rsc1[64:128, :], rsc2[64:128, :]),
                        [], inc="v")
                else:
                    dst = ksb[:, m * NT + qc * W: m * NT + qc * W + W]
                    t3 = b.op("v", lambda v, dst=dst: v.tensor_add(dst, rsc1, rsc2),
                              [], inc="v")
                rsc_free[0] = t3
                rsc_free[1] = t3
                tok_qk = t3

            for proj in range(2):
                wsrc = di["wq_in" if proj == 0 else "wk_in"]
                for m in range(DC):
                    mi = proj * DC + m
                    if m % 2 == 0:
                        islot, tokd = wo_slot_load(wsrc[:, m * 1024:(m + 2) * 1024],
                                                   via_sync=False)
                    sub = m % 2
                    sl = slots4[mi % 4]
                    wts = [tokd] + sl.wwaits()
                    tok = None
                    for d in range(DC):
                        tok = b.op("t", lambda t, islot=islot, sub=sub, d=d, sl=sl: t.matmul(
                            sl.ap,
                            woring[:, islot * 2048 + sub * 1024 + d * 128:
                                   islot * 2048 + sub * 1024 + (d + 1) * 128],
                            hq[:, d * W:(d + 1) * W], start=(d == 0), stop=(d == 7)),
                            (wts if d == 0 else []) + [th1[d]], inc="t")
                    sl.wrote(tok)
                    if sub == 1:
                        wo_slot_free[islot] = tok
                    tq = b.op("sc", lambda s, sl=sl, mi=mi: s.copy(
                        qtmp[:, (mi % 2) * 512:(mi % 2) * 512 + 512], sl.ap),
                        sl.rwaits() + [qtmp_free_t[mi % 2], qtmp_free_v[mi % 2]], inc="sc")
                    sl.read(tq)
                    if pend_rope is not None:
                        emit_rope(*pend_rope)
                    pend_rope = (mi, m, proj, tq)
                    if qc == 0 and pend_mod:
                        emit_mod_jj(pend_mod.pop(0))
            emit_rope(*pend_rope)
            pend_rope = None

            wv_slots = []
            for dpair in range(4):
                islot, tokd = wo_slot_load(
                    di["wv_in"][:, dpair * 2048:(dpair + 1) * 2048], via_sync=False)
                wv_slots.append((islot, tokd))
            for wave in range(2):
                for d in range(DC):
                    islot, tokd = wv_slots[d // 2]
                    sub = d % 2
                    for tw in range(2):
                        tt = wave * 2 + tw
                        h0, h1, bk0, bk1 = vbank_pairs[tw]
                        wts = [tokd, th1[d]] + (bk0.wwaits() + bk1.wwaits() if d == 0 else [])
                        b.op("t", lambda t, islot=islot, sub=sub, d=d, tt=tt, h0=h0: t.matmul(
                            h0, hq[:, d * W + tt * 128: d * W + tt * 128 + 128],
                            woring[:, islot * 2048 + sub * 1024: islot * 2048 + sub * 1024 + 512],
                            start=(d == 0), stop=(d == 7)), wts, inc="t")
                        tok = b.op("t", lambda t, islot=islot, sub=sub, d=d, tt=tt, h1=h1: t.matmul(
                            h1, hq[:, d * W + tt * 128: d * W + tt * 128 + 128],
                            woring[:, islot * 2048 + sub * 1024 + 512: islot * 2048 + sub * 1024 + 1024],
                            start=(d == 0), stop=(d == 7)), [], inc="t")
                        if d == 7:
                            bk0.wrote(tok)
                            bk1.wrote(tok)
                    if qc == 0 and wave == 0 and pend_mod:
                        emit_mod_jj(pend_mod.pop(0))
                if wave == 1:
                    for isl, _ in wv_slots:
                        wo_slot_free[isl] = tok
                for tw in range(2):
                    tt = wave * 2 + tw
                    kt = 4 * qc + tt
                    h0, h1, bk0, bk1 = vbank_pairs[tw]
                    vv_ = vaug[:, kt * 1040:(kt + 1) * 1040].rearrange("p (h x) -> p h x", x=65)
                    pa = h0.rearrange("p (h x) -> p h x", x=64)
                    pb = h1.rearrange("p (h x) -> p h x", x=64)
                    b.op("v", lambda v, vv=vv_, pa=pa: v.tensor_copy(vv[:, 0:8, 0:64], pa),
                         bk0.rwaits() + bk1.rwaits(), inc="v")
                    b.op("v", lambda v, vv=vv_, pb=pb: v.tensor_copy(vv[:, 8:16, 0:64], pb),
                         [], inc="v")
                    tokm = b.op("v", lambda v, vv=vv_: v.memset(vv[:, :, 64:65], 1.0), [], inc="v")
                    bk0.read(tokm)
                    bk1.read(tokm)
                    vaug_tok[kt] = tokm

            if qc == 1:
                emit_precast()
            if qc == 3:
                for j in range(3):
                    w1_pre[j] = ring_load(di["w1_in"][:, j * 1024:(j + 1) * 1024])

            Sbufs = [(psA, bA0, bA1), (psB, bB0, bB1)]
            exp_tok = [None, None]
            pv_tok = [None, None]
            nkt = 4 * qc + 4
            tokP = None
            tokg = None
            pend_gather = None

            def emit_gather(hp, toA, toB):
                wts = [toA, toB, tok_tab]
                if hp == 0:
                    wts = wts + bC.wwaits()
                b.op("t", lambda t, hp=hp: t.matmul(
                    pC[0:16, :], selgb[:, (2 * hp) * 16:(2 * hp) * 16 + 16],
                    big[:, 41216 + (2 * hp) * 512: 41216 + (2 * hp) * 512 + 512],
                    start=(hp == 0), stop=False), wts, inc="t")
                return b.op("t", lambda t, hp=hp: t.matmul(
                    pC[0:16, :], selgb[:, (2 * hp + 1) * 16:(2 * hp + 1) * 16 + 16],
                    big[:, 41216 + (2 * hp + 1) * 512: 41216 + (2 * hp + 1) * 512 + 512],
                    start=False, stop=(hp == 7)), [], inc="t")

            for hp in range(DC):
                if qc == 0 and pend_mod:
                    emit_mod_jj(pend_mod.pop(0))
                pend = None

                def emit_S(kt):
                    lo = max(0, kt * 128 - qc * W)
                    w = W - lo
                    ps, s0, s1 = Sbufs[kt % 2]
                    wts = [tok_qk, exp_tok[kt % 2]] + s0.wwaits() + s1.wwaits()
                    b.op("t", lambda t, ps=ps, hp=hp, kt=kt, lo=lo, w=w: t.matmul(
                        ps[:, lo:lo + w],
                        ksb[:, hp * NT + kt * 128: hp * NT + (kt + 1) * 128],
                        qqA[:, hp * W + lo: hp * W + lo + w],
                        start=True, stop=True), wts, inc="t")
                    tokB = b.op("t", lambda t, ps=ps, hp=hp, kt=kt, lo=lo, w=w: t.matmul(
                        ps[:, 512 + lo:512 + lo + w],
                        ksb[:, hp * NT + kt * 128: hp * NT + (kt + 1) * 128],
                        qqB[:, hp * W + lo: hp * W + lo + w],
                        start=True, stop=True), [], inc="t")
                    s0.wrote(tokB)
                    s1.wrote(tokB)
                    pin = ps[:].rearrange("p (h x) -> p h x", x=512)[:, :, lo:lo + w]
                    pout = ptb[:, (kt % 2) * 1024:(kt % 2 + 1) * 1024].rearrange(
                        "p (h x) -> p h x", x=512)[:, :, lo:lo + w]
                    te = b.op("sc", lambda s, pin=pin, pout=pout: s.activation(
                        pout, pin, AF.Exp, scale=SCALE, bias=cst[:, 1:2]),
                        [tokB, tok_small, tok_qk], inc="sc")
                    s0.read(te)
                    s1.read(te)
                    exp_tok[kt % 2] = te
                    tok_ready = te
                    if kt >= 4 * qc:
                        base_ = (kt % 2) * 1024
                        pmA = ptb[:, base_ + lo: base_ + lo + 128]
                        pmB = ptb[:, base_ + 512 + lo: base_ + 512 + lo + 128]
                        b.op("v", lambda v, pm=pmA: v.tensor_mul(pm, pm, dmaskb),
                             [te, tok_tab], inc="v")
                        tok_ready = b.op("v", lambda v, pm=pmB: v.tensor_mul(pm, pm, dmaskb),
                                         [], inc="v")
                    return tok_ready

                def emit_PV(kt, tok_ready):
                    nonlocal tokP
                    lo = max(0, kt * 128 - qc * W)
                    w = W - lo
                    wtsA = [tok_ready, vaug_tok[kt]]
                    if kt == 0:
                        wtsA = wtsA + bOA.wwaits() + bOB.wwaits()
                    vaA = vaug[:, kt * 1040 + (2 * hp) * 65: kt * 1040 + (2 * hp) * 65 + 65]
                    vaB = vaug[:, kt * 1040 + (2 * hp + 1) * 65: kt * 1040 + (2 * hp + 1) * 65 + 65]
                    b.op("t", lambda t, va=vaA, kt=kt, lo=lo, w=w, last=(kt == nkt - 1): t.matmul(
                        pOA[0:65, lo:lo + w], va,
                        ptb[:, (kt % 2) * 1024 + lo:(kt % 2) * 1024 + lo + w],
                        start=(kt == 0), stop=last), wtsA, inc="t")
                    tokP = b.op("t", lambda t, va=vaB, kt=kt, lo=lo, w=w, last=(kt == nkt - 1): t.matmul(
                        pOB[0:65, lo:lo + w], va,
                        ptb[:, (kt % 2) * 1024 + 512 + lo:(kt % 2) * 1024 + 512 + lo + w],
                        start=(kt == 0), stop=last), [], inc="t")
                    pv_tok[kt % 2] = tokP

                for kt in range(nkt):
                    tr = emit_S(kt)
                    if kt == 0 and pend_gather is not None:
                        emit_gather(*pend_gather)
                        pend_gather = None
                    if pend is not None:
                        emit_PV(*pend)
                    pend = (kt, tr)
                emit_PV(*pend)
                bOA.wrote(tokP)
                bOB.wrote(tokP)
                toA = b.op("sc", lambda s, hp=hp: s.copy(
                    oraw[:, (2 * hp) * 512:(2 * hp) * 512 + 512], pOA[0:65, :]),
                    bOA.rwaits() + [oraw_free], inc="sc")
                bOA.read(toA)
                toB = b.op("v", lambda v, hp=hp: v.tensor_copy(
                    oraw[:, (2 * hp + 1) * 512:(2 * hp + 1) * 512 + 512], pOB[0:65, :]),
                    bOB.rwaits(), inc="v")
                bOB.read(toB)
                pend_gather = (hp, toA, toB)
            if qc == 0:
                assert not pend_mod
                emit_gates()
            tokg = emit_gather(*pend_gather)
            pend_gather = None
            bC.wrote(tokg)
            tok = b.op("sc", lambda s: s.activation(var_sb[0:16, :], pC[0:16, :], AF.Ln),
                       bC.rwaits() + [rsc_free[0], hscr_free[0], hscr_free[1]], inc="sc")
            bC.read(tok)
            tok = b.op("sc", lambda s: s.activation(linv_f, var_sb[0:16, :], AF.Exp, scale=-1.0),
                       [], inc="sc")
            tok_linv = b.op("v", lambda v: v.tensor_copy(linv_b[0:16, :], linv_f), [tok], inc="v")
            if qc < NQC - 1:
                tok_rinv_next = emit_ln_stats(qc + 1, lambda c, q2=qc + 1: c * NT + q2 * W,
                                              pm=psA[:, 0:512], ps2=psA[:, 512:1024],
                                              bm=bA0, bs2=bA1)
            hscr_free[0] = tok_linv
            hscr_free[1] = tok_linv
            norm_last = None
            for h in range(H):
                lb = [bD, bC][h % 2]
                tokL = b.op("t", lambda t, h=h, lb=lb: t.matmul(
                    lb.ap, ehmb[:, h * 128:(h + 1) * 128], linv_b,
                    start=True, stop=True), [tok_linv] + lb.wwaits(), inc="t")
                lb.wrote(tokL)
                tokN = b.op("v", lambda v, h=h, lb=lb: v.tensor_mul(
                    oraw[0:64, h * 512:h * 512 + 512], lb.ap[0:64, :],
                    oraw[0:64, h * 512:h * 512 + 512]), lb.rwaits(), inc="v")
                lb.read(tokN)
                norm_last = tokN
            rsc_free[0] = tokL
            rsc_free[1] = tokL

            rscr_free[0] = pv_tok[0] if pv_tok[0] is not None else rscr_free[0]
            rscr_free[1] = pv_tok[1] if pv_tok[1] is not None else rscr_free[1]
            tok_wo = None
            for e in range(DC):
                wsrc = di["wo_in"] if qc == 0 else di["woB"]
                i, tokd = wo_slot_load(wsrc[:, e * 2048:(e + 1) * 2048], via_sync=(qc > 0),
                                       extra=([pc_full[0]] if qc > 0 else []))
                ab = [bOA, bOB][e % 2]
                wts = [tokd, norm_last] + ab.wwaits()
                tok = None
                for h in range(H):
                    tok = b.op("t", lambda t, e=e, h=h, ab=ab, i=i: t.matmul(
                        ab.ap,
                        woring[:, i * 2048 + h * 128:i * 2048 + (h + 1) * 128],
                        big[:, 41216 + h * 512: 41216 + h * 512 + 512],
                        start=(h == 0), stop=(h == 15)),
                        wts if h == 0 else [], inc="t")
                ab.wrote(tok)
                wo_slot_free[i] = tok
                tok_wo = tok
                xs = xsb[:, e * NT + qc * W: e * NT + qc * W + W]
                rscr = rscrF[:, (e % 2) * W:(e % 2) * W + W]
                tok = b.op("sc", lambda s, e=e, ab=ab, rs=rscr: s.activation(
                    rs, ab.ap, AF.Identity,
                    scale=gcl_sb[:, e:e + 1], bias=gb_f[:, e:e + 1]),
                    ab.rwaits() + [tok_gates, rscr_free[e % 2]], inc="sc")
                ab.read(tok)
                resid_last = b.op("v", lambda v, xs=xs, rs=rscr: v.tensor_add(
                    xs, xs, rs), [tok], inc="v")
                rscr_free[e % 2] = resid_last
            oraw_free = tok_wo
            qkv_gate = tok_wo
            if dbg and qc == 0:
                sdma(di["d_h1"], hq, waits=[tok_h])
                sdma(di["d_qq"], qqA, waits=[tok_qk])
                kview = ksb.rearrange("p (m n) -> p m n", n=NT)[:, :, 0:W]
                sdma(di["d_k"], kview, waits=[tok_qk])
                sdma(di["d_vg"], vaug[:, 0:4 * 1040], waits=[vaug_tok[3]])
                sdma(di["d_orw"], oraw, waits=[norm_last])
            if dbg and qc == NQC - 1:
                sdma(di["d_xa"], xsb[:], waits=[resid_last])

        tbar = b.mark("t")
        vbar = b.mark("v")

        ln2_th = {}
        for qc in range(2):
            ln2_th[qc] = emit_ln(qc, lambda c, qc=qc: c * NT + qc * W, h2,
                                 lambda c, qc=qc: c * NT + qc * W, 1,
                                 h_extra=[tbar] if qc == 0 else ())
        tok_h2 = ln2_th[1][-1]

        if dbg:
            sdma(di["d_h2"], h2, waits=[tok_h2])
        w2_free = [None, None]
        g_free = [None]
        tok_h2b = None
        for th in range(2):
            tok_gall = None
            for j in range(FJ):
                if j in w1_pre:
                    i, tokd = w1_pre.pop(j)
                else:
                    i, tokd = ring_load(di["w1_in"][:, j * 1024:(j + 1) * 1024])
                for qch in range(2):
                    sl = slots4[(j * 2 + qch) % 4]
                    base = th * 1024 + qch * W
                    ths = ln2_th[th * 2 + qch]
                    wts = [tokd] + sl.wwaits() + ([tbar] if (th == 0 and j < 2) else [])
                    tok = None
                    for d in range(DC):
                        tok = b.op("t", lambda t, i=i, d=d, sl=sl, base=base: t.matmul(
                            sl.ap, wring[:, i * 1024 + d * 128:i * 1024 + (d + 1) * 128],
                            h2[:, d * NT + base: d * NT + base + W],
                            start=(d == 0), stop=(d == 7)),
                            (wts if d == 0 else []) + [ths[d]], inc="t")
                    sl.wrote(tok)
                    if qch == 1:
                        ring_free[i] = tok
                    tokg2 = b.op("sc", lambda s, sl=sl, j=j, qch=qch, th=th: s.activation(
                        g_sb[:, j * 1024 + qch * W: j * 1024 + qch * W + W], sl.ap,
                        AF.Gelu, bias=b1_sb[:, j:j + 1]),
                        sl.rwaits() + [g_free[0], tok_small], inc="sc")
                    sl.read(tokg2)
                    tok_gall = tokg2
            if dbg and th == 0:
                sdma(di["d_g"], g_sb, waits=[tok_gall])
            if th == 0:
                for j in range(3):
                    w1_pre[j] = ring_load(di["w1_in"][:, j * 1024:(j + 1) * 1024])
            for e in range(DC):
                if th == 0 and e in (0, 1, 2):
                    if e == 0:
                        ln2_tr2 = emit_ln_stats(2, lambda c: c * NT + 2 * W)
                    elif e == 1:
                        ln2_th[2] = emit_ln_h(2, lambda c: c * NT + 2 * W, h2,
                                              lambda c: c * NT + 2 * W, 1, ln2_tr2)
                        ln2_tr3 = emit_ln_stats(3, lambda c: c * NT + 3 * W)
                    else:
                        ln2_th[3] = emit_ln_h(3, lambda c: c * NT + 3 * W, h2,
                                              lambda c: c * NT + 3 * W, 1, ln2_tr3)
                tokd = gdma(w2ring[:, (e % 2) * 4096:(e % 2) * 4096 + 2048],
                            di["w2_in"][:, e * 4096:e * 4096 + 2048],
                            waits=[w2_free[e % 2]] + ([tbar, vbar] if (th == 0 and e < 2) else []),
                            sem=f"w2{e % 2}")
                tokd2 = gdma(w2ring[:, (e % 2) * 4096 + 2048:(e % 2 + 1) * 4096],
                             di["w2_in"][:, e * 4096 + 2048:(e + 1) * 4096],
                             waits=[w2_free[e % 2]], sem=f"w2{e % 2}b")
                for qch in range(2):
                    yb = [bOA, bOB][(e * 2 + qch) % 2]
                    wts = [tokd, tok_gall] + yb.wwaits()
                    tok = None
                    for j in range(FJ):
                        tok = b.op("t", lambda t, e=e, j=j, qch=qch, yb=yb: t.matmul(
                            yb.ap,
                            w2ring[:, (e % 2) * 4096 + j * 128:(e % 2) * 4096 + (j + 1) * 128],
                            g_sb[:, j * 1024 + qch * W: j * 1024 + qch * W + W],
                            start=(j == 0), stop=(j == 31)),
                            (wts if j == 0 else []) + ([tokd2] if j == 16 else []), inc="t")
                    yb.wrote(tok)
                    g_free[0] = tok
                    w2_free[e % 2] = tok
                    xs = xsb[:, e * NT + th * 1024 + qch * W: e * NT + th * 1024 + qch * W + W]
                    rscr = rscrF[:, ((e * 2 + qch) % 2) * W:((e * 2 + qch) % 2) * W + W]
                    tok = b.op("sc", lambda s, e=e, yb=yb, rs=rscr: s.activation(
                        rs, yb.ap, AF.Identity,
                        scale=gcl_sb[:, 8 + e:8 + e + 1], bias=gb_f[:, 8 + e:8 + e + 1]),
                        yb.rwaits() + [rscr_free[(e * 2 + qch) % 2]], inc="sc")
                    yb.read(tok)
                    tok = b.op("v", lambda v, xs=xs, rs=rscr: v.tensor_add(
                        xs, xs, rs), [tok], inc="v")
                    rscr_free[(e * 2 + qch) % 2] = tok
                    yo = di["y_out"][:, e * NT + th * 1024 + qch * W:
                                     e * NT + th * 1024 + qch * W + W]
                    sdma(yo, xs, waits=[tok])

        sems = {}
        for nm in sem_names + ["t", "v", "sc"]:
            sems[nm] = E(nc.semaphore(nm))
        block = E(nc.Block())
        n_out = b.cnt["xdma"]

        @block.tensor
        def _(t):
            b.emit(t, "t", sems)

        @block.vector
        def _(v):
            b.emit(v, "v", sems)

        @block.scalar
        def _(s):
            b.emit(s, "sc", sems)

        @block.gpsimd
        def _(g):
            b.emit(g, "g", sems)

        @block.sync
        def _(s):
            b.emit(s, "s", sems)
            s.wait_ge(sems["xdma"], n_out)

    return nc



def _tile_lhs(A):
    K, M = A.shape
    return np.ascontiguousarray(
        A.reshape(K // 128, 128, M // 128, 128).transpose(1, 2, 0, 3).reshape(128, -1))


def _make_consts():
    n = np.arange(NT, dtype=np.float64)
    p = np.arange(128)
    i = (p % 64) // 2
    inv = 1.0 / (10000.0 ** (2.0 * i / 64.0))
    ang = n[None, :] * inv[:, None]
    cos = np.cos(ang).astype(np.float32)
    sin = np.sin(ang).astype(np.float32)
    perm = np.zeros((128, 128), np.float32)
    for ii in range(64):
        perm[2 * ii + 1, 2 * ii] = -1.0
        perm[2 * ii, 2 * ii + 1] = 1.0
    dmask = np.triu(np.ones((128, 128), np.float32))
    onesln = np.full((128, 128), 1.0 / 1024.0, np.float32)
    selg = np.zeros((128, 256), np.float32)
    for h in range(16):
        selg[64, h * 16 + h] = 1.0
    ehm = np.zeros((128, 2048), np.float32)
    for h in range(16):
        ehm[h, h * 128:(h + 1) * 128] = 1.0
    cstm = np.zeros((128, 8), np.float32)
    cstm[:, 0] = EPS
    cstm[:, 1] = -MSHIFT
    cstm[:, 2] = 1.0
    cstm[:, 3] = 0.25
    cstm[:, 4] = 1e-30
    ctile = np.zeros((128, 64), np.float32)
    ctile[:, 0:16] = 0.25
    ctile[:, 32:48] = 1.0
    ctile[:, 48:64] = 1e38
    return cos, sin, perm, dmask, onesln, selg, ehm, cstm, ctile


def kernel(L0_attn_x, L0_attn_Wq, L0_attn_Wk, L0_attn_Wv, L0_attn_Wo, L0_attn_bo,
           L0_ffn_w1, L0_ffn_b1, L0_ffn_w2, L0_ffn_b2, L0_c, L0_wmod):
    x = np.asarray(L0_attn_x, np.float32)
    Wq = np.asarray(L0_attn_Wq, np.float32)
    Wk = np.asarray(L0_attn_Wk, np.float32)
    Wv = np.asarray(L0_attn_Wv, np.float32)
    Wo = np.asarray(L0_attn_Wo, np.float32)
    bo = np.asarray(L0_attn_bo, np.float32)
    w1 = np.asarray(L0_ffn_w1, np.float32)
    b1 = np.asarray(L0_ffn_b1, np.float32)
    w2 = np.asarray(L0_ffn_w2, np.float32)
    b2 = np.asarray(L0_ffn_b2, np.float32)
    c = np.asarray(L0_c, np.float32)
    wmod = np.asarray(L0_wmod, np.float32)

    wq_h = _tile_lhs(np.ascontiguousarray(Wq.T))
    wk_h = _tile_lhs(np.ascontiguousarray(Wk.T))
    wv_h = np.ascontiguousarray(Wv.T.reshape(8, 128, 1024).transpose(1, 0, 2).reshape(128, 8192))
    wo_h = np.zeros((128, 16384), np.float32)
    wo_h[0:64] = Wo.T.reshape(16, 64, 8, 128).transpose(1, 2, 0, 3).reshape(64, 16384)
    w1_h = _tile_lhs(w1)
    w2_h = _tile_lhs(w2)
    wmod_h = _tile_lhs(wmod)
    cos, sin, perm, dmask, onesln, selg, ehm, cstm, ctile_h = _make_consts()
    bo_h = np.ascontiguousarray(bo.reshape(8, 128).T)
    b1_h = np.ascontiguousarray(b1.reshape(32, 128).T)
    b2_h = np.ascontiguousarray(b2.reshape(8, 128).T)

    nc = build_program()
    shared = dict(wq_in=wq_h, wk_in=wk_h, wv_in=wv_h, wo_in=wo_h, w1_in=w1_h,
                  w2_in=w2_h, wmod_in=wmod_h, cos_in=cos, sin_in=sin,
                  perm_in=perm, dmask_in=dmask, onesln_in=onesln, selg_in=selg,
                  ehm_in=ehm, cst_in=cstm, ctile_in=ctile_h,
                  bo_in=bo_h, b1_in=b1_h, b2_in=b2_h)
    in_maps = []
    for bi in range(8):
        xb = np.ascontiguousarray(
            x[bi].T.reshape(8, 128, NT).transpose(1, 0, 2).reshape(128, 8 * NT))
        cb = np.ascontiguousarray(c[bi].reshape(8, 128).T)
        m = dict(shared)
        m["x_in"] = xb
        m["c_in"] = cb
        in_maps.append(m)

    res = run_bass_kernel_spmd(nc, in_maps, core_ids=list(range(8)))
    outs = []
    for bi in range(8):
        y = np.asarray(res.results[bi]["y_out"])
        outs.append(y.reshape(128, 8, NT).transpose(1, 0, 2).reshape(1024, NT).T)
    return np.stack(outs).astype(np.float32)


# version 45
# speedup vs baseline: 1.1842x; 1.0129x over previous
import numpy as np
from contextlib import ExitStack

import concourse.bass as bass
import concourse.mybir as mybir
from concourse.bass_utils import run_bass_kernel_spmd

F32 = mybir.dt.float32
BF16 = mybir.dt.bfloat16
AF = mybir.ActivationFunctionType
ALU = mybir.AluOpType

D = 1024
NT = 2048
DC = 8
NQC = 4
W = 512
NKT = 16
H = 16
FJ = 32
MSHIFT = 16.0
EPS = 1e-5
SCALE = 1.0 / 64.0

ENGS = ("t", "v", "sc", "g", "s")


class Builder:
    def __init__(self):
        self.ops = {e: [] for e in ENGS}
        self.cnt = {}
        self.owner = {}
        self.waited = {e: {} for e in ENGS}

    def sem(self, name, owner=None):
        self.cnt[name] = 0
        self.owner[name] = owner

    def op(self, eng, fn, waits=(), inc=None, amt=1):
        w2 = []
        for tok in waits:
            if tok is None:
                continue
            s, v = tok
            if v <= 0 or self.owner.get(s) == eng:
                continue
            if self.waited[eng].get(s, 0) >= v:
                continue
            self.waited[eng][s] = v
            w2.append((s, v))
        rtok = None
        if inc is not None:
            self.cnt[inc] += amt
            rtok = (inc, self.cnt[inc])
        self.ops[eng].append((fn, w2, inc, amt))
        return rtok

    def mark(self, eng_sem):
        return (eng_sem, self.cnt[eng_sem])

    def emit(self, engobj, eng, sems):
        for fn, waits, inc, amt in self.ops[eng]:
            for s, v in waits:
                engobj.wait_ge(sems[s], v)
            inst = fn(engobj)
            if inc is not None:
                inst.then_inc(sems[inc], amt)


class Bank:

    def __init__(self, ap):
        self.ap = ap
        self.wtok = None
        self.rtoks = []

    def wwaits(self):
        return list(self.rtoks) + [self.wtok]

    def wrote(self, tok):
        self.wtok = tok
        self.rtoks = []

    def rwaits(self):
        return [self.wtok]

    def read(self, tok):
        self.rtoks.append(tok)


def build_program(dbg=False):
    nc = bass.Bass("TRN2", target_bir_lowering=False)

    di = {}
    for name, shape in [
        ("x_in", [128, DC * NT]), ("c_in", [128, 8]), ("bo_in", [128, 8]),
        ("b1_in", [128, 32]), ("b2_in", [128, 8]), ("cst_in", [128, 8]),
        ("wq_in", [128, 8192]), ("wk_in", [128, 8192]), ("wv_in", [128, 8192]),
        ("wo_in", [128, 16384]), ("w1_in", [128, 32768]), ("w2_in", [128, 32768]),
        ("wmod_in", [128, 49152]), ("cos_in", [128, NT]), ("sin_in", [128, NT]),
        ("perm_in", [128, 128]), ("dmask_in", [128, 128]), ("onesln_in", [128, 128]),
        ("selg_in", [128, 256]), ("ehm_in", [128, 2048]), ("ctile_in", [128, 64]),
    ]:
        di[name] = nc.dram_tensor(name, shape, F32, kind="ExternalInput").ap()
    di["y_out"] = nc.dram_tensor("y_out", [128, DC * NT], F32, kind="ExternalOutput").ap()
    for name, shape in [("wqB", [128, 8192]), ("wkB", [128, 8192]), ("wvB", [128, 8192]),
                        ("woB", [128, 16384]), ("w1B", [128, 32768]), ("w2B", [128, 32768])]:
        di[name] = nc.dram_tensor(name, shape, BF16, kind="Internal").ap()
    if dbg:
        for nm, shape, dt_ in [("d_mod", [128, 48], F32),
                               ("d_gbf", [128, 16], F32), ("d_h1", [128, DC * W], BF16),
                               ("d_qq", [128, DC * W], BF16), ("d_k", [128, DC * W], BF16),
                               ("d_vg", [128, 4 * 1040], BF16), ("d_orw", [65, 16 * 512], BF16),
                               ("d_xa", [128, DC * NT], F32), ("d_h2", [128, DC * NT], BF16),
                               ("d_g", [128, FJ * 1024], BF16),
                               ("d_w2", [128, 4096], BF16), ("d_y", [128, 512], F32),
                               ("d_gcl", [128, 16], F32)]:
            di[nm] = nc.dram_tensor(nm, shape, dt_, kind="ExternalOutput").ap()

    b = Builder()
    sem_names = (["xdma", "tdma", "cdma", "pdma"]
                 + [f"r{i}" for i in range(3)] + [f"q{i}" for i in range(4)]
                 + [f"q{i}b" for i in range(4)]
                 + ["h0", "h1", "w20", "w21", "w20b", "w21b"])
    for s in sem_names:
        b.sem(s, None)
    for s, o in [("t", "t"), ("v", "v"), ("sc", "sc")]:
        b.sem(s, o)

    with ExitStack() as ES:
        E = ES.enter_context
        xsb = E(nc.sbuf_tensor("xsb", [128, DC * NT], F32))
        wring = E(nc.sbuf_tensor("wring", [128, 3 * 1024], BF16))
        c_sb = E(nc.sbuf_tensor("c_sb", [128, 8], F32))
        bo_sb = E(nc.sbuf_tensor("bo_sb", [128, 8], F32))
        b1_sb = E(nc.sbuf_tensor("b1_sb", [128, 32], F32))
        b2_sb = E(nc.sbuf_tensor("b2_sb", [128, 8], F32))
        cst = E(nc.sbuf_tensor("cst", [128, 8], F32))
        ctile = E(nc.sbuf_tensor("ctile", [128, 64], F32))
        siluc = E(nc.sbuf_tensor("siluc", [128, 8], BF16))
        siluf = E(nc.sbuf_tensor("siluf", [128, 8], F32))
        mod_sb = E(nc.sbuf_tensor("mod_sb", [128, 48], F32))
        sgn_sb = E(nc.sbuf_tensor("sgn_sb", [128, 16], F32))
        gcl_sb = E(nc.sbuf_tensor("gcl_sb", [128, 16], F32))
        ca1_sb = E(nc.sbuf_tensor("ca1_sb", [128, 16], F32))
        gb_f = E(nc.sbuf_tensor("gb_f", [128, 16], F32))
        onesln = E(nc.sbuf_tensor("onesln", [128, 128], BF16))
        xbf = E(nc.sbuf_tensor("xbf", [128, 2 * W], BF16))
        xsq = E(nc.sbuf_tensor("xsq", [128, W], BF16))
        m_sb = E(nc.sbuf_tensor("m_sb", [128, W], F32))
        var_sb = E(nc.sbuf_tensor("var_sb", [128, W], F32))
        hscr = E(nc.sbuf_tensor("hscr", [128, 2 * W], F32))
        rinv_sb = E(nc.sbuf_tensor("rinv_sb", [128, W], F32))

        big = E(nc.sbuf_tensor("big", [128, 54016], BF16))
        ksb = big[:, 0:16384]
        vaug = big[:, 16384:33024]
        qqA = big[:, 33024:37120]
        qqB = big[:, 37120:41216]
        hq = big[:, 41216:45312]
        oraw = big[0:65, 41216:49408]
        ptb = big[:, 49408:51456]
        qtmp = big[:, 49408:50432]
        rsc1 = big[:, 50432:50944]
        rsc2 = big[:, 50944:51456]
        permb = big[:, 51456:51584]
        dmaskb = big[:, 51584:51712]
        selgb = big[:, 51712:51968]
        ehmb = big[:, 51968:54016]
        h2 = big[:, 0:16384]
        rscrF = big[:, 49408:51456].bitcast(F32)
        g_sb = big[:, 16384:49152]
        linv_b = big[:, 50432:50944]

        u3 = E(nc.sbuf_tensor("u3", [128, 8192], BF16))
        woring = u3[:, 0:8192]
        w2ring = u3[:, 0:8192]
        csb = E(nc.sbuf_tensor("csb", [128, 1024], BF16))
        linv_f = hscr[0:16, 0:W]

        pS = E(nc.psum_tensor("pS", [128, 2048], F32))
        psA = pS[:, 0:1024]
        psB = pS[:, 1024:2048]
        pOA = E(nc.psum_tensor("pOA", [128, 512], F32))
        pOB = E(nc.psum_tensor("pOB", [128, 512], F32))
        pC = E(nc.psum_tensor("pC", [128, 512], F32))
        pD = E(nc.psum_tensor("pD", [128, 512], F32))

        bA0, bA1 = Bank(pS[:, 0:512]), Bank(pS[:, 512:1024])
        bB0, bB1 = Bank(pS[:, 1024:1536]), Bank(pS[:, 1536:2048])
        bOA, bOB, bC, bD = Bank(pOA[:]), Bank(pOB[:]), Bank(pC[:]), Bank(pD[:])
        slots4 = [bA0, bA1, bB0, bB1]
        vbank_pairs = [(pS[:, 0:512], pS[:, 512:1024], bA0, bA1),
                       (pS[:, 1024:1536], pS[:, 1536:2048], bB0, bB1),
                       (pOA[:], pOB[:], bOA, bOB),
                       (pC[:], pD[:], bC, bD)]

        def gdma(dst_ap, src_ap, waits=(), sem="tdma"):
            return b.op("g", lambda g, d=dst_ap, s=src_ap: g.dma_start(d, s),
                        waits, inc=sem, amt=16)

        def sdma(dst_ap, src_ap, waits=(), sem="xdma"):
            return b.op("s", lambda s_, d=dst_ap, s=src_ap: s_.dma_start(d, s),
                        waits, inc=sem, amt=16)

        sdma(xsb[:], di["x_in"])
        tok_small = None
        for nm, t_ in [("c_in", c_sb), ("bo_in", bo_sb), ("b1_in", b1_sb),
                       ("b2_in", b2_sb), ("cst_in", cst), ("ctile_in", ctile)]:
            tok_small = sdma(t_[:], di[nm])
        tok_x = tok_small

        tok_tab = None
        for nm, t_ in [("perm_in", permb), ("dmask_in", dmaskb),
                       ("onesln_in", onesln[:]), ("selg_in", selgb), ("ehm_in", ehmb)]:
            tok_tab = gdma(t_, di[nm], sem="tdma")

        b.op("sc", lambda s: s.activation(siluc[:], c_sb[:], AF.Silu),
             [tok_small], inc="sc")
        b.op("v", lambda v: v.memset(big[64:128, 45312:49408], 0.0), [], inc="v")
        tok_qz = b.op("v", lambda v: v.memset(qqA[64:128, :], 0.0), [], inc="v")
        tok_qz = b.op("v", lambda v: v.memset(qqB[0:64, :], 0.0), [], inc="v")
        tok_silu = b.op("sc", lambda s: s.activation(siluf[:], c_sb[:], AF.Silu),
                        [], inc="sc")

        ring_free = [None] * 3
        ridx = [0]
        wo_slot_free = [None] * 4
        wo_slot_tok = [None] * 4
        woidx = [0]

        def wo_slot_load(src_slice, via_sync, extra=()):
            i = woidx[0] % 4
            dma = sdma if via_sync else gdma
            tokA = dma(woring[:, i * 2048:i * 2048 + 1024], src_slice[:, 0:1024],
                       waits=[wo_slot_free[i]] + list(extra), sem=f"q{i}")
            tokB = dma(woring[:, i * 2048 + 1024:(i + 1) * 2048], src_slice[:, 1024:2048],
                       waits=[wo_slot_free[i]] + list(extra), sem=f"q{i}b")
            wo_slot_tok[i] = tokB
            woidx[0] += 1
            return i, tokA, tokB

        def ring_load(src_slice, extra=(), via_sync=False):
            i = ridx[0] % 3
            dma = sdma if via_sync else gdma
            tok = dma(wring[:, i * 1024:(i + 1) * 1024], src_slice,
                      waits=[ring_free[i]] + list(extra), sem=f"r{i}")
            ridx[0] += 1
            return i, tok

        def emit_ln_stats(qc, src_off, pm=None, ps2=None, bm=None, bs2=None):
            pm = pC[:] if pm is None else pm
            ps2 = pD[:] if ps2 is None else ps2
            bm = bC if bm is None else bm
            bs2 = bD if bs2 is None else bs2
            tok_stat = None
            for c in range(DC):
                xs = xsb[:, src_off(c): src_off(c) + W]
                pp = c % 2
                tv1 = b.op("v", lambda v, xs=xs, pp=pp: v.tensor_copy(
                    xbf[:, pp * W:(pp + 1) * W], xs),
                    [tok_x, xbf_free[pp]], inc="v")
                tv2 = b.op("v", lambda v, pp=pp: v.tensor_mul(
                    xsq[:], xbf[:, pp * W:(pp + 1) * W],
                    xbf[:, pp * W:(pp + 1) * W]), [xsq_free[0]], inc="v")
                wts = [tv1, tv2, tok_tab]
                if c == 0:
                    wts = wts + bm.wwaits() + bs2.wwaits()
                b.op("t", lambda t, pp=pp, c=c, pm=pm: t.matmul(
                    pm, onesln[:], xbf[:, pp * W:(pp + 1) * W],
                    start=(c == 0), stop=(c == 7)), wts, inc="t")
                t2 = b.op("t", lambda t, c=c, ps2=ps2: t.matmul(
                    ps2, onesln[:], xsq[:],
                    start=(c == 0), stop=(c == 7)), [], inc="t")
                xbf_free[pp] = t2
                xsq_free[0] = t2
                tok_stat = t2
            bm.wrote(tok_stat)
            bs2.wrote(tok_stat)
            tok = b.op("sc", lambda s, pm=pm: s.copy(m_sb[:], pm), bm.rwaits(), inc="sc")
            bm.read(tok)
            tok_sq = b.op("sc", lambda s: s.activation(var_sb[:], m_sb[:], AF.Square),
                          [], inc="sc")
            tok = b.op("v", lambda v, ps2=ps2: v.tensor_sub(var_sb[:], ps2, var_sb[:]),
                       bs2.rwaits() + [tok_sq], inc="v")
            bs2.read(tok)
            tok = b.op("sc", lambda s: s.activation(var_sb[:], var_sb[:], AF.Ln, bias=cst[:, 0:1]),
                       [tok, tok_small], inc="sc")
            tok_rinv = b.op("sc", lambda s: s.activation(rinv_sb[:], var_sb[:], AF.Exp, scale=-0.5),
                            [], inc="sc")
            return tok_rinv

        def emit_ln_h(qc, src_off, dst, dst_off, which, tok_rinv, h_extra=()):
            o = 8 * which
            bias_base = 24 * which
            ths = []
            for c in range(DC):
                xs = xsb[:, src_off(c): src_off(c) + W]
                pp = c % 2
                hs = hscr[:, pp * W:(pp + 1) * W]
                b.op("v", lambda v, xs=xs, hs=hs: v.tensor_sub(hs, xs, m_sb[:]),
                     [tok_rinv, hscr_free[pp]], inc="v")
                tv = b.op("v", lambda v, hs=hs: v.tensor_mul(hs, hs, rinv_sb[:]), [], inc="v")
                th = b.op("sc", lambda s, hs=hs, c=c: s.activation(
                    dst[:, dst_off(c): dst_off(c) + W], hs, AF.Identity,
                    scale=ca1_sb[:, o + c:o + c + 1],
                    bias=mod_sb[:, bias_base + c:bias_base + c + 1]),
                    [tv, tok_early if which == 0 else tok_gates] + list(h_extra), inc="sc")
                hscr_free[pp] = th
                ths.append(th)
            return ths

        def emit_ln(qc, src_off, dst, dst_off, which, h_extra=()):
            tok_rinv = emit_ln_stats(qc, src_off)
            return emit_ln_h(qc, src_off, dst, dst_off, which, tok_rinv, h_extra)

        xbf_free = [None, None]
        xsq_free = [None]
        hscr_free = [None, None]
        tok_rinv_qc0 = emit_ln_stats(0, lambda c: c * NT + 0 * W)

        tok = None
        fslot_free = [None, None]
        for jj in range(16):
            i, tokd = ring_load(di["wmod_in"][:, jj * 1024:(jj + 1) * 1024])
            for d in range(DC):
                tok = b.op("t", lambda t, i=i, d=d, jj=jj: t.matmul(
                    pOB[:, jj:jj + 1],
                    wring[:, i * 1024 + d * 128:i * 1024 + (d + 1) * 128],
                    siluc[:, d:d + 1], start=(d == 0), stop=(d == 7)),
                    ([tokd, tok_silu] + (bOB.wwaits() if jj == 0 else [])) if d == 0 else [tokd],
                    inc="t")
            ring_free[i] = tok
        bOB.wrote(tok)
        tok_evA = b.op("v", lambda v: v.tensor_copy(mod_sb[:, 0:16], pOB[:, 0:16]),
                       bOB.rwaits(), inc="v")
        bOB.read(tok_evA)
        tok_early = b.op("sc", lambda s: s.activation(ca1_sb[:, 0:8], mod_sb[:, 8:16],
                                                      AF.Identity, bias=1.0),
                         [tok_evA, tok_small], inc="sc")

        pend_mod = list(range(24, 40)) + list(range(16, 24)) + list(range(40, 48))
        mod_first = [True]
        mod_last = [None]
        tok_gates = None
        tok_modsb = None

        def emit_mod_jj(jj):
            col = jj - 16
            gate_col = (16 <= jj < 24) or (40 <= jj < 48)
            first = bD.wwaits() if mod_first[0] else []
            mod_first[0] = False
            if not gate_col:
                i, tokd = ring_load(di["wmod_in"][:, jj * 1024:(jj + 1) * 1024])
                tok = None
                for d in range(DC):
                    tok = b.op("t", lambda t, i=i, d=d, col=col: t.matmul(
                        pD[:, col:col + 1],
                        wring[:, i * 1024 + d * 128:i * 1024 + (d + 1) * 128],
                        siluc[:, d:d + 1], start=(d == 0), stop=(d == 7)),
                        ([tokd, tok_silu] + first) if d == 0 else [tokd],
                        inc="t")
                ring_free[i] = tok
            else:
                tok = None
                for half in range(2):
                    fs = half
                    tokd = sdma(hscr[:, fs * W:(fs + 1) * W],
                                di["wmod_in"][:, jj * 1024 + half * 512: jj * 1024 + (half + 1) * 512],
                                waits=[fslot_free[fs], hscr_free[fs]], sem=f"h{fs}")
                    for dd in range(4):
                        d = half * 4 + dd
                        tok = b.op("t", lambda t, fs=fs, dd=dd, d=d, col=col: t.matmul(
                            pD[:, col:col + 1],
                            hscr[:, fs * W + dd * 128: fs * W + (dd + 1) * 128],
                            siluf[:, d:d + 1], start=(d == 0), stop=(d == 7)),
                            [tokd, tok_silu] + (first if d == 0 else []),
                            inc="t")
                    fslot_free[fs] = tok
            mod_last[0] = tok

        def emit_gates():
            nonlocal tok_gates, tok_modsb
            bD.wrote(mod_last[0])
            tok_modsb = b.op("v", lambda v: v.tensor_copy(mod_sb[:, 16:48], pD[:, 0:32]),
                             bD.rwaits(), inc="v")
            bD.read(tok_modsb)
            b.op("sc", lambda s: s.activation(gcl_sb[:, 0:8], mod_sb[:, 16:24], AF.Abs),
                 [tok_modsb, tok_small], inc="sc")
            tok_ab = b.op("sc", lambda s: s.activation(gcl_sb[:, 8:16], mod_sb[:, 40:48], AF.Abs),
                          [], inc="sc")
            b.op("v", lambda v: v.tensor_max(gcl_sb[:], gcl_sb[:], ctile[:, 0:16]), [tok_ab], inc="v")
            b.op("sc", lambda s: s.activation(gb_f[:, 0:8], mod_sb[:, 16:24], AF.Tanh, scale=1e30),
                 [tok_modsb], inc="sc")
            tok_sg = b.op("sc", lambda s: s.activation(gb_f[:, 8:16], mod_sb[:, 40:48], AF.Tanh, scale=1e30),
                          [], inc="sc")
            tok_dbg2 = None
            if dbg:
                tok_dbg2 = sdma(di["d_gbf"], gb_f[:], waits=[tok_sg])
            b.op("v", lambda v: v.tensor_mul(gcl_sb[:], gcl_sb[:], gb_f[:]),
                 [tok_sg, tok_dbg2], inc="v")
            b.op("sc", lambda s: s.activation(ca1_sb[:, 8:16], mod_sb[:, 32:40],
                                              AF.Identity, bias=1.0),
                 [tok_modsb], inc="sc")
            b.op("v", lambda v: v.tensor_mul(gb_f[:, 0:8], gcl_sb[:, 0:8], bo_sb[:]),
                 [tok_small], inc="v")
            tok_gates = b.op("v", lambda v: v.tensor_mul(gb_f[:, 8:16], gcl_sb[:, 8:16], b2_sb[:]),
                             [], inc="v")
            hscr_free[0] = fslot_free[0]
            hscr_free[1] = fslot_free[1]
            if dbg:
                sdma(di["d_mod"], mod_sb[:], waits=[tok_gates, tok_modsb])
                sdma(di["d_gcl"], gcl_sb[:], waits=[tok_gates])


        pc_full = [None]
        w1_pre = {}

        def emit_precast():
            plan = [("wo_in", "woB", 16384)]
            for fn, bn, width in plan:
                for c0 in range(0, width, 2048):
                    i, tokA, tokB = wo_slot_load(di[fn][:, c0:c0 + 2048], via_sync=False)
                    tokw = sdma(di[bn][:, c0:c0 + 2048], woring[:, i * 2048:(i + 1) * 2048],
                                waits=[tokA, tokB], sem="pdma")
                    wo_slot_free[i] = tokw
            pc_full[0] = tokw

        qtmp_free_t = [None, None]
        qtmp_free_v = [None, None]
        rsc_free = [None, None]
        rscr_free = [None, None]
        tok_rinv_next = tok_rinv_qc0
        cs_tok = [None]
        cs_free = [None]
        vaug_tok = [None] * NKT
        oraw_free = None
        qkv_gate = None
        for qc in range(NQC):
            gdma(csb[:, 0:512], di["cos_in"][:, qc * W:(qc + 1) * W], waits=[cs_free[0]],
                 sem="cdma")
            cs_tok[0] = gdma(csb[:, 512:1024], di["sin_in"][:, qc * W:(qc + 1) * W],
                             waits=[], sem="cdma")
            th1 = emit_ln_h(qc, lambda c, qc=qc: c * NT + qc * W, hq,
                            lambda c: c * W, 0, tok_rinv_next,
                            h_extra=[qkv_gate])
            tok_h = th1[-1]

            tok_qk = None
            pend_rope = None

            def emit_rope(mi, m, proj, tq):
                nonlocal tok_qk
                qh = qtmp[:, (mi % 2) * 512:(mi % 2) * 512 + 512]
                rb = [bOA, bOB][mi % 2]
                tr = b.op("t", lambda t, rb=rb, qh=qh: t.matmul(
                    rb.ap, permb, qh, start=True, stop=True),
                    [tq, tok_tab] + rb.wwaits(), inc="t")
                rb.wrote(tr)
                qtmp_free_t[mi % 2] = tr
                t1 = b.op("v", lambda v, qh=qh: v.tensor_mul(
                    rsc1, qh, csb[:, 0:512]),
                    [rsc_free[0], cs_tok[0], tq], inc="v")
                qtmp_free_v[mi % 2] = t1
                t2 = b.op("v", lambda v, rb=rb: v.tensor_mul(
                    rsc2, rb.ap, csb[:, 512:1024]),
                    rb.rwaits() + [rsc_free[1], t1], inc="v")
                rb.read(t2)
                cs_free[0] = t2
                if proj == 0:
                    b.op("v", lambda v, m=m, qc=qc: v.tensor_add(
                        qqA[0:64, m * W:(m + 1) * W], rsc1[0:64, :], rsc2[0:64, :]),
                        [tok_qz], inc="v")
                    t3 = b.op("v", lambda v, m=m, qc=qc: v.tensor_add(
                        qqB[64:128, m * W:(m + 1) * W], rsc1[64:128, :], rsc2[64:128, :]),
                        [], inc="v")
                else:
                    dst = ksb[:, m * NT + qc * W: m * NT + qc * W + W]
                    t3 = b.op("v", lambda v, dst=dst: v.tensor_add(dst, rsc1, rsc2),
                              [], inc="v")
                rsc_free[0] = t3
                rsc_free[1] = t3
                tok_qk = t3

            for proj in range(2):
                wsrc = di["wq_in" if proj == 0 else "wk_in"]
                for m in range(DC):
                    mi = proj * DC + m
                    if m % 2 == 0:
                        islot, tokdA, tokdB = wo_slot_load(wsrc[:, m * 1024:(m + 2) * 1024],
                                                           via_sync=False)
                    sub = m % 2
                    tokd = tokdA if sub == 0 else tokdB
                    sl = slots4[mi % 4]
                    wts = [tokd] + sl.wwaits()
                    tok = None
                    for d in range(DC):
                        tok = b.op("t", lambda t, islot=islot, sub=sub, d=d, sl=sl: t.matmul(
                            sl.ap,
                            woring[:, islot * 2048 + sub * 1024 + d * 128:
                                   islot * 2048 + sub * 1024 + (d + 1) * 128],
                            hq[:, d * W:(d + 1) * W], start=(d == 0), stop=(d == 7)),
                            (wts if d == 0 else []) + [th1[d]], inc="t")
                    sl.wrote(tok)
                    if sub == 1:
                        wo_slot_free[islot] = tok
                    tq = b.op("sc", lambda s, sl=sl, mi=mi: s.copy(
                        qtmp[:, (mi % 2) * 512:(mi % 2) * 512 + 512], sl.ap),
                        sl.rwaits() + [qtmp_free_t[mi % 2], qtmp_free_v[mi % 2]], inc="sc")
                    sl.read(tq)
                    if pend_rope is not None:
                        emit_rope(*pend_rope)
                    pend_rope = (mi, m, proj, tq)
                    if qc == 0 and pend_mod:
                        emit_mod_jj(pend_mod.pop(0))
            emit_rope(*pend_rope)
            pend_rope = None

            wv_slots = []
            for dpair in range(4):
                islot, tokdA, tokdB = wo_slot_load(
                    di["wv_in"][:, dpair * 2048:(dpair + 1) * 2048], via_sync=False)
                wv_slots.append((islot, tokdA, tokdB))
            for wave in range(2):
                for d in range(DC):
                    islot, tokdA, tokdB = wv_slots[d // 2]
                    sub = d % 2
                    tokd = tokdA if sub == 0 else tokdB
                    for tw in range(2):
                        tt = wave * 2 + tw
                        h0, h1, bk0, bk1 = vbank_pairs[tw]
                        wts = [tokd, th1[d]] + (bk0.wwaits() + bk1.wwaits() if d == 0 else [])
                        b.op("t", lambda t, islot=islot, sub=sub, d=d, tt=tt, h0=h0: t.matmul(
                            h0, hq[:, d * W + tt * 128: d * W + tt * 128 + 128],
                            woring[:, islot * 2048 + sub * 1024: islot * 2048 + sub * 1024 + 512],
                            start=(d == 0), stop=(d == 7)), wts, inc="t")
                        tok = b.op("t", lambda t, islot=islot, sub=sub, d=d, tt=tt, h1=h1: t.matmul(
                            h1, hq[:, d * W + tt * 128: d * W + tt * 128 + 128],
                            woring[:, islot * 2048 + sub * 1024 + 512: islot * 2048 + sub * 1024 + 1024],
                            start=(d == 0), stop=(d == 7)), [], inc="t")
                        if d == 7:
                            bk0.wrote(tok)
                            bk1.wrote(tok)
                    if qc == 0 and wave == 0 and pend_mod:
                        emit_mod_jj(pend_mod.pop(0))
                if wave == 1:
                    for isl, _, _ in wv_slots:
                        wo_slot_free[isl] = tok
                for tw in range(2):
                    tt = wave * 2 + tw
                    kt = 4 * qc + tt
                    h0, h1, bk0, bk1 = vbank_pairs[tw]
                    vv_ = vaug[:, kt * 1040:(kt + 1) * 1040].rearrange("p (h x) -> p h x", x=65)
                    pa = h0.rearrange("p (h x) -> p h x", x=64)
                    pb = h1.rearrange("p (h x) -> p h x", x=64)
                    b.op("v", lambda v, vv=vv_, pa=pa: v.tensor_copy(vv[:, 0:8, 0:64], pa),
                         bk0.rwaits() + bk1.rwaits(), inc="v")
                    b.op("v", lambda v, vv=vv_, pb=pb: v.tensor_copy(vv[:, 8:16, 0:64], pb),
                         [], inc="v")
                    tokm = b.op("v", lambda v, vv=vv_: v.memset(vv[:, :, 64:65], 1.0), [], inc="v")
                    bk0.read(tokm)
                    bk1.read(tokm)
                    vaug_tok[kt] = tokm

            if qc == 1:
                emit_precast()
            if qc == 3:
                for j in range(3):
                    w1_pre[j] = ring_load(di["w1_in"][:, j * 1024:(j + 1) * 1024])

            Sbufs = [(psA, bA0, bA1), (psB, bB0, bB1)]
            exp_tok = [None, None]
            pv_tok = [None, None]
            nkt = 4 * qc + 4
            tokP = None
            tokg = None
            pend_gather = None

            def emit_gather(hp, toA, toB):
                wts = [toA, toB, tok_tab]
                if hp == 0:
                    wts = wts + bC.wwaits()
                b.op("t", lambda t, hp=hp: t.matmul(
                    pC[0:16, :], selgb[:, (2 * hp) * 16:(2 * hp) * 16 + 16],
                    big[:, 41216 + (2 * hp) * 512: 41216 + (2 * hp) * 512 + 512],
                    start=(hp == 0), stop=False), wts, inc="t")
                return b.op("t", lambda t, hp=hp: t.matmul(
                    pC[0:16, :], selgb[:, (2 * hp + 1) * 16:(2 * hp + 1) * 16 + 16],
                    big[:, 41216 + (2 * hp + 1) * 512: 41216 + (2 * hp + 1) * 512 + 512],
                    start=False, stop=(hp == 7)), [], inc="t")

            for hp in range(DC):
                if qc == 0 and pend_mod:
                    emit_mod_jj(pend_mod.pop(0))
                pend = None

                def emit_S(kt):
                    lo = max(0, kt * 128 - qc * W)
                    w = W - lo
                    ps, s0, s1 = Sbufs[kt % 2]
                    wts = [tok_qk, exp_tok[kt % 2]] + s0.wwaits() + s1.wwaits()
                    b.op("t", lambda t, ps=ps, hp=hp, kt=kt, lo=lo, w=w: t.matmul(
                        ps[:, lo:lo + w],
                        ksb[:, hp * NT + kt * 128: hp * NT + (kt + 1) * 128],
                        qqA[:, hp * W + lo: hp * W + lo + w],
                        start=True, stop=True), wts, inc="t")
                    tokB = b.op("t", lambda t, ps=ps, hp=hp, kt=kt, lo=lo, w=w: t.matmul(
                        ps[:, 512 + lo:512 + lo + w],
                        ksb[:, hp * NT + kt * 128: hp * NT + (kt + 1) * 128],
                        qqB[:, hp * W + lo: hp * W + lo + w],
                        start=True, stop=True), [], inc="t")
                    s0.wrote(tokB)
                    s1.wrote(tokB)
                    pin = ps[:].rearrange("p (h x) -> p h x", x=512)[:, :, lo:lo + w]
                    pout = ptb[:, (kt % 2) * 1024:(kt % 2 + 1) * 1024].rearrange(
                        "p (h x) -> p h x", x=512)[:, :, lo:lo + w]
                    te = b.op("sc", lambda s, pin=pin, pout=pout: s.activation(
                        pout, pin, AF.Exp, scale=SCALE, bias=cst[:, 1:2]),
                        [tokB, tok_small, tok_qk], inc="sc")
                    s0.read(te)
                    s1.read(te)
                    exp_tok[kt % 2] = te
                    tok_ready = te
                    if kt >= 4 * qc:
                        base_ = (kt % 2) * 1024
                        pmA = ptb[:, base_ + lo: base_ + lo + 128]
                        pmB = ptb[:, base_ + 512 + lo: base_ + 512 + lo + 128]
                        b.op("v", lambda v, pm=pmA: v.tensor_mul(pm, pm, dmaskb),
                             [te, tok_tab], inc="v")
                        tok_ready = b.op("v", lambda v, pm=pmB: v.tensor_mul(pm, pm, dmaskb),
                                         [], inc="v")
                    return tok_ready

                def emit_PV(kt, tok_ready):
                    nonlocal tokP
                    lo = max(0, kt * 128 - qc * W)
                    w = W - lo
                    wtsA = [tok_ready, vaug_tok[kt]]
                    if kt == 0:
                        wtsA = wtsA + bOA.wwaits() + bOB.wwaits()
                    vaA = vaug[:, kt * 1040 + (2 * hp) * 65: kt * 1040 + (2 * hp) * 65 + 65]
                    vaB = vaug[:, kt * 1040 + (2 * hp + 1) * 65: kt * 1040 + (2 * hp + 1) * 65 + 65]
                    b.op("t", lambda t, va=vaA, kt=kt, lo=lo, w=w, last=(kt == nkt - 1): t.matmul(
                        pOA[0:65, lo:lo + w], va,
                        ptb[:, (kt % 2) * 1024 + lo:(kt % 2) * 1024 + lo + w],
                        start=(kt == 0), stop=last), wtsA, inc="t")
                    tokP = b.op("t", lambda t, va=vaB, kt=kt, lo=lo, w=w, last=(kt == nkt - 1): t.matmul(
                        pOB[0:65, lo:lo + w], va,
                        ptb[:, (kt % 2) * 1024 + 512 + lo:(kt % 2) * 1024 + 512 + lo + w],
                        start=(kt == 0), stop=last), [], inc="t")
                    pv_tok[kt % 2] = tokP

                for kt in range(nkt):
                    tr = emit_S(kt)
                    if kt == 0 and pend_gather is not None:
                        emit_gather(*pend_gather)
                        pend_gather = None
                    if pend is not None:
                        emit_PV(*pend)
                    pend = (kt, tr)
                emit_PV(*pend)
                bOA.wrote(tokP)
                bOB.wrote(tokP)
                toA = b.op("sc", lambda s, hp=hp: s.copy(
                    oraw[:, (2 * hp) * 512:(2 * hp) * 512 + 512], pOA[0:65, :]),
                    bOA.rwaits() + [oraw_free], inc="sc")
                bOA.read(toA)
                toB = b.op("v", lambda v, hp=hp: v.tensor_copy(
                    oraw[:, (2 * hp + 1) * 512:(2 * hp + 1) * 512 + 512], pOB[0:65, :]),
                    bOB.rwaits(), inc="v")
                bOB.read(toB)
                pend_gather = (hp, toA, toB)
            if qc == 0:
                assert not pend_mod
                emit_gates()
            tokg = emit_gather(*pend_gather)
            pend_gather = None
            bC.wrote(tokg)
            tok = b.op("sc", lambda s: s.activation(var_sb[0:16, :], pC[0:16, :], AF.Ln),
                       bC.rwaits() + [rsc_free[0], hscr_free[0], hscr_free[1]], inc="sc")
            bC.read(tok)
            tok = b.op("sc", lambda s: s.activation(linv_f, var_sb[0:16, :], AF.Exp, scale=-1.0),
                       [], inc="sc")
            tok_linv = b.op("v", lambda v: v.tensor_copy(linv_b[0:16, :], linv_f), [tok], inc="v")
            if qc < NQC - 1:
                tok_rinv_next = emit_ln_stats(qc + 1, lambda c, q2=qc + 1: c * NT + q2 * W,
                                              pm=psA[:, 0:512], ps2=psA[:, 512:1024],
                                              bm=bA0, bs2=bA1)
            hscr_free[0] = tok_linv
            hscr_free[1] = tok_linv
            norm_last = None
            for h in range(H):
                lb = [bD, bC][h % 2]
                tokL = b.op("t", lambda t, h=h, lb=lb: t.matmul(
                    lb.ap, ehmb[:, h * 128:(h + 1) * 128], linv_b,
                    start=True, stop=True), [tok_linv] + lb.wwaits(), inc="t")
                lb.wrote(tokL)
                tokN = b.op("v", lambda v, h=h, lb=lb: v.tensor_mul(
                    oraw[0:64, h * 512:h * 512 + 512], lb.ap[0:64, :],
                    oraw[0:64, h * 512:h * 512 + 512]), lb.rwaits(), inc="v")
                lb.read(tokN)
                norm_last = tokN
            rsc_free[0] = tokL
            rsc_free[1] = tokL

            rscr_free[0] = pv_tok[0] if pv_tok[0] is not None else rscr_free[0]
            rscr_free[1] = pv_tok[1] if pv_tok[1] is not None else rscr_free[1]
            tok_wo = None
            for e in range(DC):
                wsrc = di["wo_in"] if qc == 0 else di["woB"]
                i, tokdA, tokdB = wo_slot_load(wsrc[:, e * 2048:(e + 1) * 2048],
                                               via_sync=(qc > 0),
                                               extra=([pc_full[0]] if qc > 0 else []))
                ab = [bOA, bOB][e % 2]
                wts = [tokdA, norm_last] + ab.wwaits()
                tok = None
                for h in range(H):
                    tok = b.op("t", lambda t, e=e, h=h, ab=ab, i=i: t.matmul(
                        ab.ap,
                        woring[:, i * 2048 + h * 128:i * 2048 + (h + 1) * 128],
                        big[:, 41216 + h * 512: 41216 + h * 512 + 512],
                        start=(h == 0), stop=(h == 15)),
                        (wts if h == 0 else []) + ([tokdB] if h == 8 else []), inc="t")
                ab.wrote(tok)
                wo_slot_free[i] = tok
                tok_wo = tok
                xs = xsb[:, e * NT + qc * W: e * NT + qc * W + W]
                rscr = rscrF[:, (e % 2) * W:(e % 2) * W + W]
                tok = b.op("sc", lambda s, e=e, ab=ab, rs=rscr: s.activation(
                    rs, ab.ap, AF.Identity,
                    scale=gcl_sb[:, e:e + 1], bias=gb_f[:, e:e + 1]),
                    ab.rwaits() + [tok_gates, rscr_free[e % 2]], inc="sc")
                ab.read(tok)
                resid_last = b.op("v", lambda v, xs=xs, rs=rscr: v.tensor_add(
                    xs, xs, rs), [tok], inc="v")
                rscr_free[e % 2] = resid_last
            oraw_free = tok_wo
            qkv_gate = tok_wo
            if dbg and qc == 0:
                sdma(di["d_h1"], hq, waits=[tok_h])
                sdma(di["d_qq"], qqA, waits=[tok_qk])
                kview = ksb.rearrange("p (m n) -> p m n", n=NT)[:, :, 0:W]
                sdma(di["d_k"], kview, waits=[tok_qk])
                sdma(di["d_vg"], vaug[:, 0:4 * 1040], waits=[vaug_tok[3]])
                sdma(di["d_orw"], oraw, waits=[norm_last])
            if dbg and qc == NQC - 1:
                sdma(di["d_xa"], xsb[:], waits=[resid_last])

        tbar = b.mark("t")
        vbar = b.mark("v")

        ln2_th = {}
        for qc in range(2):
            ln2_th[qc] = emit_ln(qc, lambda c, qc=qc: c * NT + qc * W, h2,
                                 lambda c, qc=qc: c * NT + qc * W, 1,
                                 h_extra=[tbar] if qc == 0 else ())
        tok_h2 = ln2_th[1][-1]

        if dbg:
            sdma(di["d_h2"], h2, waits=[tok_h2])
        w2_free = [None, None]
        g_free = [None]
        tok_h2b = None
        for th in range(2):
            tok_gall = None
            for j in range(FJ):
                if j in w1_pre:
                    i, tokd = w1_pre.pop(j)
                else:
                    i, tokd = ring_load(di["w1_in"][:, j * 1024:(j + 1) * 1024])
                for qch in range(2):
                    sl = slots4[(j * 2 + qch) % 4]
                    base = th * 1024 + qch * W
                    ths = ln2_th[th * 2 + qch]
                    wts = [tokd] + sl.wwaits() + ([tbar] if (th == 0 and j < 2) else [])
                    tok = None
                    for d in range(DC):
                        tok = b.op("t", lambda t, i=i, d=d, sl=sl, base=base: t.matmul(
                            sl.ap, wring[:, i * 1024 + d * 128:i * 1024 + (d + 1) * 128],
                            h2[:, d * NT + base: d * NT + base + W],
                            start=(d == 0), stop=(d == 7)),
                            (wts if d == 0 else []) + [ths[d]], inc="t")
                    sl.wrote(tok)
                    if qch == 1:
                        ring_free[i] = tok
                    tokg2 = b.op("sc", lambda s, sl=sl, j=j, qch=qch, th=th: s.activation(
                        g_sb[:, j * 1024 + qch * W: j * 1024 + qch * W + W], sl.ap,
                        AF.Gelu, bias=b1_sb[:, j:j + 1]),
                        sl.rwaits() + [g_free[0], tok_small], inc="sc")
                    sl.read(tokg2)
                    tok_gall = tokg2
            if dbg and th == 0:
                sdma(di["d_g"], g_sb, waits=[tok_gall])
            if th == 0:
                for j in range(3):
                    w1_pre[j] = ring_load(di["w1_in"][:, j * 1024:(j + 1) * 1024])
            for e in range(DC):
                if th == 0 and e in (0, 1, 2):
                    if e == 0:
                        ln2_tr2 = emit_ln_stats(2, lambda c: c * NT + 2 * W)
                    elif e == 1:
                        ln2_th[2] = emit_ln_h(2, lambda c: c * NT + 2 * W, h2,
                                              lambda c: c * NT + 2 * W, 1, ln2_tr2)
                        ln2_tr3 = emit_ln_stats(3, lambda c: c * NT + 3 * W)
                    else:
                        ln2_th[3] = emit_ln_h(3, lambda c: c * NT + 3 * W, h2,
                                              lambda c: c * NT + 3 * W, 1, ln2_tr3)
                tokd = gdma(w2ring[:, (e % 2) * 4096:(e % 2) * 4096 + 2048],
                            di["w2_in"][:, e * 4096:e * 4096 + 2048],
                            waits=[w2_free[e % 2]] + ([tbar, vbar] if (th == 0 and e < 2) else []),
                            sem=f"w2{e % 2}")
                tokd2 = gdma(w2ring[:, (e % 2) * 4096 + 2048:(e % 2 + 1) * 4096],
                             di["w2_in"][:, e * 4096 + 2048:(e + 1) * 4096],
                             waits=[w2_free[e % 2]], sem=f"w2{e % 2}b")
                for qch in range(2):
                    yb = [bOA, bOB][(e * 2 + qch) % 2]
                    wts = [tokd, tok_gall] + yb.wwaits()
                    tok = None
                    for j in range(FJ):
                        tok = b.op("t", lambda t, e=e, j=j, qch=qch, yb=yb: t.matmul(
                            yb.ap,
                            w2ring[:, (e % 2) * 4096 + j * 128:(e % 2) * 4096 + (j + 1) * 128],
                            g_sb[:, j * 1024 + qch * W: j * 1024 + qch * W + W],
                            start=(j == 0), stop=(j == 31)),
                            (wts if j == 0 else []) + ([tokd2] if j == 16 else []), inc="t")
                    yb.wrote(tok)
                    g_free[0] = tok
                    w2_free[e % 2] = tok
                    xs = xsb[:, e * NT + th * 1024 + qch * W: e * NT + th * 1024 + qch * W + W]
                    rscr = rscrF[:, ((e * 2 + qch) % 2) * W:((e * 2 + qch) % 2) * W + W]
                    tok = b.op("sc", lambda s, e=e, yb=yb, rs=rscr: s.activation(
                        rs, yb.ap, AF.Identity,
                        scale=gcl_sb[:, 8 + e:8 + e + 1], bias=gb_f[:, 8 + e:8 + e + 1]),
                        yb.rwaits() + [rscr_free[(e * 2 + qch) % 2]], inc="sc")
                    yb.read(tok)
                    tok = b.op("v", lambda v, xs=xs, rs=rscr: v.tensor_add(
                        xs, xs, rs), [tok], inc="v")
                    rscr_free[(e * 2 + qch) % 2] = tok
                    yo = di["y_out"][:, e * NT + th * 1024 + qch * W:
                                     e * NT + th * 1024 + qch * W + W]
                    sdma(yo, xs, waits=[tok])

        sems = {}
        for nm in sem_names + ["t", "v", "sc"]:
            sems[nm] = E(nc.semaphore(nm))
        block = E(nc.Block())
        n_out = b.cnt["xdma"]

        @block.tensor
        def _(t):
            b.emit(t, "t", sems)

        @block.vector
        def _(v):
            b.emit(v, "v", sems)

        @block.scalar
        def _(s):
            b.emit(s, "sc", sems)

        @block.gpsimd
        def _(g):
            b.emit(g, "g", sems)

        @block.sync
        def _(s):
            b.emit(s, "s", sems)
            s.wait_ge(sems["xdma"], n_out)

    return nc



def _tile_lhs(A):
    K, M = A.shape
    return np.ascontiguousarray(
        A.reshape(K // 128, 128, M // 128, 128).transpose(1, 2, 0, 3).reshape(128, -1))


def _make_consts():
    n = np.arange(NT, dtype=np.float64)
    p = np.arange(128)
    i = (p % 64) // 2
    inv = 1.0 / (10000.0 ** (2.0 * i / 64.0))
    ang = n[None, :] * inv[:, None]
    cos = np.cos(ang).astype(np.float32)
    sin = np.sin(ang).astype(np.float32)
    perm = np.zeros((128, 128), np.float32)
    for ii in range(64):
        perm[2 * ii + 1, 2 * ii] = -1.0
        perm[2 * ii, 2 * ii + 1] = 1.0
    dmask = np.triu(np.ones((128, 128), np.float32))
    onesln = np.full((128, 128), 1.0 / 1024.0, np.float32)
    selg = np.zeros((128, 256), np.float32)
    for h in range(16):
        selg[64, h * 16 + h] = 1.0
    ehm = np.zeros((128, 2048), np.float32)
    for h in range(16):
        ehm[h, h * 128:(h + 1) * 128] = 1.0
    cstm = np.zeros((128, 8), np.float32)
    cstm[:, 0] = EPS
    cstm[:, 1] = -MSHIFT
    cstm[:, 2] = 1.0
    cstm[:, 3] = 0.25
    cstm[:, 4] = 1e-30
    ctile = np.zeros((128, 64), np.float32)
    ctile[:, 0:16] = 0.25
    ctile[:, 32:48] = 1.0
    ctile[:, 48:64] = 1e38
    return cos, sin, perm, dmask, onesln, selg, ehm, cstm, ctile


def kernel(L0_attn_x, L0_attn_Wq, L0_attn_Wk, L0_attn_Wv, L0_attn_Wo, L0_attn_bo,
           L0_ffn_w1, L0_ffn_b1, L0_ffn_w2, L0_ffn_b2, L0_c, L0_wmod):
    x = np.asarray(L0_attn_x, np.float32)
    Wq = np.asarray(L0_attn_Wq, np.float32)
    Wk = np.asarray(L0_attn_Wk, np.float32)
    Wv = np.asarray(L0_attn_Wv, np.float32)
    Wo = np.asarray(L0_attn_Wo, np.float32)
    bo = np.asarray(L0_attn_bo, np.float32)
    w1 = np.asarray(L0_ffn_w1, np.float32)
    b1 = np.asarray(L0_ffn_b1, np.float32)
    w2 = np.asarray(L0_ffn_w2, np.float32)
    b2 = np.asarray(L0_ffn_b2, np.float32)
    c = np.asarray(L0_c, np.float32)
    wmod = np.asarray(L0_wmod, np.float32)

    wq_h = _tile_lhs(np.ascontiguousarray(Wq.T))
    wk_h = _tile_lhs(np.ascontiguousarray(Wk.T))
    wv_h = np.ascontiguousarray(Wv.T.reshape(8, 128, 1024).transpose(1, 0, 2).reshape(128, 8192))
    wo_h = np.zeros((128, 16384), np.float32)
    wo_h[0:64] = Wo.T.reshape(16, 64, 8, 128).transpose(1, 2, 0, 3).reshape(64, 16384)
    w1_h = _tile_lhs(w1)
    w2_h = _tile_lhs(w2)
    wmod_h = _tile_lhs(wmod)
    cos, sin, perm, dmask, onesln, selg, ehm, cstm, ctile_h = _make_consts()
    bo_h = np.ascontiguousarray(bo.reshape(8, 128).T)
    b1_h = np.ascontiguousarray(b1.reshape(32, 128).T)
    b2_h = np.ascontiguousarray(b2.reshape(8, 128).T)

    nc = build_program()
    shared = dict(wq_in=wq_h, wk_in=wk_h, wv_in=wv_h, wo_in=wo_h, w1_in=w1_h,
                  w2_in=w2_h, wmod_in=wmod_h, cos_in=cos, sin_in=sin,
                  perm_in=perm, dmask_in=dmask, onesln_in=onesln, selg_in=selg,
                  ehm_in=ehm, cst_in=cstm, ctile_in=ctile_h,
                  bo_in=bo_h, b1_in=b1_h, b2_in=b2_h)
    in_maps = []
    for bi in range(8):
        xb = np.ascontiguousarray(
            x[bi].T.reshape(8, 128, NT).transpose(1, 0, 2).reshape(128, 8 * NT))
        cb = np.ascontiguousarray(c[bi].reshape(8, 128).T)
        m = dict(shared)
        m["x_in"] = xb
        m["c_in"] = cb
        in_maps.append(m)

    res = run_bass_kernel_spmd(nc, in_maps, core_ids=list(range(8)))
    outs = []
    for bi in range(8):
        y = np.asarray(res.results[bi]["y_out"])
        outs.append(y.reshape(128, 8, NT).transpose(1, 0, 2).reshape(1024, NT).T)
    return np.stack(outs).astype(np.float32)
